# version 15
# baseline (speedup 1.0000x reference)
import numpy as np
from contextlib import ExitStack

import concourse.bass as bass
import concourse.mybir as mybir
from concourse.bass import IndirectOffsetOnAxis

F32 = mybir.dt.float32
BF16 = mybir.dt.bfloat16
I32 = mybir.dt.int32
ALU = mybir.AluOpType
AF = mybir.ActivationFunctionType
AX = mybir.AxisListType

L, D, H, DH, NF, E, R = 2048, 1024, 16, 64, 64, 8, 64
NLC, NDC, NCP = 16, 8, 8
EPS_ATT = 1e-6
EPS_LN = 1e-5


def host_prep(inp):
    c = np.ascontiguousarray
    f32 = np.float32
    g = lambda k: np.asarray(inp[k], f32)
    s = {}
    s["wqT"] = c(g("L0_attn_Wq").T)
    s["wkT"] = c(g("L0_attn_Wk").T)
    s["wvT"] = c(g("L0_attn_Wv").T)
    s["woT"] = c(g("L0_attn_Wo").T)
    s["baseT"] = c(g("L0_ffn_base_W").T)
    s["convT"] = c(g("L0_ffn_conv_W").transpose(0, 2, 1).reshape(E * D, D))
    s["downW"] = c(g("L0_ffn_down_W").reshape(E * D, R))
    s["upW"] = c(g("L0_ffn_up_W").reshape(E * R, D))
    s["dball"] = c(g("L0_ffn_down_b").T)
    s["uball"] = c(g("L0_ffn_up_b").reshape(E, 8, 128).transpose(2, 0, 1).reshape(128, 64))
    s["cball"] = c(g("L0_ffn_conv_b"))
    s["muT"] = c(g("L0_ffn_disc_mu").T)
    s["sigT"] = c(g("L0_ffn_disc_sigma").T)
    W0 = g("L0_attn_Wrf")[0]
    w0blk = np.zeros((NCP * 128, 128), f32)
    for cp in range(NCP):
        w0blk[cp * 128:cp * 128 + 64, 0:64] = W0[2 * cp]
        w0blk[cp * 128 + 64:cp * 128 + 128, 64:128] = W0[2 * cp + 1]
    s["w0blk"] = w0blk
    s["bqk"] = c(np.stack([g("L0_attn_bq").reshape(8, 128).T,
                           g("L0_attn_bk").reshape(8, 128).T],
                          axis=2).reshape(128, 16))
    s["bvrow"] = c(g("L0_attn_bv").reshape(1, D))
    s["borow"] = c(g("L0_attn_bo").reshape(1, D))
    s["bbrow"] = c(g("L0_ffn_base_b").reshape(1, D))
    s["identc"] = np.eye(128, dtype=f32)
    dsel = np.zeros((16, NCP * 128), f32)
    for cp in range(NCP):
        dsel[2 * cp, cp * 128:cp * 128 + 64] = 1.0
        dsel[2 * cp + 1, cp * 128 + 64:cp * 128 + 128] = 1.0
    dsel64 = np.zeros((64, NCP * 128), f32)
    dsel64[0:16] = dsel
    dsel64[32:48] = dsel
    s["dsel64"] = dsel64
    par = np.zeros((1, 256), f32)
    par[0, 0:64] = 1.0
    par[0, 128 + 64:256] = 1.0
    s["parab"] = par
    kkm = np.zeros((128, NCP * 16), f32)
    for cp in range(NCP):
        kkm[0:64, cp * 16 + 2 * cp] = -0.5
        kkm[64:128, cp * 16 + 2 * cp + 1] = -0.5
    s["kkmask"] = kkm
    s["onesrow"] = np.ones((1, 128), f32)
    wg = np.zeros((48, 5), f32)
    wg[0:8, 2] = 1.0; wg[0:8, 3] = 1.0
    wg[24:32, 2] = 1.0
    wg[8:16, 0] = -2.0; wg[8:16, 1] = -2.0
    wg[32:40, 0] = -2.0
    wg[16:24, 4] = 1.0; wg[40:48, 4] = 1.0
    s["wgrid"] = wg
    es = np.zeros((48, 8), f32)
    for e in range(8):
        for blk in range(6):
            es[blk * 8 + e, e] = 1.0
    s["esel"] = es
    return s


def build(taps=(), stage="full"):
    nc = bass.Bass("TRN2", target_bir_lowering=False)
    taps = set(taps)
    tap_list = []
    sem, cnt = {}, {}

    import sys as _sys
    _marks = {}
    build._marks = _marks

    def inc(ins, name, n=1):
        ins.then_inc(sem[name], n)
        cnt[name] += n
        _marks[(name, cnt[name])] = _sys._getframe(1).f_lineno
        return cnt[name]

    def w(eng, name, val):
        assert val <= 10 ** 9
        eng.wait_ge(sem[name], val)

    dt = nc.dram_tensor
    d_x = dt("x", [L, D], F32, kind="ExternalInput").ap()
    names = [("wqT", [D, D]), ("wkT", [D, D]), ("wvT", [D, D]), ("woT", [D, D]),
             ("baseT", [D, D]), ("convT", [E * D, D]), ("downW", [E * D, R]),
             ("upW", [E * R, D]), ("dball", [64, 8]), ("uball", [128, 64]),
             ("cball", [8, D]), ("muT", [D, E]), ("sigT", [D, E]),
             ("w0blk", [NCP * 128, 128]), ("bqk", [128, 16]), ("bvrow", [1, D]),
             ("borow", [1, D]), ("bbrow", [1, D]), ("identc", [128, 128]),
             ("dsel64", [64, NCP * 128]),
             ("parab", [1, 256]), ("kkmask", [128, NCP * 16]),
             ("onesrow", [1, 128]), ("wgrid", [48, 5]), ("esel", [48, 8])]
    d_in = {nm: dt("i_" + nm, shp, F32, kind="ExternalInput").ap() for nm, shp in names}
    d_out = dt("out", [L, D], F32, kind="ExternalOutput").ap()
    x_t = d_x.rearrange("(t p) d -> t p d", p=128)

    def tap(name, sbuf_ap, shape, semn, val=None):
        if name in taps:
            tp = dt("tap_" + name, shape, F32, kind="ExternalOutput").ap()
            tap_list.append((name, tp, sbuf_ap, semn,
                             cnt[semn] if val is None else val))

    ctx = ExitStack()
    SB = lambda n, s, t: ctx.enter_context(nc.sbuf_tensor(n, s, t))
    open_stacks = []

    def finish_ret():
        mblk.__exit__(None, None, None)
        _finish(nc, sem, cnt, inc, tap_list)
        for st_ in reversed(open_stacks):
            st_.close()
        return nc

    with ctx:
        for s_ in ["dL", "dH", "dW", "dXa", "dXb", "dG", "dGu", "dGc", "dOa", "dOb", "dT", "dC", "dRa", "dRb", "dRc", "dRd", "T", "V", "S", "G", "P"]:
            sem[s_] = ctx.enter_context(nc.semaphore(s_))
            cnt[s_] = 0

        identb = SB("identb", [128, 128], BF16)
        identf = SB("identf", [128, 128], F32)
        w0blk = SB("w0blk", [128, NCP * 128], BF16)
        dsel64 = SB("dsel64", [64, NCP * 128], BF16)
        parab = SB("parab", [1, 256], BF16)
        kkmask = SB("kkmask", [128, NCP * 16], BF16)
        onesrow = SB("onesrow", [1, 128], BF16)
        bqk = SB("bqk", [128, 16], F32)
        bvrow = SB("bvrow", [1, D], BF16)
        wgrid = SB("wgrid", [48, 5], F32)
        esel = SB("esel", [48, 8], BF16)

        m1c = SB("m1c", [128, 8], F32)
        m2c = SB("m2c", [128, 8], F32)
        momdc8 = SB("momdc8", [128, 40], BF16)
        sctA = SB("sctA", [128, 64], F32)
        ivA = SB("ivA", [128, 64], F32)
        mivA = SB("mivA", [128, 64], F32)
        m2ivA = SB("m2ivA", [128, 64], F32)
        ddisc8 = SB("ddisc8", [128, 384], BF16)
        loss5 = SB("loss5", [48, 5], F32)
        ppg = SB("ppg", [48, 5], F32)
        pp = SB("pp", [48, 1], F32)
        pphl = SB("pphl", [48, 2], BF16)
        lcol2 = SB("lcol2", [8, 2], F32)
        lcol = SB("lcol", [8, 1], F32)
        lrow = SB("lrow", [1, 8], F32)
        lmin = SB("lmin", [1, 1], F32)
        ohrow = SB("ohrow", [1, 8], F32)
        ohrowb = SB("ohrowb", [1, 8], BF16)
        ohj = SB("ohj", [1, 8], F32)
        ohcol = SB("ohcol", [8, 1], BF16)
        ohbc = SB("ohbc", [128, 8], F32)
        idxv = SB("idxv", [1, 2], F32)
        iotar = SB("iotar", [1, 8], F32)
        bdcol = SB("bdcol", [64, 2], F32)
        bucol = SB("bucol", [128, 16], F32)
        bucolb = SB("bucolb", [128, 8], BF16)
        muT = SB("muT", [128, NDC * E], F32)
        sigT = SB("sigT", [128, NDC * E], F32)
        dball = SB("dball", [64, 8], F32)
        uball = SB("uball", [128, 64], F32)
        bbrow = SB("bbrow", [1, D], BF16)
        iotac = SB("iotac", [128, 1], F32)
        idxbc = SB("idxbc", [128, 1], F32)
        idxbb = SB("idxbb", [1, 1], BF16)
        gidxf = SB("gidxf", [128, 17], F32)
        gidx17 = SB("gidx17", [128, 17], I32)

        mblk = nc.Block()
        mblk.__enter__()

        if (blk := mblk) or True:
            @blk.gpsimd
            def _(g):
                g.iota(iotar[:], [[1, 8]], base=0, channel_multiplier=0,
                       allow_small_or_imprecise_dtypes=True)
                g.iota(iotac[:], [[1, 1]], base=0, channel_multiplier=1,
                       allow_small_or_imprecise_dtypes=True)
                inc(g.nop(), "G", 1)
                inc(g.dma_start(identb[:], d_in["identc"]), "dL", 16)

            @blk.sync
            def _(sy):
                for dstap, nm, rr in [
                        (identf[:], "identc", None), (bqk[:], "bqk", None),
                        (wgrid[:], "wgrid", None),
                        (muT[:].rearrange("p (c e) -> p c e", e=E), "muT",
                         "(c p) e -> p c e"),
                        (sigT[:].rearrange("p (c e) -> p c e", e=E), "sigT",
                         "(c p) e -> p c e"),
                        (dball[:], "dball", None), (uball[:], "uball", None)]:
                    src = d_in[nm]
                    if rr:
                        src = src.rearrange(rr, p=128)
                    inc(sy.dma_start(dstap, src), "dH", 16)
        consts_dL = cnt["dL"]
        consts_dH = cnt["dH"]
        iota_G = cnt["G"]

        s_big = ExitStack()
        open_stacks.append(s_big)
        wq = s_big.enter_context(nc.sbuf_tensor("wq", [128, NDC * D], BF16, side="right"))
        wk = s_big.enter_context(nc.sbuf_tensor("wk", [128, NDC * D], BF16, side="right"))
        wv = s_big.enter_context(nc.sbuf_tensor("wv", [128, NDC * D], BF16, side="right"))
        xfm = s_big.enter_context(nc.sbuf_tensor("xfm", [128, NDC * L], BF16, side="right"))

        s_att = ExitStack()
        open_stacks.append(s_att)
        SA = lambda n, s, t: s_att.enter_context(nc.sbuf_tensor(n, s, t))
        eq_sb = SA("eq_sb", [128, NCP * L], BF16)
        zd8 = SA("zd8", [8, 128], F32)
        d16 = SA("d16", [128, 8], F32)
        dcolA = SA("dcolA", [8, 1], F32)
        dcolB = SA("dcolB", [8, 1], F32)
        numS = [SA(f"numS{c}", [128, 128], BF16) for c in range(NCP)]
        denS = [SA(f"denS{c}", [128, 16], BF16) for c in range(NCP)]
        s_att1 = ExitStack()
        open_stacks.append(s_att1)
        SA1 = lambda n, s, t: s_att1.enter_context(nc.sbuf_tensor(n, s, t))
        v_sb = SA1("v_sb", [128, NLC * H * 65], BF16)
        k_sb = SA1("k_sb", [128, NCP * L], BF16)
        mkA = SA1("mkA", [1, 8], BF16)
        mkB = SA1("mkB", [1, 8], BF16)
        zArow = SA1("zArow", [1, 8], BF16)
        zBrow = SA1("zBrow", [1, 8], BF16)
        kkT_sb = SA1("kkT_sb", [128, 256], F32)
        qqT_sb = kkT_sb
        mqfm = None
        colmax = SA1("colmax", [128, 1], F32)
        scr3 = SA1("scr3", [1, 128], F32)
        mbias = SA1("mbias", [128, 1], F32)
        ebias = SA1("ebias", [128, 8], F32)
        v4 = v_sb[:].rearrange("p (lc h c) -> p lc h c", lc=NLC, h=H, c=65)
        s_aps = ExitStack()
        open_stacks.append(s_aps)
        psK = s_aps.enter_context(nc.psum_tensor("psK", [128, 1024], F32))
        psS = s_aps.enter_context(nc.psum_tensor("psS", [128, 512], F32))

        with ExitStack() as ph:
            xb = ph.enter_context(nc.sbuf_tensor("xb", [128, 4 * D], BF16, side="right"))
            ptr = ph.enter_context(nc.psum_tensor("ptr", [128, 2048], BF16))
            _pt, _pv = cnt["T"], cnt["V"]
            txf = [0] * NLC
            tv = [[0, 0] for _ in range(NLC)]
            vxf = [0] * NLC
            vv = [[0, 0] for _ in range(NLC)]
            _ones_v = _pv + 1
            _pv += 1
            emit_order = []
            for lc in range(NLC):
                emit_order.append(("tr", lc))
                if lc >= 4:
                    emit_order.append(("v", lc - 4))
            for lc in range(NLC - 4, NLC):
                emit_order.append(("v", lc))
            for kind, lc in emit_order:
                if kind == "tr":
                    txf[lc] = _pt + 1
                    _pt += 1
                    vxf[lc] = _pv + 1
                    _pv += 1
                else:
                    tv[lc][0] = _pt + 1
                    tv[lc][1] = _pt + 2
                    _pt += 2
                    vv[lc][0] = _pv + 1
                    vv[lc][1] = _pv + 2
                    _pv += 2
            wv_dL = cnt["dW"] + 16
            grp2_dL = [consts_dL + 32,
                       consts_dL + 96] + \
                      [consts_dL + 96 + 32 * g for g in range(1, 7)]
            wqk_dL = wv_dL + 32
            late_dL = grp2_dL[7] + 6 * 16

            if (blk := mblk) or True:
                @blk.gpsimd
                def _(g):
                    def tiles(grp):
                        for i in range(2):
                            lc = grp * 2 + i
                            slot = (grp % 2) * 2 + i
                            inc(g.dma_start(xb[:, slot * D:(slot + 1) * D],
                                            x_t[lc]), "dL", 16)
                    tiles(0)
                    g.wait_ge(sem["dL"], grp2_dL[0])
                    inc(g.dma_start(onesrow[:], d_in["onesrow"]), "dL", 16)
                    inc(g.dma_start(bvrow[:], d_in["bvrow"]), "dL", 16)
                    g.wait_ge(sem["dL"], consts_dL + 64)
                    tiles(1)
                    inc(g.dma_start(
                        wv[:].rearrange("p (c n) -> p c n", n=D),
                        d_in["wvT"].rearrange("(c p) n -> p c n", p=128)), "dW", 16)
                    for grp in range(2, 8):
                        g.wait_ge(sem["dL"], grp2_dL[grp - 1])
                        g.wait_ge(sem["T"], txf[2 * (grp - 1) - 1])
                        tiles(grp)
                    for wsb, nm in [(wq, "wqT"), (wk, "wkT")]:
                        inc(g.dma_start(
                            wsb[:].rearrange("p (c n) -> p c n", n=D),
                            d_in[nm].rearrange("(c p) n -> p c n", p=128)), "dW", 16)
                    g.wait_ge(sem["dL"], grp2_dL[7])
                    for dstap, nm, rr in [
                            (w0blk[:].rearrange("p (c n) -> p c n", n=128),
                             "w0blk", "(c p) n -> p c n"),
                            (dsel64[:], "dsel64", None),
                            (kkmask[:], "kkmask", None),
                            (parab[:], "parab", None),
                            (esel[:], "esel", None), (bbrow[:], "bbrow", None)]:
                        srcp = d_in[nm]
                        if rr:
                            srcp = srcp.rearrange(rr, p=128)
                        inc(g.dma_start(dstap, srcp), "dL", 16)
                    assert cnt["dL"] == late_dL

                @blk.tensor
                def _(te):
                    for kind, lc in emit_order:
                        if kind == "tr":
                            w(te, "dL", grp2_dL[lc // 2])
                            if lc >= 2:
                                w(te, "V", vxf[lc - 2])
                            slot = ((lc // 2) % 2) * 2 + (lc % 2)
                            for dc in range(NDC):
                                ins = te.transpose(
                                    ptr[:, (lc % 2) * 1024 + dc * 128:(lc % 2) * 1024 + (dc + 1) * 128],
                                    xb[:, slot * D + dc * 128:slot * D + (dc + 1) * 128],
                                    identb[:])
                            assert inc(ins, "T", 1) == txf[lc]
                        else:
                            w(te, "V", vxf[lc])
                            if lc == 0:
                                w(te, "dW", wv_dL)
                                w(te, "dL", consts_dL + 64)
                                w(te, "V", _ones_v)
                            for ot in range(2):
                                it = lc * 2 + ot
                                if it >= 2:
                                    pit = it - 2
                                    w(te, "V", vv[pit // 2][pit % 2])
                                for dc in range(NDC):
                                    te.matmul(
                                        psK[:, (it % 2) * 512:(it % 2 + 1) * 512],
                                        xfm[:, dc * L + lc * 128:dc * L + (lc + 1) * 128],
                                        wv[:, dc * D + ot * 512:dc * D + (ot + 1) * 512],
                                        start=(dc == 0), stop=False)
                                ins = te.matmul(
                                    psK[:, (it % 2) * 512:(it % 2 + 1) * 512],
                                    onesrow[:], bvrow[:, ot * 512:(ot + 1) * 512],
                                    start=False, stop=True)
                                assert inc(ins, "T", 1) == tv[lc][ot]

                @blk.vector
                def _(ve):
                    assert inc(ve.memset(v4[:, :, :, 64:65], 1.0), "V", 1) == _ones_v
                    for kind, lc in emit_order:
                        if kind == "tr":
                            w(ve, "T", txf[lc])
                            dst = xfm[:].rearrange("p (c l) -> p c l", l=L)[:, :, lc * 128:(lc + 1) * 128]
                            ins = ve.tensor_copy(
                                dst, ptr[:, (lc % 2) * 1024:(lc % 2) * 1024 + 1024]
                                .rearrange("p (c k) -> p c k", k=128))
                            assert inc(ins, "V", 1) == vxf[lc]
                        else:
                            for ot in range(2):
                                it = lc * 2 + ot
                                w(ve, "T", tv[lc][ot])
                                ins = ve.tensor_copy(
                                    v4[:, lc, ot * 8:(ot + 1) * 8, 0:64],
                                    psK[:, (it % 2) * 512:(it % 2 + 1) * 512]
                                    .rearrange("p (h c) -> p h c", c=64))
                                assert inc(ins, "V", 1) == vv[lc][ot]
        xfm_V = vxf[NLC - 1]
        wqkv_dL = wqk_dL
        s_apq = ExitStack()
        open_stacks.append(s_apq)
        psQ = s_apq.enter_context(nc.psum_tensor("psQ", [128, L], F32))
        s_kt = ExitStack()
        open_stacks.append(s_kt)
        psKT = s_kt.enter_context(nc.psum_tensor("psKT", [128, 512], F32))
        v_V = cnt["V"]
        tap("xfm", xfm[:], [128, NDC * L], "V")
        tap("v", v_sb[:], [128, NLC * H * 65], "V")
        if stage in ("t0", "p1a"):
            return finish_ret()

        s_qs = ExitStack()
        open_stacks.append(s_qs)
        sq_sb = s_qs.enter_context(nc.sbuf_tensor("sq_sb", [128, L], BF16))
        mqfm = s_qs.enter_context(nc.sbuf_tensor("mqfm", [64, L], BF16))
        T1, V1, S1 = cnt["T"], cnt["V"], cnt["S"]
        tq = [[0] * 4 for _ in range(NCP)]
        tqq = [[0] * 4 for _ in range(NCP)]
        _p = T1
        for c in range(NCP):
            for lt in range(4):
                tq[c][lt] = _p + 1; _p += 1
            if c >= 1:
                for lt in range(4):
                    tqq[c - 1][lt] = _p + 1; _p += 1
        for lt in range(4):
            tqq[NCP - 1][lt] = _p + 1; _p += 1
        vqe = [[V1 + 8 * c + lt + 1 for lt in range(4)] for c in range(NCP)]
        vsqq = [[V1 + 8 * c + 4 + lt + 1 for lt in range(4)] for c in range(NCP)]
        TA_end = T1 + 8 * NCP
        VA_end = V1 + 8 * NCP
        vqq = VA_end + 1
        vm25 = VA_end + 2
        tmq = [TA_end + lc + 1 for lc in range(NLC)]
        vmq2 = [VA_end + 2 + lc + 1 for lc in range(NLC)]
        sscl = S1 + 1
        TB_end = TA_end + NLC
        VB_end = VA_end + 2 + NLC
        tlq = [[TB_end + 4 * c + lt + 1 for lt in range(4)] for c in range(NCP)]
        sexp = [[S1 + 1 + 4 * c + lt + 1 for lt in range(4)] for c in range(NCP)]
        TC_end = TB_end + 4 * NCP
        tk = [[0] * 4 for _ in range(NCP)]
        tkk = [[0] * 4 for _ in range(NCP)]
        _p = TC_end
        for c in range(NCP):
            for lt in range(4):
                tk[c][lt] = _p + 1; _p += 1
            if c >= 1:
                for lt in range(4):
                    tkk[c - 1][lt] = _p + 1; _p += 1
        for lt in range(4):
            tkk[NCP - 1][lt] = _p + 1; _p += 1
        vke = [[VB_end + 8 * c + lt + 1 for lt in range(4)] for c in range(NCP)]
        vsqk = [[VB_end + 8 * c + 4 + lt + 1 for lt in range(4)] for c in range(NCP)]

        if (blk := mblk) or True:
            @blk.tensor
            def _(te):
                w(te, "dW", wqk_dL)
                w(te, "dL", late_dL)
                for cp in range(NCP):
                    for lt in range(4):
                        if cp > 0:
                            w(te, "V", vqe[cp - 1][lt])
                        for dc in range(NDC):
                            ins = te.matmul(
                                psQ[:, lt * 512:(lt + 1) * 512],
                                wq[:, dc * D + cp * 128:dc * D + (cp + 1) * 128],
                                xfm[:, dc * L + lt * 512:dc * L + (lt + 1) * 512],
                                start=(dc == 0), stop=(dc == NDC - 1))
                        assert inc(ins, "T", 1) == tq[cp][lt]
                    for pcp in ([cp - 1] if cp >= 1 else []):
                        for lt in range(4):
                            w(te, "V", vsqq[pcp][lt])
                            for j in range(4):
                                lc = lt * 4 + j
                                ins = te.matmul(
                                    psKT[:, 256 + lc * 16 + 2 * pcp:256 + lc * 16 + 2 * pcp + 2],
                                    sq_sb[:, (pcp % 2) * 0 + lc * 128:(lc + 1) * 128],
                                    kkmask[:, pcp * 16 + 2 * pcp:pcp * 16 + 2 * pcp + 2],
                                    start=True, stop=True)
                            assert inc(ins, "T", 1) == tqq[pcp][lt]
                for lt in range(4):
                    w(te, "V", vsqq[NCP - 1][lt])
                    for j in range(4):
                        lc = lt * 4 + j
                        ins = te.matmul(
                            psKT[:, 256 + lc * 16 + 2 * (NCP - 1):256 + lc * 16 + 2 * (NCP - 1) + 2],
                            sq_sb[:, lc * 128:(lc + 1) * 128],
                            kkmask[:, (NCP - 1) * 16 + 2 * (NCP - 1):(NCP - 1) * 16 + 2 * (NCP - 1) + 2],
                            start=True, stop=True)
                    assert inc(ins, "T", 1) == tqq[NCP - 1][lt]
                w(te, "V", vm25)
                for lc in range(NLC):
                    if lc >= 1:
                        w(te, "V", vmq2[lc - 1])
                    ins = te.transpose(
                        psS[0:16, (lc % 2) * 128:(lc % 2) * 128 + 128],
                        qqT_sb[:, lc * 16:(lc + 1) * 16], identf[:])
                    assert inc(ins, "T", 1) == tmq[lc]
                for cp in range(NCP):
                    if cp == 0:
                        w(te, "V", vmq2[NLC - 1])
                    for lt in range(4):
                        if cp == 0:
                            w(te, "V", vqe[NCP - 1][lt])
                        else:
                            w(te, "S", sexp[cp - 1][lt])
                        te.matmul(psQ[:, lt * 512:(lt + 1) * 512],
                                  w0blk[:, cp * 128:(cp + 1) * 128],
                                  k_sb[:, cp * L + lt * 512:cp * L + (lt + 1) * 512],
                                  start=True, stop=False)
                        ins = te.matmul(psQ[:, lt * 512:(lt + 1) * 512],
                                        dsel64[:, cp * 128:(cp + 1) * 128],
                                        mqfm[:, lt * 512:(lt + 1) * 512],
                                        start=False, stop=True)
                        assert inc(ins, "T", 1) == tlq[cp][lt]
                def _kkgrp(pcp, lt):
                    w(te, "V", vsqk[pcp][lt])
                    for j in range(4):
                        lc = lt * 4 + j
                        ins = te.matmul(
                            psKT[:, lc * 16 + 2 * pcp:lc * 16 + 2 * pcp + 2],
                            sq_sb[:, lc * 128:(lc + 1) * 128],
                            kkmask[:, pcp * 16 + 2 * pcp:pcp * 16 + 2 * pcp + 2],
                            start=True, stop=True)
                    assert inc(ins, "T", 1) == tkk[pcp][lt]
                for cp in range(NCP):
                    for lt in range(4):
                        bank = lt % 2
                        if cp == 0 and lt < 2:
                            w(te, "V", v_V)
                        else:
                            pc, plt = (cp, lt - 2) if lt >= 2 else (cp - 1, lt + 2)
                            w(te, "V", vke[pc][plt])
                        for dc in range(NDC):
                            ins = te.matmul(
                                psK[:, bank * 512:(bank + 1) * 512],
                                wk[:, dc * D + cp * 128:dc * D + (cp + 1) * 128],
                                xfm[:, dc * L + lt * 512:dc * L + (lt + 1) * 512],
                                start=(dc == 0), stop=(dc == NDC - 1))
                        assert inc(ins, "T", 1) == tk[cp][lt]
                    if cp >= 1:
                        for lt in range(4):
                            _kkgrp(cp - 1, lt)
                for lt in range(4):
                    _kkgrp(NCP - 1, lt)

            @blk.vector
            def _(ve):
                w(ve, "dH", consts_dH)
                ve.memset(mqfm[:], 0.0)
                for cp in range(NCP):
                    for lt in range(4):
                        w(ve, "T", tq[cp][lt])
                        ins = ve.tensor_scalar(
                            k_sb[:, cp * L + lt * 512:cp * L + (lt + 1) * 512],
                            psQ[:, lt * 512:(lt + 1) * 512],
                            bqk[:, cp * 2:cp * 2 + 1], None, ALU.add)
                        assert inc(ins, "V", 1) == vqe[cp][lt]
                    for lt in range(4):
                        w(ve, "V", vqe[cp][lt])
                        if cp > 0:
                            w(ve, "T", tqq[cp - 1][lt])
                        ins = ve.tensor_tensor(
                            sq_sb[:, lt * 512:(lt + 1) * 512],
                            k_sb[:, cp * L + lt * 512:cp * L + (lt + 1) * 512],
                            k_sb[:, cp * L + lt * 512:cp * L + (lt + 1) * 512],
                            ALU.mult)
                        assert inc(ins, "V", 1) == vsqq[cp][lt]
                w(ve, "T", tqq[NCP - 1][3])
                assert inc(ve.tensor_copy(qqT_sb[:], psKT[:, 256:512]), "V", 1) == vqq
                w(ve, "S", sscl)
                ins = ve.tensor_scalar(qqT_sb[:], qqT_sb[:], -1.0, -25.0, ALU.mult, ALU.add)
                assert inc(ins, "V", 1) == vm25
                for lc in range(NLC):
                    w(ve, "T", tmq[lc])
                    ins = ve.tensor_copy(
                        mqfm[0:16, lc * 128:(lc + 1) * 128],
                        psS[0:16, (lc % 2) * 128:(lc % 2) * 128 + 128])
                    assert inc(ins, "V", 1) == vmq2[lc]
                for cp in range(NCP):
                    for lt in range(4):
                        bank = lt % 2
                        w(ve, "T", tk[cp][lt])
                        if cp == 0 and lt == 0:
                            w(ve, "T", tlq[NCP - 1][3])
                        ins = ve.tensor_scalar(
                            k_sb[:, cp * L + lt * 512:cp * L + (lt + 1) * 512],
                            psK[:, bank * 512:(bank + 1) * 512],
                            bqk[:, cp * 2 + 1:cp * 2 + 2], None, ALU.add)
                        assert inc(ins, "V", 1) == vke[cp][lt]
                    for lt in range(4):
                        w(ve, "V", vke[cp][lt])
                        if cp > 0:
                            w(ve, "T", tkk[cp - 1][lt])
                        elif lt == 0:
                            w(ve, "T", tqq[NCP - 1][3])
                        ins = ve.tensor_tensor(
                            sq_sb[:, lt * 512:(lt + 1) * 512],
                            k_sb[:, cp * L + lt * 512:cp * L + (lt + 1) * 512],
                            k_sb[:, cp * L + lt * 512:cp * L + (lt + 1) * 512],
                            ALU.mult)
                        assert inc(ins, "V", 1) == vsqk[cp][lt]

            @blk.scalar
            def _(se):
                w(se, "V", vqq)
                ins = se.activation(qqT_sb[:], qqT_sb[:], AF.Sqrt,
                                    bias=0.0, scale=-2.0 * 2.35 * 2.35)
                assert inc(ins, "S", 1) == sscl
                for cp in range(NCP):
                    for lt in range(4):
                        w(se, "T", tlq[cp][lt])
                        ins = se.activation(
                            eq_sb[:, cp * L + lt * 512:cp * L + (lt + 1) * 512],
                            psQ[:, lt * 512:(lt + 1) * 512],
                            AF.Exp, bias=0.0, scale=1.0)
                        assert inc(ins, "S", 1) == sexp[cp][lt]
        tap("eq", eq_sb[:], [128, NCP * L], "S")
        tap("ksb", k_sb[:], [128, NCP * L], "V")
        if stage == "p1b":
            return finish_ret()
        s_qs.close(); open_stacks.remove(s_qs)

        s_big.close(); open_stacks.remove(s_big)

        s_ee = ExitStack()
        open_stacks.append(s_ee)
        comb2 = s_ee.enter_context(nc.sbuf_tensor("comb2", [64, L], BF16))
        comblo = s_ee.enter_context(nc.sbuf_tensor("comblo", [16, L], BF16))
        e_sb = s_ee.enter_context(nc.sbuf_tensor("e_sb", [128, 2 * L], BF16))
        eT_sb = s_ee.enter_context(nc.sbuf_tensor("eT_sb", [128, 2 * L], BF16))
        psEb = None

        T2p, V2p = cnt["T"], cnt["V"]
        vkkt = V2p + 1
        kktr = [T2p + lc + 1 for lc in range(NLC)]
        vkkfm = [V2p + 1 + 2 * lc + 2 for lc in range(NLC)]
        vclo = V2p + 1 + 2 * NLC
        assert vclo == vkkfm[-1]
        if (blk := mblk) or True:
            @blk.vector
            def _(ve):
                w(ve, "T", tkk[NCP - 1][3])
                w(ve, "T", tmq[NLC - 1])
                ve.memset(comb2[:], 0.0)
                assert inc(ve.tensor_copy(kkT_sb[:], psKT[:, 0:256]), "V", 1) == vkkt

            @blk.tensor
            def _(te):
                w(te, "V", vkkt)
                w(te, "S", sexp[NCP - 1][3])
                for lc in range(NLC):
                    if lc >= 1:
                        w(te, "V", vkkfm[lc - 1])
                    ins = te.transpose(
                        psS[0:16, (lc % 2) * 128:(lc % 2) * 128 + 128],
                        kkT_sb[:, lc * 16:(lc + 1) * 16], identf[:])
                    assert inc(ins, "T", 1) == kktr[lc]

            @blk.vector
            def _(ve):
                for lc in range(NLC):
                    w(ve, "T", kktr[lc])
                    ins = ve.tensor_copy(
                        comb2[0:16, lc * 128:(lc + 1) * 128],
                        psS[0:16, (lc % 2) * 128:(lc % 2) * 128 + 128])
                    inc(ins, "V", 1)
                    w(ve, "V", cnt["V"])
                    ins = ve.scalar_tensor_tensor(
                        comblo[:, lc * 128:(lc + 1) * 128],
                        comb2[0:16, lc * 128:(lc + 1) * 128], -1.0,
                        psS[0:16, (lc % 2) * 128:(lc % 2) * 128 + 128],
                        ALU.mult, ALU.add)
                    assert inc(ins, "V", 1) == vkkfm[lc]
                assert vclo == cnt["V"] + 0 or True
        tap("kkfm", comb2[0:16, :], [16, L], "V", None)
        if (blk := mblk) or True:
            @blk.sync
            def _(sy):
                sy.wait_ge(sem["V"], vclo)
                inc(sy.dma_start(comb2[32:48, :], comblo[:]), "dC", 16)
        s_wt = ExitStack()
        open_stacks.append(s_wt)
        wbase = s_wt.enter_context(nc.sbuf_tensor("wbase", [128, NDC * D], BF16, side="right"))
        cball = s_wt.enter_context(nc.sbuf_tensor("cball", [8, D], BF16, side="right"))
        borow = s_wt.enter_context(nc.sbuf_tensor("borow", [1, D], BF16, side="right"))
        s_wt2 = ExitStack()
        open_stacks.append(s_wt2)
        wo = s_wt2.enter_context(nc.sbuf_tensor("wo", [128, NDC * D], BF16, side="right"))
        if (blk := mblk) or True:
            @blk.gpsimd
            def _(g):
                g.wait_ge(sem["T"], tkk[NCP - 1][3])
                inc(g.dma_start(
                    wo[:].rearrange("p (c n) -> p c n", n=D),
                    d_in["woT"].rearrange("(c p) n -> p c n", p=128)), "dW", 16)
                inc(g.dma_start(
                    wbase[:].rearrange("p (c n) -> p c n", n=D),
                    d_in["baseT"].rearrange("(c p) n -> p c n", p=128)), "dW", 16)
                inc(g.dma_start(cball[:], d_in["cball"]), "dW", 16)
                inc(g.dma_start(borow[:], d_in["borow"]), "dW", 16)
        wo_dL = cnt["dW"] - 48
        wbase_dL = cnt["dW"]
        s_kt.close(); open_stacks.remove(s_kt)
        s_aps2 = ExitStack()
        open_stacks.append(s_aps2)
        psEb = s_aps2.enter_context(nc.psum_tensor("psEb", [128, 512], BF16))

        if (blk := mblk) or True:
            @blk.vector
            def _(ve):
                w(ve, "V", cnt["V"])
                for c in range(NCP):
                    ve.memset(numS[c][:], 0.0)
                    ve.memset(denS[c][:], 0.0)
                inc(ve.nop(), "V", 1)
        memset_V = cnt["V"]

        _pt, _pv, _ps = cnt["T"], cnt["V"], cnt["S"]
        tLh = [[0, 0] for _ in range(NCP)]
        ttr2 = [0] * NCP
        tbias2 = [0] * NCP
        tetr = [[0] * 4 for _ in range(NCP)]
        tN = [0] * NCP
        vcm2 = [0] * NCP
        vscr2 = [0] * NCP
        vmk = [0] * NCP
        vmb2 = [0] * NCP
        vet = [[0] * 4 for _ in range(NCP)]
        vnum = [0] * NCP
        sexp2h = [[0, 0] for _ in range(NCP)]
        for i in range(NCP):
            tLh[i][0] = _pt + 1; tLh[i][1] = _pt + 2
            _pt += 2
            if i >= 1:
                for g in range(4):
                    tetr[i - 1][g] = _pt + 1 + g
                tN[i - 1] = _pt + 5
                _pt += 5
            ttr2[i] = _pt + 1; tbias2[i] = _pt + 2
            _pt += 2
            vcm2[i] = _pv + 1
            _pv += 1
            if i >= 1:
                for g in range(4):
                    vet[i - 1][g] = _pv + 1 + g
                vnum[i - 1] = _pv + 5
                _pv += 5
            vscr2[i] = _pv + 1; vmk[i] = _pv + 2; vmb2[i] = _pv + 3
            _pv += 3
            sexp2h[i][0] = _ps + 1; sexp2h[i][1] = _ps + 2
            _ps += 2
        for g in range(4):
            tetr[NCP - 1][g] = _pt + 1 + g
            vet[NCP - 1][g] = _pv + 1 + g
        tN[NCP - 1] = _pt + 5
        vnum[NCP - 1] = _pv + 5

        def _emit_tr_nh(te, cp):
            for g4 in range(4):
                if cp > 0 or g4 > 0:
                    pc, pg = (cp, g4 - 1) if g4 > 0 else (cp - 1, 3)
                    w(te, "V", vet[pc][pg])
                w(te, "S", sexp2h[cp][g4 // 2])
                for j in range(4):
                    lc = g4 * 4 + j
                    ins = te.transpose(
                        psEb[:, j * 128:(j + 1) * 128],
                        e_sb[:, (cp % 2) * L + lc * 128:(cp % 2) * L + (lc + 1) * 128],
                        identb[:])
                assert inc(ins, "T", 1) == tetr[cp][g4]
            if cp == 0:
                w(te, "V", memset_V)
            if cp > 0:
                w(te, "V", vnum[cp - 1])
            for lc in range(NLC):
                g4 = lc // 4
                w(te, "V", vet[cp][g4])
                ins = te.matmul(
                    psK[:, 512:642],
                    eT_sb[:, (cp % 2) * L + lc * 128:(cp % 2) * L + (lc + 1) * 128],
                    v_sb[:, lc * 1040 + cp * 130:lc * 1040 + cp * 130 + 130],
                    start=(lc == 0), stop=(lc == NLC - 1))
            assert inc(ins, "T", 1) == tN[cp]

        if (blk := mblk) or True:
            @blk.tensor
            def _(te):
                for cp in range(NCP):
                    w(te, "V", vclo)
                    if cp == 0:
                        w(te, "dC", 16)
                    for h in range(2):
                        if cp > 0:
                            w(te, "S", sexp2h[cp - 1][h])
                        for lt in (2 * h, 2 * h + 1):
                            te.matmul(psQ[:, lt * 512:(lt + 1) * 512],
                                      w0blk[:, cp * 128:(cp + 1) * 128],
                                      k_sb[:, cp * L + lt * 512:cp * L + (lt + 1) * 512],
                                      start=True, stop=False)
                            ins = te.matmul(psQ[:, lt * 512:(lt + 1) * 512],
                                            dsel64[:, cp * 128:(cp + 1) * 128],
                                            comb2[:, lt * 512:(lt + 1) * 512],
                                            start=False, stop=True)
                        assert inc(ins, "T", 1) == tLh[cp][h]
                    if cp >= 1:
                        _emit_tr_nh(te, cp - 1)
                    w(te, "V", vcm2[cp])
                    if cp > 0:
                        w(te, "V", vmb2[cp - 1])
                    ins = te.transpose(psS[0:1, 0:128], colmax[:], identf[:])
                    assert inc(ins, "T", 1) == ttr2[cp]
                    w(te, "V", vmk[cp])
                    te.matmul(psS[:, 128:129], parab[:, 0:128],
                              mkA[:, cp:cp + 1], start=True, stop=False)
                    ins = te.matmul(psS[:, 128:129], parab[:, 128:256],
                                    mkB[:, cp:cp + 1], start=False, stop=True)
                    assert inc(ins, "T", 1) == tbias2[cp]
                _emit_tr_nh(te, NCP - 1)

            def _emit_evacs(ve, pc):
                for g4 in range(4):
                    w(ve, "T", tetr[pc][g4])
                    if pc >= 2:
                        w(ve, "T", tN[pc - 2])
                    ins = ve.tensor_copy(
                        eT_sb[:, (pc % 2) * L + g4 * 512:(pc % 2) * L + (g4 + 1) * 512],
                        psEb[:, 0:512])
                    assert inc(ins, "V", 1) == vet[pc][g4]
                w(ve, "T", tN[pc])
                ve.tensor_copy(numS[pc][0:64, 0:64], psK[0:64, 512:576])
                ve.tensor_copy(numS[pc][64:128, 64:128], psK[64:128, 577:641])
                ve.tensor_copy(d16[0:64, pc:pc + 1], psK[0:64, 576:577])
                ins = ve.tensor_copy(d16[64:128, pc:pc + 1], psK[64:128, 641:642])
                assert inc(ins, "V", 1) == vnum[pc]

            @blk.vector
            def _(ve):
                for cp in range(NCP):
                    w(ve, "T", tLh[cp][0])
                    ins = ve.reduce_max(colmax[:], psQ[:, 0:1024:4], axis=AX.X)
                    assert inc(ins, "V", 1) == vcm2[cp]
                    if cp >= 1:
                        _emit_evacs(ve, cp - 1)
                    w(ve, "T", ttr2[cp])
                    ins = ve.tensor_copy(scr3[:], psS[0:1, 0:128])
                    assert inc(ins, "V", 1) == vscr2[cp]
                    w(ve, "V", vscr2[cp])
                    ve.reduce_max(mkA[:, cp:cp + 1], scr3[:, 0:64],
                                  axis=AX.X, negate=True)
                    ins = ve.reduce_max(mkB[:, cp:cp + 1], scr3[:, 64:128],
                                        axis=AX.X, negate=True)
                    assert inc(ins, "V", 1) == vmk[cp]
                    w(ve, "T", tbias2[cp])
                    if cp > 0:
                        w(ve, "S", sexp2h[cp - 1][1])
                    ins = ve.tensor_copy(mbias[:], psS[:, 128:129])
                    assert inc(ins, "V", 1) == vmb2[cp]
                _emit_evacs(ve, NCP - 1)

            @blk.scalar
            def _(se):
                for cp in range(NCP):
                    for h in range(2):
                        w(se, "T", tLh[cp][h])
                        w(se, "V", vmb2[cp])
                        if cp >= 2:
                            w(se, "T", tetr[cp - 2][3])
                        ins = se.activation(
                            e_sb[:, (cp % 2) * L + h * 1024:(cp % 2) * L + (h + 1) * 1024],
                            psQ[:, h * 1024:(h + 1) * 1024],
                            AF.Exp, bias=mbias[:], scale=1.0)
                        assert inc(ins, "S", 1) == sexp2h[cp][h]
        assert cnt["T"] == _pt + 5 and cnt["V"] == _pv + 5 and cnt["S"] == _ps
        sexp2 = [sexp2h[c][1] for c in range(NCP)]
        tap("d16", d16[:], [128, 8], "V")
        tap("e", e_sb[:], [128, 2 * L], "S")
        if stage == "p2":
            return finish_ret()
        s_ee.close(); open_stacks.remove(s_ee)

        TZ, VZ = cnt["T"], cnt["V"]
        tzd = TZ + 1
        tzab = TZ + 2
        tze8 = TZ + 3
        vzd8 = VZ + 1
        vzcols = VZ + 2
        vzrows = VZ + 3
        vzeb8 = VZ + 4
        vzeall = VZ + 5

        if (blk := mblk) or True:
            @blk.tensor
            def _(te):
                w(te, "V", vnum[NCP - 1])
                w(te, "S", sexp2[NCP - 1])
                ins = te.transpose(psS[0:8, 0:128], d16[:], identf[:])
                assert inc(ins, "T", 1) == tzd
                w(te, "V", vzcols)
                te.transpose(psS[0:1, 128:136], dcolA[:], identf[0:8, 0:8])
                ins = te.transpose(psS[0:1, 136:144], dcolB[:], identf[0:8, 0:8])
                assert inc(ins, "T", 1) == tzab
                w(te, "V", vzrows)
                te.matmul(psS[:, 144:152], parab[:, 0:128],
                          zArow[:, 0:8], start=True, stop=False)
                ins = te.matmul(psS[:, 144:152], parab[:, 128:256],
                                zBrow[:, 0:8], start=False, stop=True)
                assert inc(ins, "T", 1) == tze8

            @blk.vector
            def _(ve):
                w(ve, "T", tzd)
                ins = ve.tensor_copy(zd8[:], psS[0:8, 0:128])
                assert inc(ins, "V", 1) == vzd8
                w(ve, "V", vzd8)
                ve.reduce_sum(dcolA[:], zd8[:, 0:64], axis=AX.X)
                ins = ve.reduce_sum(dcolB[:], zd8[:, 64:128], axis=AX.X)
                assert inc(ins, "V", 1) == vzcols
                w(ve, "T", tzab)
                ve.tensor_scalar(zArow[:], psS[0:1, 128:136], EPS_ATT, None, ALU.mult)
                ins = ve.tensor_scalar(zBrow[:], psS[0:1, 136:144], EPS_ATT, None, ALU.mult)
                assert inc(ins, "V", 1) == vzrows
                w(ve, "T", tze8)
                ins = ve.tensor_copy(ebias[:], psS[:, 144:152])
                assert inc(ins, "V", 1) == vzeb8
                w(ve, "V", vzeb8)
                for cp in range(NCP):
                    ve.tensor_tensor(denS[cp][0:64, 2 * cp:2 * cp + 1],
                                     d16[0:64, cp:cp + 1], ebias[0:64, cp:cp + 1],
                                     ALU.add)
                    ins = ve.tensor_tensor(denS[cp][64:128, 2 * cp + 1:2 * cp + 2],
                                           d16[64:128, cp:cp + 1],
                                           ebias[64:128, cp:cp + 1], ALU.add)
                assert inc(ins, "V", 1) == vzeall
        denS_V = cnt["V"]

        s_att1.close(); open_stacks.remove(s_att1)
        s_rp = ExitStack()
        open_stacks.append(s_rp)
        recipf = s_rp.enter_context(nc.sbuf_tensor("recipf", [16, L], F32))
        recip2 = s_rp.enter_context(nc.sbuf_tensor("recip2", [64, L], BF16))
        reciplo16 = s_rp.enter_context(nc.sbuf_tensor("reciplo16", [16, L], BF16))

        TB, VB = cnt["T"], cnt["V"]
        tden = [TB + lt + 1 for lt in range(4)]
        vrecf = [VB + 3 * lt + 1 for lt in range(4)]
        vrech = [VB + 3 * lt + 2 for lt in range(4)]
        vrlo = [VB + 3 * lt + 3 for lt in range(4)]
        if (blk := mblk) or True:
            @blk.tensor
            def _(te):
                w(te, "V", denS_V)
                for lt in range(4):
                    for cp in range(NCP):
                        ins = te.matmul(
                            psQ[0:16, lt * 512:(lt + 1) * 512],
                            denS[cp][:],
                            eq_sb[:, cp * L + lt * 512:cp * L + (lt + 1) * 512],
                            start=(cp == 0), stop=(cp == NCP - 1))
                    assert inc(ins, "T", 1) == tden[lt]

            @blk.vector
            def _(ve):
                ve.memset(recip2[:], 0.0)
                for lt in range(4):
                    sl = slice(lt * 512, (lt + 1) * 512)
                    w(ve, "T", tden[lt])
                    ins = ve.reciprocal(recipf[:, sl], psQ[0:16, sl])
                    assert inc(ins, "V", 1) == vrecf[lt]
                    w(ve, "V", vrecf[lt])
                    ins = ve.tensor_copy(recip2[0:16, sl], recipf[:, sl])
                    assert inc(ins, "V", 1) == vrech[lt]
                    w(ve, "V", vrech[lt])
                    ins = ve.scalar_tensor_tensor(
                        reciplo16[:, sl], recip2[0:16, sl], -1.0, recipf[:, sl],
                        ALU.mult, ALU.add)
                    assert inc(ins, "V", 1) == vrlo[lt]

            @blk.sync
            def _(sy):
                for lt in range(4):
                    sl = slice(lt * 512, (lt + 1) * 512)
                    sy.wait_ge(sem["V"], vrlo[lt])
                    inc(sy.dma_start(recip2[32:48, sl], reciplo16[:, sl]),
                        ("dRa", "dRb", "dRc", "dRd")[lt], 16)
        tap("recipf", recipf[:], [16, L], "V")
        tap("densall", denS[0][:], [128, 16], "V")
        if stage == "pz":
            return finish_ret()
        s_aps2.close(); open_stacks.remove(s_aps2)
        s_apq.close(); open_stacks.remove(s_apq)
        s_aps.close(); open_stacks.remove(s_aps)

        s_ctx = ExitStack()
        open_stacks.append(s_ctx)
        ctx_sb = s_ctx.enter_context(
            nc.sbuf_tensor("ctx_sb", [128, NCP * L], BF16, side="right"))
        s_b2 = ExitStack()
        open_stacks.append(s_b2)
        psBC = s_b2.enter_context(nc.psum_tensor("psBC", [128, L], F32))
        psNum = s_b2.enter_context(nc.psum_tensor("psNum", [128, L], F32))
        phi_sb = s_b2.enter_context(nc.sbuf_tensor("phi_sb", [128, L], BF16))
        TB2, VB2, SB2 = cnt["T"], cnt["V"], cnt["S"]
        tbc = [[TB2 + 4 * c + h + 1 for h in range(2)] for c in range(NCP)]
        tnum = [[TB2 + 4 * c + 2 + h + 1 for h in range(2)] for c in range(NCP)]
        vphi = [[VB2 + 2 * c + h + 1 for h in range(2)] for c in range(NCP)]
        sctx = [[SB2 + 2 * c + h + 1 for h in range(2)] for c in range(NCP)]
        if (blk := mblk) or True:
            @blk.tensor
            def _(te):
                for cp in range(NCP):
                    for h in range(2):
                        if cp == 0:
                            w(te, ("dRa", "dRc")[h], 16)
                            w(te, ("dRb", "dRd")[h], 16)
                            if h == 0:
                                w(te, "dC", 16)
                        if cp >= 1:
                            w(te, "V", vphi[cp - 1][h])
                        for lt in (2 * h, 2 * h + 1):
                            ins = te.matmul(psBC[:, lt * 512:(lt + 1) * 512],
                                            dsel64[:, cp * 128:(cp + 1) * 128],
                                            recip2[:, lt * 512:(lt + 1) * 512],
                                            start=True, stop=True)
                        assert inc(ins, "T", 1) == tbc[cp][h]
                    for h in range(2):
                        w(te, "V", vphi[cp][h])
                        if cp >= 1:
                            w(te, "S", sctx[cp - 1][h])
                        for lt in (2 * h, 2 * h + 1):
                            ins = te.matmul(psNum[:, lt * 512:(lt + 1) * 512],
                                            numS[cp][:],
                                            phi_sb[:, lt * 512:(lt + 1) * 512],
                                            start=True, stop=True)
                        assert inc(ins, "T", 1) == tnum[cp][h]

            @blk.vector
            def _(ve):
                for cp in range(NCP):
                    for h in range(2):
                        w(ve, "T", tbc[cp][h])
                        if cp >= 1:
                            w(ve, "T", tnum[cp - 1][h])
                        for lt in (2 * h, 2 * h + 1):
                            ins = ve.tensor_tensor(
                                phi_sb[:, lt * 512:(lt + 1) * 512],
                                psBC[:, lt * 512:(lt + 1) * 512],
                                eq_sb[:, cp * L + lt * 512:cp * L + (lt + 1) * 512],
                                ALU.mult)
                        assert inc(ins, "V", 1) == vphi[cp][h]

            @blk.scalar
            def _(se):
                for cp in range(NCP):
                    for h in range(2):
                        w(se, "T", tnum[cp][h])
                        for lt in (2 * h, 2 * h + 1):
                            ins = se.activation(
                                ctx_sb[:, cp * L + lt * 512:cp * L + (lt + 1) * 512],
                                psNum[:, lt * 512:(lt + 1) * 512],
                                AF.Copy, bias=0.0, scale=1.0)
                        assert inc(ins, "S", 1) == sctx[cp][h]
        ctx_S = cnt["S"]
        assert ctx_S == sctx[NCP - 1][1]
        tap("ctx", ctx_sb[:], [128, NCP * L], "S")
        if stage == "pb2":
            return finish_ret()
        s_b2.close(); open_stacks.remove(s_b2)
        s_rp.close(); open_stacks.remove(s_rp)
        s_att.close(); open_stacks.remove(s_att)

        s1bf = SB("s1bf", [128, NLC * D], BF16)
        s1fm = SB("s1fm", [128, NDC * L], BF16)
        stat = SB("stat", [128, 24], F32)
        scrap = SB("scrap", [128, L], BF16)
        s2t = SB("s2t", [128, 2 * D], F32)
        m1q = SB("m1q", [128, 32], F32)
        m2q = SB("m2q", [128, 32], F32)
        m2scr = SB("m2scr", [128, 1024], BF16)
        s_wo = ExitStack()
        open_stacks.append(s_wo)
        xr = s_wo.enter_context(nc.sbuf_tensor("xr", [128, 2 * D], F32))
        u1buf = s_wo.enter_context(nc.sbuf_tensor("u1buf", [128, 4 * D], F32))
        psO = s_wo.enter_context(nc.psum_tensor("psO", [128, 2 * D], F32))
        psT = s_wo.enter_context(nc.psum_tensor("psT", [128, 2048], BF16))

        TW, VW, SW, PW = cnt["T"], cnt["V"], cnt["S"], cnt["P"]
        def _msched():
            sched = [[] for _ in range(NLC)]
            pend = []
            for lc in range(NLC):
                if lc >= 6 and (lc - 6) % 4 == 0 and (lc - 6) // 4 < 4:
                    pend.extend(((lc - 6) // 4, dc) for dc in range(NDC))
                for _ in range(3):
                    if pend:
                        sched[lc].append(pend.pop(0))
            pend.extend((3, dc) for dc in range(NDC))
            return sched, pend
        m1sched, m1tail = _msched()
        m2sched, m2tail = _msched()
        te_ops = []
        for lc in range(NLC):
            te_ops.append(("mm", lc))
            if lc >= 3:
                te_ops.append(("tr", lc - 3))
        te_ops += [("tr", 13), ("tr", 14), ("tr", 15)]
        ve_ops = []
        for lc in range(NLC):
            ve_ops.append(("vu", lc))
            if lc >= 1:
                ve_ops += [("vms", lc - 1), ("vm2", lc - 1), ("vst", lc - 1)]
            if lc >= 2:
                ve_ops += [("vrc", lc - 2), ("vnm", lc - 2)]
            if lc >= 3:
                ve_ops.append(("vfev", lc - 3))
            ve_ops += [("m1", q, dc) for q, dc in m1sched[lc]]
        ve_ops += [("vms", 15), ("vm2", 15), ("vst", 15),
                   ("vrc", 14), ("vnm", 14), ("vfev", 13),
                   ("vrc", 15), ("vnm", 15), ("vfev", 14), ("vfev", 15)]
        ve_ops += [("m1", q, dc) for q, dc in m1tail]
        se_ops = []
        for lc in range(NLC):
            se_ops.append(("ssq", lc))
            if lc >= 1:
                se_ops.append(("ssd", lc - 1))
            if lc >= 2:
                se_ops.append(("sapply", lc - 2))
            se_ops += [("m2", q, dc) for q, dc in m2sched[lc]]
        se_ops += [("ssd", 15), ("sapply", 14), ("sapply", 15)]
        se_ops += [("m2", q, dc) for q, dc in m2tail]
        tmk = {op: TW + i + 1 for i, op in enumerate(te_ops)}
        vmk_ = {op: VW + i + 1 for i, op in enumerate(ve_ops)}
        smk = {op: SW + i + 1 for i, op in enumerate(se_ops)}
        two = [tmk[("mm", lc)] for lc in range(NLC)]
        tftr = [tmk[("tr", lc)] for lc in range(NLC)]
        vu = [vmk_[("vu", lc)] for lc in range(NLC)]
        vms = [vmk_[("vms", lc)] for lc in range(NLC)]
        vm2 = [vmk_[("vm2", lc)] for lc in range(NLC)]
        vst = [vmk_[("vst", lc)] for lc in range(NLC)]
        vrc = [vmk_[("vrc", lc)] for lc in range(NLC)]
        vnm = [vmk_[("vnm", lc)] for lc in range(NLC)]
        vfev = [vmk_[("vfev", lc)] for lc in range(NLC)]
        vm1_all = VW + len(ve_ops)
        ssq = [smk[("ssq", lc)] for lc in range(NLC)]
        ssd = [smk[("ssd", lc)] for lc in range(NLC)]
        sapply = [smk[("sapply", lc)] for lc in range(NLC)]
        sm2_all = SW + len(se_ops)
        xr_dL = [16 * (lc // 2 + 1) for lc in range(NLC)]
        s1fm3 = s1fm[:].rearrange("p (c l) -> p c l", l=L)
        if (blk := mblk) or True:
            @blk.sync
            def _(sy):
                sy.wait_ge(sem["S"], ctx_S)
                for lc in range(NLC):
                    if lc >= 2:
                        sy.wait_ge(sem["V"], vu[lc - 2])
                    inc(sy.dma_start(xr[:, (lc % 2) * D:(lc % 2 + 1) * D], x_t[lc]),
                        "dXa" if lc % 2 == 0 else "dXb", 16)

            @blk.tensor
            def _(te):
                w(te, "dW", wbase_dL)
                w(te, "S", ctx_S)
                for op in te_ops:
                    kind, lc = op
                    if kind == "mm":
                        if lc >= 2:
                            w(te, "V", vu[lc - 2])
                        for ot in range(2):
                            for c8 in range(NCP):
                                te.matmul(
                                    psO[:, (lc % 2) * D + ot * 512:(lc % 2) * D + (ot + 1) * 512],
                                    ctx_sb[:, c8 * L + lc * 128:c8 * L + (lc + 1) * 128],
                                    wo[:, c8 * D + ot * 512:c8 * D + (ot + 1) * 512],
                                    start=(c8 == 0), stop=False)
                            ins = te.matmul(
                                psO[:, (lc % 2) * D + ot * 512:(lc % 2) * D + (ot + 1) * 512],
                                onesrow[:], borow[:, ot * 512:(ot + 1) * 512],
                                start=False, stop=True)
                        assert inc(ins, "T", 1) == two[lc]
                    else:
                        w(te, "S", sapply[lc])
                        if lc >= 2:
                            w(te, "V", vfev[lc - 2])
                        for dc in range(NDC):
                            ins = te.transpose(
                                psT[:, (lc % 2) * 1024 + dc * 128:(lc % 2) * 1024 + (dc + 1) * 128],
                                s1bf[:, lc * D + dc * 128:lc * D + (dc + 1) * 128],
                                identb[:])
                        assert inc(ins, "T", 1) == tftr[lc]

            @blk.vector
            def _(ve):
                for op in ve_ops:
                    kind, a = op[0], op[1]
                    p2 = a % 2
                    if kind == "vu":
                        w(ve, "T", two[a])
                        w(ve, "dXa" if a % 2 == 0 else "dXb", xr_dL[a])
                        if a >= 2:
                            w(ve, "V", vms[a - 2])
                        if a >= 4:
                            w(ve, "S", sapply[a - 4])
                        ins = ve.scalar_tensor_tensor(
                            u1buf[:, (a % 4) * D:(a % 4 + 1) * D],
                            psO[:, p2 * D:(p2 + 1) * D], 1.0,
                            xr[:, p2 * D:(p2 + 1) * D],
                            ALU.mult, ALU.add, accum_out=stat[:, p2:p2 + 1])
                        assert inc(ins, "V", 1) == vu[a]
                    elif kind == "vms":
                        w(ve, "S", ssq[a])
                        if a >= 2:
                            w(ve, "V", vnm[a - 2])
                        ve.tensor_scalar(stat[:, 4 + p2:5 + p2], stat[:, p2:p2 + 1],
                                         1.0 / D, None, ALU.mult)
                        ins = ve.tensor_scalar(stat[:, 6 + p2:7 + p2],
                                               stat[:, 2 + p2:3 + p2],
                                               1.0 / D, None, ALU.mult)
                        assert inc(ins, "V", 1) == vms[a]
                    elif kind == "vm2":
                        w(ve, "V", vms[a])
                        ins = ve.tensor_tensor(stat[:, 8 + p2:9 + p2],
                                               stat[:, 4 + p2:5 + p2],
                                               stat[:, 4 + p2:5 + p2], ALU.mult)
                        assert inc(ins, "V", 1) == vm2[a]
                    elif kind == "vst":
                        w(ve, "V", vm2[a])
                        ins = ve.scalar_tensor_tensor(stat[:, 10 + p2:11 + p2],
                                                      stat[:, 6 + p2:7 + p2], EPS_LN,
                                                      stat[:, 8 + p2:9 + p2],
                                                      ALU.add, ALU.subtract)
                        assert inc(ins, "V", 1) == vst[a]
                    elif kind == "vrc":
                        w(ve, "S", ssd[a])
                        if a >= 4:
                            w(ve, "S", sapply[a - 4])
                        ins = ve.reciprocal(stat[:, 14 + a % 4:15 + a % 4],
                                            stat[:, 12 + p2:13 + p2])
                        assert inc(ins, "V", 1) == vrc[a]
                    elif kind == "vnm":
                        w(ve, "V", vrc[a])
                        ins = ve.scalar_tensor_tensor(
                            stat[:, 18 + a % 4:19 + a % 4],
                            stat[:, 4 + p2:5 + p2], -1.0,
                            stat[:, 14 + a % 4:15 + a % 4], ALU.mult, ALU.mult)
                        assert inc(ins, "V", 1) == vnm[a]
                    elif kind == "vfev":
                        w(ve, "T", tftr[a])
                        ins = ve.tensor_copy(
                            s1fm3[:, :, a * 128:(a + 1) * 128],
                            psT[:, p2 * 1024:p2 * 1024 + 1024]
                            .rearrange("p (c k) -> p c k", k=128))
                        assert inc(ins, "V", 1) == vfev[a]
                    else:
                        q, dc = op[1], op[2]
                        w(ve, "V", vfev[4 * q + 3])
                        ins = ve.reduce_sum(
                            m1q[:, q * 8 + dc:q * 8 + dc + 1],
                            s1fm[:, dc * L + q * 512:dc * L + (q + 1) * 512],
                            axis=AX.X)
                        inc(ins, "V", 1)
                assert cnt["V"] == vm1_all

            @blk.scalar
            def _(se):
                for op in se_ops:
                    kind, a = op[0], op[1]
                    p2 = a % 2
                    if kind == "ssq":
                        w(se, "V", vu[a])
                        if a >= 1:
                            w(se, "S", ssq[a - 1])
                        ins = se.activation(scrap[:, 0:D],
                                            u1buf[:, (a % 4) * D:(a % 4 + 1) * D],
                                            AF.Square,
                                            accum_out=stat[:, 2 + p2:3 + p2])
                        assert inc(ins, "S", 1) == ssq[a]
                    elif kind == "ssd":
                        w(se, "V", vst[a])
                        ins = se.activation(stat[:, 12 + p2:13 + p2],
                                            stat[:, 10 + p2:11 + p2],
                                            AF.Sqrt, bias=0.0, scale=1.0)
                        assert inc(ins, "S", 1) == ssd[a]
                    elif kind == "sapply":
                        w(se, "V", vnm[a])
                        ins = se.activation(s1bf[:, a * D:(a + 1) * D],
                                            u1buf[:, (a % 4) * D:(a % 4 + 1) * D],
                                            AF.Identity,
                                            bias=stat[:, 18 + a % 4:19 + a % 4],
                                            scale=stat[:, 14 + a % 4:15 + a % 4])
                        assert inc(ins, "S", 1) == sapply[a]
                    else:
                        q, dc = op[1], op[2]
                        w(se, "S", cnt["S"])
                        w(se, "V", vfev[4 * q + 3])
                        ins = se.activation(
                            m2scr[:, (dc % 2) * 512:(dc % 2) * 512 + 512],
                            s1fm[:, dc * L + q * 512:dc * L + (q + 1) * 512],
                            AF.Square, accum_out=m2q[:, q * 8 + dc:q * 8 + dc + 1])
                        inc(ins, "S", 1)
                assert cnt["S"] == sm2_all
        tap("s1", s1bf[:], [128, NLC * D], "S")
        tap("s1fm", s1fm[:], [128, NDC * L], "V")
        if stage in ("pwo", "ft"):
            return finish_ret()
        s_wo.close(); open_stacks.remove(s_wo)
        s_ctx.close(); open_stacks.remove(s_ctx)
        s_wt2.close(); open_stacks.remove(s_wt2)


        convsel = SB("convsel", [128, NDC * D], BF16)
        biasrow = SB("biasrow", [1, D], BF16)
        dwsel = SB("dwsel", [128, NDC * R], BF16)
        upsel = SB("upsel", [64, D], BF16)
        h_sb = SB("h_sb", [64, L], BF16)
        a_sb = SB("a_sb", [128, NDC * L], BF16)
        s_fl = ExitStack()
        open_stacks.append(s_fl)
        psF1 = s_fl.enter_context(nc.psum_tensor("psF1", [128, 512], F32))

        fl_dH = consts_dH

        if (blk := mblk) or True:
            @blk.vector
            def _(ve):
                w(ve, "S", sm2_all)
                w(ve, "V", vm1_all)
                inc(ve.tensor_tensor(m1c[:], m1q[:, 0:8], m1q[:, 8:16], ALU.add),
                    "V", 1)
                inc(ve.tensor_tensor(m2c[:], m2q[:, 0:8], m2q[:, 8:16], ALU.add),
                    "V", 1)
                w(ve, "V", cnt["V"])
                inc(ve.tensor_tensor(m1c[:], m1c[:], m1q[:, 16:24], ALU.add), "V", 1)
                inc(ve.tensor_tensor(m2c[:], m2c[:], m2q[:, 16:24], ALU.add), "V", 1)
                w(ve, "V", cnt["V"])
                inc(ve.tensor_tensor(m1c[:], m1c[:], m1q[:, 24:32], ALU.add), "V", 1)
                inc(ve.tensor_tensor(m2c[:], m2c[:], m2q[:, 24:32], ALU.add), "V", 1)
        mom_V, mom_S = cnt["V"], cnt["S"]

        V3, T3 = cnt["V"], cnt["T"]
        tfl = [T3 + dc + 1 for dc in range(NDC)]
        if (blk := mblk) or True:
            @blk.vector
            def _(ve):
                w(ve, "dH", fl_dH)
                w(ve, "dL", cnt["dL"])
                inc(ve.memset(momdc8[:].rearrange("p (c k) -> p c k", k=5)[:, :, 4:5],
                              float(L)), "V", 1)
                ins = ve.tensor_tensor(sctA[:], sigT[:], sigT[:], ALU.mult)
                inc(ins, "V", 1)
                w(ve, "V", cnt["V"])
                inc(ve.reciprocal(ivA[:], sctA[:]), "V", 1)
                w(ve, "V", cnt["V"])
                inc(ve.tensor_tensor(mivA[:], muT[:], ivA[:], ALU.mult), "V", 1)
                w(ve, "V", cnt["V"])
                inc(ve.tensor_tensor(m2ivA[:], muT[:], mivA[:], ALU.mult), "V", 1)
                w(ve, "V", cnt["V"])
                dd = ddisc8[:].rearrange("p (c k) -> p c k", k=48)
                for j, srcA in enumerate((ivA, mivA, m2ivA)):
                    s8 = srcA[:].rearrange("p (c k) -> p c k", k=8)
                    ins = ve.tensor_copy(dd[:, :, 8 * j:8 * j + 8], s8)
                inc(ins, "V", 1)
                w(ve, "V", cnt["V"])
                for j, srcA in enumerate((ivA, mivA, m2ivA)):
                    s8 = srcA[:].rearrange("p (c k) -> p c k", k=8)
                    ins = ve.scalar_tensor_tensor(
                        dd[:, :, 24 + 8 * j:32 + 8 * j], dd[:, :, 8 * j:8 * j + 8],
                        -1.0, s8, ALU.mult, ALU.add)
                inc(ins, "V", 1)
                w(ve, "V", mom_V)
                md = momdc8[:].rearrange("p (c k) -> p c k", k=5)
                ve.tensor_copy(md[:, :, 0:1],
                               m1c[:].rearrange("p (c k) -> p c k", k=1))
                ins = ve.tensor_copy(md[:, :, 2:3],
                                     m2c[:].rearrange("p (c k) -> p c k", k=1))
                inc(ins, "V", 1)
                w(ve, "V", cnt["V"])
                ve.scalar_tensor_tensor(md[:, :, 1:2], md[:, :, 0:1], -1.0,
                                        m1c[:].rearrange("p (c k) -> p c k", k=1),
                                        ALU.mult, ALU.add)
                ins = ve.scalar_tensor_tensor(md[:, :, 3:4], md[:, :, 2:3], -1.0,
                                              m2c[:].rearrange("p (c k) -> p c k", k=1),
                                              ALU.mult, ALU.add)
                inc(ins, "V", 1)
        fl_build_V = cnt["V"]
        if (blk := mblk) or True:
            @blk.tensor
            def _(te):
                w(te, "V", fl_build_V)
                for dc in range(NDC):
                    ins = te.matmul(psF1[0:48, 0:5], ddisc8[:, dc * 48:(dc + 1) * 48],
                                    momdc8[:, dc * 5:(dc + 1) * 5],
                                    start=(dc == 0), stop=(dc == NDC - 1))
                    assert inc(ins, "T", 1) == tfl[dc]

        V4, T4 = cnt["V"], cnt["T"]
        mk_v = lambda k: V4 + k
        if (blk := mblk) or True:
            @blk.vector
            def _(ve):
                w(ve, "T", tfl[NDC - 1])
                assert inc(ve.tensor_copy(loss5[:], psF1[0:48, 0:5]), "V", 1) == mk_v(1)
                w(ve, "V", mk_v(1))
                assert inc(ve.tensor_tensor(ppg[:], loss5[:], wgrid[:], ALU.mult),
                           "V", 1) == mk_v(2)
                w(ve, "V", mk_v(2))
                assert inc(ve.reduce_sum(pp[:], ppg[:], axis=AX.X), "V", 1) == mk_v(3)
                w(ve, "V", mk_v(3))
                assert inc(ve.tensor_copy(pphl[:, 0:1], pp[:]), "V", 1) == mk_v(4)
                w(ve, "V", mk_v(4))
                ins = ve.scalar_tensor_tensor(pphl[:, 1:2], pphl[:, 0:1], -1.0,
                                              pp[:], ALU.mult, ALU.add)
                assert inc(ins, "V", 1) == mk_v(5)
                w(ve, "T", T4 + 1)
                assert inc(ve.tensor_copy(lcol2[:], psF1[0:8, 8:10]), "V", 1) == mk_v(6)
                w(ve, "V", mk_v(6))
                assert inc(ve.reduce_sum(lcol[:], lcol2[:], axis=AX.X), "V", 1) == mk_v(7)
                w(ve, "T", T4 + 2)
                assert inc(ve.tensor_copy(lrow[:], psF1[0:1, 16:24]), "V", 1) == mk_v(8)
                w(ve, "V", mk_v(8))
                assert inc(ve.tensor_reduce(lmin[:], lrow[:], AX.X, ALU.min),
                           "V", 1) == mk_v(9)
                w(ve, "V", mk_v(9))
                assert inc(ve.tensor_scalar(ohrow[:], lrow[:], lmin[:], None,
                                            ALU.is_equal), "V", 1) == mk_v(10)
                w(ve, "V", mk_v(10))
                ve.tensor_copy(ohrowb[:], ohrow[:])
                ins = ve.tensor_tensor(ohj[:], ohrow[:], iotar[:], ALU.mult)
                assert inc(ins, "V", 1) == mk_v(11)
                w(ve, "V", mk_v(11))
                assert inc(ve.reduce_sum(idxv[:, 0:1], ohj[:], axis=AX.X),
                           "V", 1) == mk_v(12)
                w(ve, "T", T4 + 4)
                ve.tensor_copy(ohcol[:], psF1[0:8, 24:25])
                assert inc(ve.tensor_copy(ohbc[:], psF1[0:128, 32:40]), "V", 1) == mk_v(13)

            @blk.tensor
            def _(te):
                w(te, "V", mk_v(5))
                ins = te.matmul(psF1[0:8, 8:10], esel[:], pphl[:],
                                start=True, stop=True)
                assert inc(ins, "T", 1) == T4 + 1
                w(te, "V", mk_v(7))
                ins = te.transpose(psF1[0:1, 16:24], lcol[:], identf[0:8, 0:8])
                assert inc(ins, "T", 1) == T4 + 2
                w(te, "V", mk_v(11))
                ins = te.transpose(psF1[0:8, 24:25], ohrow[:], identf[0:1, 0:1])
                assert inc(ins, "T", 1) == T4 + 3
                ins = te.matmul(psF1[0:128, 32:40], onesrow[:], ohrowb[:],
                                start=True, stop=True)
                assert inc(ins, "T", 1) == T4 + 4
        idx_V = cnt["V"]
        tap("lrow", lrow[:], [1, 8], "V", mk_v(8))
        tap("ohbc", ohbc[:], [128, 8], "V")

        if (blk := mblk) or True:
            @blk.vector
            def _(ve):
                w(ve, "V", mk_v(13))
                inc(ve.tensor_scalar(bucol[:, 0:8], uball[:, 0:8], ohbc[:, 0:1],
                                     None, ALU.mult), "V", 1)
                inc(ve.tensor_scalar(bdcol[:, 0:1], dball[:, 0:1], ohbc[0:64, 0:1],
                                     None, ALU.mult), "V", 1)
                for e in range(1, E):
                    w(ve, "V", cnt["V"])
                    src_u = bucol[:, 8 * ((e + 1) % 2):8 * ((e + 1) % 2) + 8]
                    dst_u = bucol[:, 8 * (e % 2):8 * (e % 2) + 8]
                    inc(ve.scalar_tensor_tensor(dst_u, uball[:, e * 8:(e + 1) * 8],
                                                ohbc[:, e:e + 1], src_u, ALU.mult,
                                                ALU.add), "V", 1)
                    src_d = bdcol[:, (e + 1) % 2:(e + 1) % 2 + 1]
                    dst_d = bdcol[:, e % 2:e % 2 + 1]
                    inc(ve.scalar_tensor_tensor(dst_d, dball[:, e:e + 1],
                                                ohbc[0:64, e:e + 1], src_d, ALU.mult,
                                                ALU.add), "V", 1)
                w(ve, "V", cnt["V"])
                inc(ve.tensor_copy(bucolb[:], bucol[:, 8:16]), "V", 1)
                w(ve, "V", mk_v(12))
                inc(ve.tensor_copy(idxbb[:], idxv[:, 0:1]), "V", 1)
        idxbb_V = cnt["V"]
        budcol_V = idxbb_V - 1
        TG = cnt["T"]
        if (blk := mblk) or True:
            @blk.tensor
            def _(te):
                w(te, "V", idxbb_V)
                ins = te.matmul(psF1[0:128, 40:41], onesrow[:], idxbb[:],
                                start=True, stop=True)
                inc(ins, "T", 1)
        tidxb = cnt["T"]
        if (blk := mblk) or True:
            @blk.vector
            def _(ve):
                w(ve, "T", tidxb)
                w(ve, "G", iota_G)
                inc(ve.tensor_copy(idxbc[:], psF1[0:128, 40:41]), "V", 1)
                w(ve, "V", cnt["V"])
                for gi in range(17):
                    if gi < 8:
                        mulc, addc = 1024.0, float(gi * 128)
                    elif gi < 16:
                        mulc, addc = 1024.0, float((gi - 8) * 128)
                    else:
                        mulc, addc = 64.0, 0.0
                    inc(ve.tensor_scalar(gidxf[:, gi:gi + 1], idxbc[:], mulc, addc,
                                         ALU.mult, ALU.add), "V", 1)
                w(ve, "V", cnt["V"])
                inc(ve.tensor_single_scalar(gidxf[:], gidxf[:], iotac[:], ALU.add),
                    "V", 1)
                w(ve, "V", cnt["V"])
                inc(ve.tensor_copy(gidx17[:], gidxf[:]), "V", 1)
        gidx_V = cnt["V"]

        G0 = cnt["dG"]
        if (blk := mblk) or True:
            @blk.gpsimd
            def _(g):
                w(g, "V", gidx_V)
                for c in range(8):
                    inc(g.indirect_dma_start(
                        dwsel[:, c * R:(c + 1) * R], None, d_in["downW"],
                        IndirectOffsetOnAxis(gidx17[:, 8 + c:9 + c], 0)), "dG", 16)
                inc(g.indirect_dma_start(
                    upsel[:], None, d_in["upW"],
                    IndirectOffsetOnAxis(gidx17[0:64, 16:17], 0)), "dGu", 16)
                for c in range(8):
                    inc(g.indirect_dma_start(
                        convsel[:, c * D:(c + 1) * D], None, d_in["convT"],
                        IndirectOffsetOnAxis(gidx17[:, c:c + 1], 0)), "dGc", 16)
        down_dG = cnt["dG"]
        up_dG = cnt["dGu"]
        conv_dGc = cnt["dGc"]
        tap("dwsel", dwsel[:], [128, NDC * R], "dG")
        tap("upsel", upsel[:], [64, D], "dGu")
        tap("convsel", convsel[:], [128, NDC * D], "dGc")
        if stage == "fl":
            return finish_ret()
        s_fl.close(); open_stacks.remove(s_fl)

        s_fn = ExitStack()
        open_stacks.append(s_fn)
        psH = s_fn.enter_context(nc.psum_tensor("psH", [64, 1024], F32))
        psA2 = s_fn.enter_context(nc.psum_tensor("psA2", [128, 1024], F32))
        psO2 = s_fn.enter_context(nc.psum_tensor("psO2", [128, 2 * D], F32))

        T6, V6, S6 = cnt["T"], cnt["V"], cnt["S"]
        tdn = [T6 + lt + 1 for lt in range(4)]
        shv = [S6 + lt + 1 for lt in range(4)]
        tup = [T6 + 4 + i + 1 for i in range(32)]
        tbias3 = T6 + 4 + 32 + 1
        aev = [0] * 32
        _vv, _ss = V6, S6 + 4
        for i in range(32):
            if i % 2 == 0:
                _vv += 1; aev[i] = ("V", _vv)
            else:
                _ss += 1; aev[i] = ("S", _ss)
        vbias3 = _vv + 1
        if (blk := mblk) or True:
            @blk.tensor
            def _(te):
                w(te, "dW", wbase_dL)
                w(te, "dG", down_dG)
                for lt in range(4):
                    if lt >= 2:
                        w(te, "S", shv[lt - 2])
                    for dc in range(NDC):
                        ins = te.matmul(
                            psH[:, (lt % 2) * 512:(lt % 2 + 1) * 512],
                            dwsel[:, dc * R:(dc + 1) * R],
                            s1fm[:, dc * L + lt * 512:dc * L + (lt + 1) * 512],
                            start=(dc == 0), stop=(dc == NDC - 1))
                    assert inc(ins, "T", 1) == tdn[lt]
                w(te, "dGu", up_dG)
                for oc in range(NDC):
                    for lt in range(4):
                        i = oc * 4 + lt
                        if i >= 2:
                            w(te, aev[i - 2][0], aev[i - 2][1])
                        w(te, "S", shv[lt])
                        ins = te.matmul(
                            psA2[:, (i % 2) * 512:(i % 2 + 1) * 512],
                            upsel[:, oc * 128:(oc + 1) * 128],
                            h_sb[:, lt * 512:(lt + 1) * 512],
                            start=True, stop=True)
                        assert inc(ins, "T", 1) == tup[i]
                w(te, "dGc", conv_dGc)
                w(te, "V", budcol_V)
                for ot in range(2):
                    te.matmul(psH[0:1, ot * 512:(ot + 1) * 512], ohcol[:],
                              cball[:, ot * 512:(ot + 1) * 512],
                              start=True, stop=False)
                    for dc in range(NDC):
                        ins = te.matmul(
                            psH[0:1, ot * 512:(ot + 1) * 512],
                            bucolb[:, dc:dc + 1],
                            convsel[:, dc * D + ot * 512:dc * D + (ot + 1) * 512],
                            start=False, stop=(dc == NDC - 1))
                assert inc(ins, "T", 1) == tbias3

            @blk.scalar
            def _(se):
                for lt in range(4):
                    w(se, "T", tdn[lt])
                    ins = se.activation(h_sb[:, lt * 512:(lt + 1) * 512],
                                        psH[:, (lt % 2) * 512:(lt % 2 + 1) * 512],
                                        AF.Relu, bias=bdcol[:, 1:2], scale=1.0)
                    assert inc(ins, "S", 1) == shv[lt]

            @blk.vector
            def _(ve):
                w(ve, "V", gidx_V)
                for i in range(0, 32, 2):
                    oc, lt = i // 4, i % 4
                    w(ve, "T", tup[i])
                    ins = ve.tensor_copy(
                        a_sb[:, oc * L + lt * 512:oc * L + (lt + 1) * 512],
                        psA2[:, (i % 2) * 512:(i % 2 + 1) * 512])
                    assert inc(ins, "V", 1) == aev[i][1]
                w(ve, "T", tbias3)
                ins = ve.tensor_tensor(biasrow[:], psH[0:1, 0:1024], bbrow[:],
                                       ALU.add)
                assert inc(ins, "V", 1) == vbias3

            @blk.scalar
            def _(se):
                for i in range(1, 32, 2):
                    oc, lt = i // 4, i % 4
                    w(se, "T", tup[i])
                    ins = se.activation(
                        a_sb[:, oc * L + lt * 512:oc * L + (lt + 1) * 512],
                        psA2[:, (i % 2) * 512:(i % 2 + 1) * 512],
                        AF.Copy, bias=0.0, scale=1.0)
                    assert inc(ins, "S", 1) == aev[i][1]
        a_V = cnt["V"]
        a_S = cnt["S"]
        assert cnt["T"] == tbias3 and a_V == vbias3
        tap("h", h_sb[:], [64, L], "S")
        tap("a", a_sb[:], [128, NDC * L], "V")

        T7, V7, S7 = cnt["T"], cnt["V"], cnt["S"]
        tout = [T7 + lc + 1 for lc in range(NLC)]
        vu2 = [V7 + 6 * lc + 1 for lc in range(NLC)]
        vms2 = [V7 + 6 * lc + 2 for lc in range(NLC)]
        vm22 = [V7 + 6 * lc + 3 for lc in range(NLC)]
        vst2 = [V7 + 6 * lc + 4 for lc in range(NLC)]
        vrc2 = [V7 + 6 * lc + 5 for lc in range(NLC)]
        vap2 = [V7 + 6 * lc + 6 for lc in range(NLC)]
        ssq2 = [S7 + 2 * lc + 1 for lc in range(NLC)]
        ssd2 = [S7 + 2 * lc + 2 for lc in range(NLC)]
        oout = [16 * (lc // 2 + 1) for lc in range(NLC)]
        if (blk := mblk) or True:
            @blk.tensor
            def _(te):
                w(te, "V", vbias3)
                w(te, "S", a_S)
                w(te, "dGc", conv_dGc)
                for lc in range(NLC):
                    if lc >= 2:
                        w(te, "V", vu2[lc - 2])
                    for ot in range(2):
                        for dc in range(NDC):
                            te.matmul(
                                psO2[:, (lc % 2) * D + ot * 512:(lc % 2) * D + (ot + 1) * 512],
                                s1fm[:, dc * L + lc * 128:dc * L + (lc + 1) * 128],
                                wbase[:, dc * D + ot * 512:dc * D + (ot + 1) * 512],
                                start=(dc == 0), stop=False)
                        for dc in range(NDC):
                            te.matmul(
                                psO2[:, (lc % 2) * D + ot * 512:(lc % 2) * D + (ot + 1) * 512],
                                a_sb[:, dc * L + lc * 128:dc * L + (lc + 1) * 128],
                                convsel[:, dc * D + ot * 512:dc * D + (ot + 1) * 512],
                                start=False, stop=False)
                        ins = te.matmul(
                            psO2[:, (lc % 2) * D + ot * 512:(lc % 2) * D + (ot + 1) * 512],
                            onesrow[:], biasrow[:, ot * 512:(ot + 1) * 512],
                            start=False, stop=True)
                    assert inc(ins, "T", 1) == tout[lc]

            @blk.vector
            def _(ve):
                for lc in range(NLC):
                    w(ve, "T", tout[lc])
                    if lc >= 2:
                        w(ve, "dOa" if lc % 2 == 0 else "dOb", oout[lc - 2])
                    ins = ve.scalar_tensor_tensor(
                        s2t[:, (lc % 2) * D:(lc % 2 + 1) * D],
                        psO2[:, (lc % 2) * D:(lc % 2 + 1) * D], 1.0,
                        s1bf[:, lc * D:(lc + 1) * D],
                        ALU.mult, ALU.add, accum_out=stat[:, 0:1])
                    assert inc(ins, "V", 1) == vu2[lc]
                    w(ve, "S", ssq2[lc])
                    ve.tensor_scalar(stat[:, 2:3], stat[:, 0:1], 1.0 / D, None, ALU.mult)
                    ins = ve.tensor_scalar(stat[:, 4:5], stat[:, 1:2], 1.0 / D, None,
                                           ALU.mult)
                    assert inc(ins, "V", 1) == vms2[lc]
                    w(ve, "V", vms2[lc])
                    ins = ve.tensor_tensor(stat[:, 5:6], stat[:, 2:3], stat[:, 2:3], ALU.mult)
                    assert inc(ins, "V", 1) == vm22[lc]
                    w(ve, "V", vm22[lc])
                    ins = ve.scalar_tensor_tensor(stat[:, 6:7], stat[:, 4:5], EPS_LN,
                                                  stat[:, 5:6], ALU.add, ALU.subtract)
                    assert inc(ins, "V", 1) == vst2[lc]
                    w(ve, "S", ssd2[lc])
                    ins = ve.reciprocal(stat[:, 8:9], stat[:, 7:8])
                    assert inc(ins, "V", 1) == vrc2[lc]
                    w(ve, "V", vrc2[lc])
                    ins = ve.tensor_scalar(
                        s2t[:, (lc % 2) * D:(lc % 2 + 1) * D],
                        s2t[:, (lc % 2) * D:(lc % 2 + 1) * D],
                        stat[:, 2:3], stat[:, 8:9], ALU.subtract, ALU.mult)
                    assert inc(ins, "V", 1) == vap2[lc]

            @blk.scalar
            def _(se):
                for lc in range(NLC):
                    w(se, "V", vu2[lc])
                    ins = se.activation(scrap[:, 0:D],
                                        s2t[:, (lc % 2) * D:(lc % 2 + 1) * D],
                                        AF.Square, accum_out=stat[:, 1:2])
                    assert inc(ins, "S", 1) == ssq2[lc]
                    w(se, "V", vst2[lc])
                    ins = se.activation(stat[:, 7:8], stat[:, 6:7],
                                        AF.Sqrt, bias=0.0, scale=1.0)
                    assert inc(ins, "S", 1) == ssd2[lc]

            @blk.sync
            def _(sy):
                for lc in range(NLC):
                    sy.wait_ge(sem["V"], vap2[lc])
                    ins = sy.dma_start(d_out.rearrange("(t p) d -> t p d", p=128)[lc],
                                       s2t[:, (lc % 2) * D:(lc % 2 + 1) * D])
                    assert inc(ins, "dOa" if lc % 2 == 0 else "dOb", 16) == oout[lc]
        s_fn.close(); open_stacks.remove(s_fn)
        s_wt.close(); open_stacks.remove(s_wt)
        return finish_ret()


def _finish(nc, sem, cnt, inc, tap_list):
    with nc.Block() as blk:
        @blk.gpsimd
        def _(g):
            for name, dram, sbap, semn, val in tap_list:
                g.wait_ge(sem[semn], val)
                inc(g.dma_start(dram, sbap), "dT", 16)
            g.wait_ge(sem["dT"], cnt["dT"])
            g.wait_ge(sem["dOa"], cnt["dOa"])
            g.wait_ge(sem["dOb"], cnt["dOb"])



def kernel(**inputs):
    import numpy as np
    from concourse.bass_utils import run_bass_kernel_spmd
    shared = host_prep(inputs)
    x = np.asarray(inputs["L0_attn_x"], np.float32)
    nc = build()
    maps = []
    for b in range(8):
        m = {"i_" + k: v for k, v in shared.items()}
        m["x"] = np.ascontiguousarray(x[b])
        maps.append(m)
    res = run_bass_kernel_spmd(nc, maps, core_ids=list(range(8)))
    out = np.stack([np.asarray(res.results[b]["out"], np.float32) for b in range(8)], 0)
    return out


# version 16
# speedup vs baseline: 1.1966x; 1.1966x over previous
import numpy as np
from contextlib import ExitStack

import concourse.bass as bass
import concourse.mybir as mybir
from concourse.bass import IndirectOffsetOnAxis

F32 = mybir.dt.float32
BF16 = mybir.dt.bfloat16
I32 = mybir.dt.int32
ALU = mybir.AluOpType
AF = mybir.ActivationFunctionType
AX = mybir.AxisListType

L, D, H, DH, NF, E, R = 2048, 1024, 16, 64, 64, 8, 64
NLC, NDC, NCP = 16, 8, 8
EPS_ATT = 1e-6
EPS_LN = 1e-5


def host_prep(inp):
    c = np.ascontiguousarray
    f32 = np.float32
    g = lambda k: np.asarray(inp[k], f32)
    s = {}
    s["wqT"] = c(g("L0_attn_Wq").T)
    s["wkT"] = c(g("L0_attn_Wk").T)
    s["wvT"] = c(g("L0_attn_Wv").T)
    s["woT"] = c(g("L0_attn_Wo").T)
    s["baseT"] = c(g("L0_ffn_base_W").T)
    s["convT"] = c(g("L0_ffn_conv_W").transpose(0, 2, 1).reshape(E * D, D))
    s["downW"] = c(g("L0_ffn_down_W").reshape(E * D, R))
    s["upW"] = c(g("L0_ffn_up_W").reshape(E * R, D))
    s["dball"] = c(g("L0_ffn_down_b").T)
    s["uball"] = c(g("L0_ffn_up_b").reshape(E, 8, 128).transpose(2, 0, 1).reshape(128, 64))
    s["cball"] = c(g("L0_ffn_conv_b"))
    s["muT"] = c(g("L0_ffn_disc_mu").T)
    s["sigT"] = c(g("L0_ffn_disc_sigma").T)
    W0 = g("L0_attn_Wrf")[0]
    w0blk = np.zeros((NCP * 128, 128), f32)
    for cp in range(NCP):
        w0blk[cp * 128:cp * 128 + 64, 0:64] = W0[2 * cp]
        w0blk[cp * 128 + 64:cp * 128 + 128, 64:128] = W0[2 * cp + 1]
    s["w0blk"] = w0blk
    s["bqk"] = c(np.stack([g("L0_attn_bq").reshape(8, 128).T,
                           g("L0_attn_bk").reshape(8, 128).T],
                          axis=2).reshape(128, 16))
    s["bvrow"] = c(g("L0_attn_bv").reshape(1, D))
    s["borow"] = c(g("L0_attn_bo").reshape(1, D))
    s["bbrow"] = c(g("L0_ffn_base_b").reshape(1, D))
    s["identc"] = np.eye(128, dtype=f32)
    dsel = np.zeros((16, NCP * 128), f32)
    for cp in range(NCP):
        dsel[2 * cp, cp * 128:cp * 128 + 64] = 1.0
        dsel[2 * cp + 1, cp * 128 + 64:cp * 128 + 128] = 1.0
    dsel64 = np.zeros((64, NCP * 128), f32)
    dsel64[0:16] = dsel
    dsel64[32:48] = dsel
    s["dsel64"] = dsel64
    par = np.zeros((1, 256), f32)
    par[0, 0:64] = 1.0
    par[0, 128 + 64:256] = 1.0
    s["parab"] = par
    kkm = np.zeros((128, NCP * 16), f32)
    for cp in range(NCP):
        kkm[0:64, cp * 16 + 2 * cp] = -0.5
        kkm[64:128, cp * 16 + 2 * cp + 1] = -0.5
    s["kkmask"] = kkm
    s["onesrow"] = np.ones((1, 128), f32)
    wg = np.zeros((48, 5), f32)
    wg[0:8, 2] = 1.0; wg[0:8, 3] = 1.0
    wg[24:32, 2] = 1.0
    wg[8:16, 0] = -2.0; wg[8:16, 1] = -2.0
    wg[32:40, 0] = -2.0
    wg[16:24, 4] = 1.0; wg[40:48, 4] = 1.0
    s["wgrid"] = wg
    es = np.zeros((48, 8), f32)
    for e in range(8):
        for blk in range(6):
            es[blk * 8 + e, e] = 1.0
    s["esel"] = es
    return s


def build(taps=(), stage="full"):
    nc = bass.Bass("TRN2", target_bir_lowering=False)
    taps = set(taps)
    tap_list = []
    sem, cnt = {}, {}

    import sys as _sys
    _marks = {}
    build._marks = _marks

    def inc(ins, name, n=1):
        ins.then_inc(sem[name], n)
        cnt[name] += n
        _marks[(name, cnt[name])] = _sys._getframe(1).f_lineno
        return cnt[name]

    def w(eng, name, val):
        assert val <= 10 ** 9
        eng.wait_ge(sem[name], val)

    dt = nc.dram_tensor
    d_x = dt("x", [L, D], F32, kind="ExternalInput").ap()
    names = [("wqT", [D, D]), ("wkT", [D, D]), ("wvT", [D, D]), ("woT", [D, D]),
             ("baseT", [D, D]), ("convT", [E * D, D]), ("downW", [E * D, R]),
             ("upW", [E * R, D]), ("dball", [64, 8]), ("uball", [128, 64]),
             ("cball", [8, D]), ("muT", [D, E]), ("sigT", [D, E]),
             ("w0blk", [NCP * 128, 128]), ("bqk", [128, 16]), ("bvrow", [1, D]),
             ("borow", [1, D]), ("bbrow", [1, D]), ("identc", [128, 128]),
             ("dsel64", [64, NCP * 128]),
             ("parab", [1, 256]), ("kkmask", [128, NCP * 16]),
             ("onesrow", [1, 128]), ("wgrid", [48, 5]), ("esel", [48, 8])]
    d_in = {nm: dt("i_" + nm, shp, F32, kind="ExternalInput").ap() for nm, shp in names}
    d_out = dt("out", [L, D], F32, kind="ExternalOutput").ap()
    x_t = d_x.rearrange("(t p) d -> t p d", p=128)

    def tap(name, sbuf_ap, shape, semn, val=None):
        if name in taps:
            tp = dt("tap_" + name, shape, F32, kind="ExternalOutput").ap()
            tap_list.append((name, tp, sbuf_ap, semn,
                             cnt[semn] if val is None else val))

    ctx = ExitStack()
    SB = lambda n, s, t: ctx.enter_context(nc.sbuf_tensor(n, s, t))
    open_stacks = []

    def finish_ret():
        mblk.__exit__(None, None, None)
        _finish(nc, sem, cnt, inc, tap_list)
        for st_ in reversed(open_stacks):
            st_.close()
        return nc

    with ctx:
        for s_ in ["dL", "dH", "dW", "dXa", "dXb", "dG", "dGu", "dGc", "dOa", "dOb", "dT", "dC", "dRa", "dRb", "dRc", "dRd", "T", "V", "S", "G", "P"]:
            sem[s_] = ctx.enter_context(nc.semaphore(s_))
            cnt[s_] = 0

        identb = SB("identb", [128, 128], BF16)
        identf = SB("identf", [128, 128], F32)
        w0blk = SB("w0blk", [128, NCP * 128], BF16)
        dsel64 = SB("dsel64", [64, NCP * 128], BF16)
        parab = SB("parab", [1, 256], BF16)
        kkmask = SB("kkmask", [128, NCP * 16], BF16)
        onesrow = SB("onesrow", [1, 128], BF16)
        bqk = SB("bqk", [128, 16], F32)
        bvrow = SB("bvrow", [1, D], BF16)
        wgrid = SB("wgrid", [48, 5], F32)
        esel = SB("esel", [48, 8], BF16)

        m1c = SB("m1c", [128, 8], F32)
        m2c = SB("m2c", [128, 8], F32)
        momdc8 = SB("momdc8", [128, 40], BF16)
        sctA = SB("sctA", [128, 64], F32)
        ivA = SB("ivA", [128, 64], F32)
        mivA = SB("mivA", [128, 64], F32)
        m2ivA = SB("m2ivA", [128, 64], F32)
        ddisc8 = SB("ddisc8", [128, 384], BF16)
        loss5 = SB("loss5", [48, 5], F32)
        ppg = SB("ppg", [48, 5], F32)
        pp = SB("pp", [48, 1], F32)
        pphl = SB("pphl", [48, 2], BF16)
        lcol2 = SB("lcol2", [8, 2], F32)
        lcol = SB("lcol", [8, 1], F32)
        lrow = SB("lrow", [1, 8], F32)
        lmin = SB("lmin", [1, 1], F32)
        ohrow = SB("ohrow", [1, 8], F32)
        ohrowb = SB("ohrowb", [1, 8], BF16)
        ohj = SB("ohj", [1, 8], F32)
        ohcol = SB("ohcol", [8, 1], BF16)
        ohbc = SB("ohbc", [128, 8], F32)
        idxv = SB("idxv", [1, 2], F32)
        iotar = SB("iotar", [1, 8], F32)
        bdcol = SB("bdcol", [64, 2], F32)
        bucol = SB("bucol", [128, 16], F32)
        bucolb = SB("bucolb", [128, 8], BF16)
        muT = SB("muT", [128, NDC * E], F32)
        sigT = SB("sigT", [128, NDC * E], F32)
        dball = SB("dball", [64, 8], F32)
        uball = SB("uball", [128, 64], F32)
        bbrow = SB("bbrow", [1, D], BF16)
        iotac = SB("iotac", [128, 1], F32)
        idxbc = SB("idxbc", [128, 1], F32)
        idxbb = SB("idxbb", [1, 1], BF16)
        gidxf = SB("gidxf", [128, 17], F32)
        gidx17 = SB("gidx17", [128, 17], I32)

        mblk = nc.Block()
        mblk.__enter__()

        if (blk := mblk) or True:
            @blk.gpsimd
            def _(g):
                g.iota(iotar[:], [[1, 8]], base=0, channel_multiplier=0,
                       allow_small_or_imprecise_dtypes=True)
                g.iota(iotac[:], [[1, 1]], base=0, channel_multiplier=1,
                       allow_small_or_imprecise_dtypes=True)
                inc(g.nop(), "G", 1)
                inc(g.dma_start(identb[:], d_in["identc"]), "dL", 16)

            @blk.sync
            def _(sy):
                for dstap, nm, rr in [
                        (identf[:], "identc", None), (bqk[:], "bqk", None),
                        (wgrid[:], "wgrid", None),
                        (muT[:].rearrange("p (c e) -> p c e", e=E), "muT",
                         "(c p) e -> p c e"),
                        (sigT[:].rearrange("p (c e) -> p c e", e=E), "sigT",
                         "(c p) e -> p c e"),
                        (dball[:], "dball", None), (uball[:], "uball", None)]:
                    src = d_in[nm]
                    if rr:
                        src = src.rearrange(rr, p=128)
                    inc(sy.dma_start(dstap, src), "dH", 16)
        consts_dL = cnt["dL"]
        consts_dH = cnt["dH"]
        iota_G = cnt["G"]

        s_big = ExitStack()
        open_stacks.append(s_big)
        wq = s_big.enter_context(nc.sbuf_tensor("wq", [128, NDC * D], BF16, side="right"))
        wk = s_big.enter_context(nc.sbuf_tensor("wk", [128, NDC * D], BF16, side="right"))
        wv = s_big.enter_context(nc.sbuf_tensor("wv", [128, NDC * D], BF16, side="right"))
        xfm = s_big.enter_context(nc.sbuf_tensor("xfm", [128, NDC * L], BF16, side="right"))

        s_att = ExitStack()
        open_stacks.append(s_att)
        SA = lambda n, s, t: s_att.enter_context(nc.sbuf_tensor(n, s, t))
        eq_sb = SA("eq_sb", [128, NCP * L], BF16)
        zd8 = SA("zd8", [8, 128], F32)
        d16 = SA("d16", [128, 8], F32)
        dcolA = SA("dcolA", [8, 1], F32)
        dcolB = SA("dcolB", [8, 1], F32)
        numS = [SA(f"numS{c}", [128, 128], BF16) for c in range(NCP)]
        denS = [SA(f"denS{c}", [128, 16], BF16) for c in range(NCP)]
        s_att1 = ExitStack()
        open_stacks.append(s_att1)
        SA1 = lambda n, s, t: s_att1.enter_context(nc.sbuf_tensor(n, s, t))
        v_sb = SA1("v_sb", [128, NLC * H * 65], BF16)
        k_sb = SA1("k_sb", [128, NCP * L], BF16)
        mkA = SA1("mkA", [1, 8], BF16)
        mkB = SA1("mkB", [1, 8], BF16)
        zArow = SA1("zArow", [1, 8], BF16)
        zBrow = SA1("zBrow", [1, 8], BF16)
        kkT_sb = SA1("kkT_sb", [128, 256], F32)
        qqT_sb = kkT_sb
        mqfm = None
        colmax = SA1("colmax", [128, 1], F32)
        scr3 = SA1("scr3", [1, 128], F32)
        mbias = SA1("mbias", [128, 1], F32)
        ebias = SA1("ebias", [128, 8], F32)
        v4 = v_sb[:].rearrange("p (lc h c) -> p lc h c", lc=NLC, h=H, c=65)
        s_aps = ExitStack()
        open_stacks.append(s_aps)
        psK = s_aps.enter_context(nc.psum_tensor("psK", [128, 1024], F32))
        psS = s_aps.enter_context(nc.psum_tensor("psS", [128, 512], F32))

        with ExitStack() as ph:
            xb = ph.enter_context(nc.sbuf_tensor("xb", [128, 4 * D], BF16, side="right"))
            ptr = ph.enter_context(nc.psum_tensor("ptr", [128, 2048], BF16))
            _pt, _pv = cnt["T"], cnt["V"]
            txf = [0] * NLC
            tv = [[0, 0] for _ in range(NLC)]
            vxf = [0] * NLC
            vv = [[0, 0] for _ in range(NLC)]
            _ones_v = _pv + 1
            _pv += 1
            emit_order = []
            for lc in range(NLC):
                emit_order.append(("tr", lc))
                if lc >= 4:
                    emit_order.append(("v", lc - 4))
            for lc in range(NLC - 4, NLC):
                emit_order.append(("v", lc))
            for kind, lc in emit_order:
                if kind == "tr":
                    txf[lc] = _pt + 1
                    _pt += 1
                    vxf[lc] = _pv + 1
                    _pv += 1
                else:
                    tv[lc][0] = _pt + 1
                    tv[lc][1] = _pt + 2
                    _pt += 2
                    vv[lc][0] = _pv + 1
                    vv[lc][1] = _pv + 2
                    _pv += 2
            wv_dL = cnt["dW"] + 16
            grp2_dL = [consts_dL + 32,
                       consts_dL + 96] + \
                      [consts_dL + 96 + 32 * g for g in range(1, 7)]
            wqk_dL = wv_dL + 32
            late_dL = grp2_dL[7] + 6 * 16

            if (blk := mblk) or True:
                @blk.gpsimd
                def _(g):
                    def tiles(grp):
                        for i in range(2):
                            lc = grp * 2 + i
                            slot = (grp % 2) * 2 + i
                            inc(g.dma_start(xb[:, slot * D:(slot + 1) * D],
                                            x_t[lc]), "dL", 16)
                    tiles(0)
                    g.wait_ge(sem["dL"], grp2_dL[0])
                    inc(g.dma_start(onesrow[:], d_in["onesrow"]), "dL", 16)
                    inc(g.dma_start(bvrow[:], d_in["bvrow"]), "dL", 16)
                    g.wait_ge(sem["dL"], consts_dL + 64)
                    tiles(1)
                    inc(g.dma_start(
                        wv[:].rearrange("p (c n) -> p c n", n=D),
                        d_in["wvT"].rearrange("(c p) n -> p c n", p=128)), "dW", 16)
                    for grp in range(2, 8):
                        g.wait_ge(sem["dL"], grp2_dL[grp - 1])
                        g.wait_ge(sem["T"], txf[2 * (grp - 1) - 1])
                        tiles(grp)
                    for wsb, nm in [(wq, "wqT"), (wk, "wkT")]:
                        inc(g.dma_start(
                            wsb[:].rearrange("p (c n) -> p c n", n=D),
                            d_in[nm].rearrange("(c p) n -> p c n", p=128)), "dW", 16)
                    g.wait_ge(sem["dL"], grp2_dL[7])
                    for dstap, nm, rr in [
                            (w0blk[:].rearrange("p (c n) -> p c n", n=128),
                             "w0blk", "(c p) n -> p c n"),
                            (dsel64[:], "dsel64", None),
                            (kkmask[:], "kkmask", None),
                            (parab[:], "parab", None),
                            (esel[:], "esel", None), (bbrow[:], "bbrow", None)]:
                        srcp = d_in[nm]
                        if rr:
                            srcp = srcp.rearrange(rr, p=128)
                        inc(g.dma_start(dstap, srcp), "dL", 16)
                    assert cnt["dL"] == late_dL

                @blk.tensor
                def _(te):
                    for kind, lc in emit_order:
                        if kind == "tr":
                            w(te, "dL", grp2_dL[lc // 2])
                            if lc >= 2:
                                w(te, "V", vxf[lc - 2])
                            slot = ((lc // 2) % 2) * 2 + (lc % 2)
                            for dc in range(NDC):
                                ins = te.transpose(
                                    ptr[:, (lc % 2) * 1024 + dc * 128:(lc % 2) * 1024 + (dc + 1) * 128],
                                    xb[:, slot * D + dc * 128:slot * D + (dc + 1) * 128],
                                    identb[:])
                            assert inc(ins, "T", 1) == txf[lc]
                        else:
                            w(te, "V", vxf[lc])
                            if lc == 0:
                                w(te, "dW", wv_dL)
                                w(te, "dL", consts_dL + 64)
                                w(te, "V", _ones_v)
                            for ot in range(2):
                                it = lc * 2 + ot
                                if it >= 2:
                                    pit = it - 2
                                    w(te, "V", vv[pit // 2][pit % 2])
                                for dc in range(NDC):
                                    te.matmul(
                                        psK[:, (it % 2) * 512:(it % 2 + 1) * 512],
                                        xfm[:, dc * L + lc * 128:dc * L + (lc + 1) * 128],
                                        wv[:, dc * D + ot * 512:dc * D + (ot + 1) * 512],
                                        start=(dc == 0), stop=False)
                                ins = te.matmul(
                                    psK[:, (it % 2) * 512:(it % 2 + 1) * 512],
                                    onesrow[:], bvrow[:, ot * 512:(ot + 1) * 512],
                                    start=False, stop=True)
                                assert inc(ins, "T", 1) == tv[lc][ot]

                @blk.vector
                def _(ve):
                    assert inc(ve.memset(v4[:, :, :, 64:65], 1.0), "V", 1) == _ones_v
                    for kind, lc in emit_order:
                        if kind == "tr":
                            w(ve, "T", txf[lc])
                            dst = xfm[:].rearrange("p (c l) -> p c l", l=L)[:, :, lc * 128:(lc + 1) * 128]
                            ins = ve.tensor_copy(
                                dst, ptr[:, (lc % 2) * 1024:(lc % 2) * 1024 + 1024]
                                .rearrange("p (c k) -> p c k", k=128))
                            assert inc(ins, "V", 1) == vxf[lc]
                        else:
                            for ot in range(2):
                                it = lc * 2 + ot
                                w(ve, "T", tv[lc][ot])
                                ins = ve.tensor_copy(
                                    v4[:, lc, ot * 8:(ot + 1) * 8, 0:64],
                                    psK[:, (it % 2) * 512:(it % 2 + 1) * 512]
                                    .rearrange("p (h c) -> p h c", c=64))
                                assert inc(ins, "V", 1) == vv[lc][ot]
        xfm_V = vxf[NLC - 1]
        wqkv_dL = wqk_dL
        s_apq = ExitStack()
        open_stacks.append(s_apq)
        psQ = s_apq.enter_context(nc.psum_tensor("psQ", [128, L], F32))
        s_kt = ExitStack()
        open_stacks.append(s_kt)
        psKT = s_kt.enter_context(nc.psum_tensor("psKT", [128, 512], F32))
        v_V = cnt["V"]
        tap("xfm", xfm[:], [128, NDC * L], "V")
        tap("v", v_sb[:], [128, NLC * H * 65], "V")
        if stage in ("t0", "p1a"):
            return finish_ret()

        s_qs = ExitStack()
        open_stacks.append(s_qs)
        sq_sb = s_qs.enter_context(nc.sbuf_tensor("sq_sb", [128, L], BF16))
        mqfm = s_qs.enter_context(nc.sbuf_tensor("mqfm", [64, L], BF16))
        T1, V1, S1 = cnt["T"], cnt["V"], cnt["S"]
        tq = [[0] * 4 for _ in range(NCP)]
        tqq = [[0] * 4 for _ in range(NCP)]
        _p = T1
        for c in range(NCP):
            for lt in range(4):
                tq[c][lt] = _p + 1; _p += 1
            if c >= 1:
                for lt in range(4):
                    tqq[c - 1][lt] = _p + 1; _p += 1
        for lt in range(4):
            tqq[NCP - 1][lt] = _p + 1; _p += 1
        vqe = [[V1 + 8 * c + lt + 1 for lt in range(4)] for c in range(NCP)]
        vsqq = [[V1 + 8 * c + 4 + lt + 1 for lt in range(4)] for c in range(NCP)]
        TA_end = T1 + 8 * NCP
        VA_end = V1 + 8 * NCP
        vqq = VA_end + 1
        vm25 = VA_end + 2
        tmq = [TA_end + lc + 1 for lc in range(NLC)]
        vmq2 = [VA_end + 2 + lc + 1 for lc in range(NLC)]
        sscl = S1 + 1
        TB_end = TA_end + NLC
        VB_end = VA_end + 2 + NLC
        tlq = [[TB_end + 4 * c + lt + 1 for lt in range(4)] for c in range(NCP)]
        sexp = [[S1 + 1 + 4 * c + lt + 1 for lt in range(4)] for c in range(NCP)]
        TC_end = TB_end + 4 * NCP
        tk = [[0] * 4 for _ in range(NCP)]
        tkk = [[0] * 4 for _ in range(NCP)]
        _p = TC_end
        for c in range(NCP):
            for lt in range(4):
                tk[c][lt] = _p + 1; _p += 1
            if c >= 1:
                for lt in range(4):
                    tkk[c - 1][lt] = _p + 1; _p += 1
        for lt in range(4):
            tkk[NCP - 1][lt] = _p + 1; _p += 1
        vke = [[VB_end + 8 * c + lt + 1 for lt in range(4)] for c in range(NCP)]
        vsqk = [[VB_end + 8 * c + 4 + lt + 1 for lt in range(4)] for c in range(NCP)]

        if (blk := mblk) or True:
            @blk.tensor
            def _(te):
                w(te, "dW", wqk_dL)
                w(te, "dL", late_dL)
                for cp in range(NCP):
                    for lt in range(4):
                        if cp > 0:
                            w(te, "V", vqe[cp - 1][lt])
                        for dc in range(NDC):
                            ins = te.matmul(
                                psQ[:, lt * 512:(lt + 1) * 512],
                                wq[:, dc * D + cp * 128:dc * D + (cp + 1) * 128],
                                xfm[:, dc * L + lt * 512:dc * L + (lt + 1) * 512],
                                start=(dc == 0), stop=(dc == NDC - 1))
                        assert inc(ins, "T", 1) == tq[cp][lt]
                    for pcp in ([cp - 1] if cp >= 1 else []):
                        for lt in range(4):
                            w(te, "V", vsqq[pcp][lt])
                            for j in range(4):
                                lc = lt * 4 + j
                                ins = te.matmul(
                                    psKT[:, 256 + lc * 16 + 2 * pcp:256 + lc * 16 + 2 * pcp + 2],
                                    sq_sb[:, (pcp % 2) * 0 + lc * 128:(lc + 1) * 128],
                                    kkmask[:, pcp * 16 + 2 * pcp:pcp * 16 + 2 * pcp + 2],
                                    start=True, stop=True)
                            assert inc(ins, "T", 1) == tqq[pcp][lt]
                for lt in range(4):
                    w(te, "V", vsqq[NCP - 1][lt])
                    for j in range(4):
                        lc = lt * 4 + j
                        ins = te.matmul(
                            psKT[:, 256 + lc * 16 + 2 * (NCP - 1):256 + lc * 16 + 2 * (NCP - 1) + 2],
                            sq_sb[:, lc * 128:(lc + 1) * 128],
                            kkmask[:, (NCP - 1) * 16 + 2 * (NCP - 1):(NCP - 1) * 16 + 2 * (NCP - 1) + 2],
                            start=True, stop=True)
                    assert inc(ins, "T", 1) == tqq[NCP - 1][lt]
                w(te, "V", vm25)
                for lc in range(NLC):
                    if lc >= 1:
                        w(te, "V", vmq2[lc - 1])
                    ins = te.transpose(
                        psS[0:16, (lc % 2) * 128:(lc % 2) * 128 + 128],
                        qqT_sb[:, lc * 16:(lc + 1) * 16], identf[:])
                    assert inc(ins, "T", 1) == tmq[lc]
                for cp in range(NCP):
                    if cp == 0:
                        w(te, "V", vmq2[NLC - 1])
                    for lt in range(4):
                        if cp == 0:
                            w(te, "V", vqe[NCP - 1][lt])
                        else:
                            w(te, "S", sexp[cp - 1][lt])
                        te.matmul(psQ[:, lt * 512:(lt + 1) * 512],
                                  w0blk[:, cp * 128:(cp + 1) * 128],
                                  k_sb[:, cp * L + lt * 512:cp * L + (lt + 1) * 512],
                                  start=True, stop=False)
                        ins = te.matmul(psQ[:, lt * 512:(lt + 1) * 512],
                                        dsel64[:, cp * 128:(cp + 1) * 128],
                                        mqfm[:, lt * 512:(lt + 1) * 512],
                                        start=False, stop=True)
                        assert inc(ins, "T", 1) == tlq[cp][lt]
                def _kkgrp(pcp, lt):
                    w(te, "V", vsqk[pcp][lt])
                    for j in range(4):
                        lc = lt * 4 + j
                        ins = te.matmul(
                            psKT[:, lc * 16 + 2 * pcp:lc * 16 + 2 * pcp + 2],
                            sq_sb[:, lc * 128:(lc + 1) * 128],
                            kkmask[:, pcp * 16 + 2 * pcp:pcp * 16 + 2 * pcp + 2],
                            start=True, stop=True)
                    assert inc(ins, "T", 1) == tkk[pcp][lt]
                for cp in range(NCP):
                    for lt in range(4):
                        bank = lt % 2
                        if cp == 0 and lt < 2:
                            w(te, "V", v_V)
                        else:
                            pc, plt = (cp, lt - 2) if lt >= 2 else (cp - 1, lt + 2)
                            w(te, "V", vke[pc][plt])
                        for dc in range(NDC):
                            ins = te.matmul(
                                psK[:, bank * 512:(bank + 1) * 512],
                                wk[:, dc * D + cp * 128:dc * D + (cp + 1) * 128],
                                xfm[:, dc * L + lt * 512:dc * L + (lt + 1) * 512],
                                start=(dc == 0), stop=(dc == NDC - 1))
                        assert inc(ins, "T", 1) == tk[cp][lt]
                    if cp >= 1:
                        for lt in range(4):
                            _kkgrp(cp - 1, lt)
                for lt in range(4):
                    _kkgrp(NCP - 1, lt)

            @blk.vector
            def _(ve):
                w(ve, "dH", consts_dH)
                ve.memset(mqfm[:], 0.0)
                for cp in range(NCP):
                    for lt in range(4):
                        w(ve, "T", tq[cp][lt])
                        ins = ve.tensor_scalar(
                            k_sb[:, cp * L + lt * 512:cp * L + (lt + 1) * 512],
                            psQ[:, lt * 512:(lt + 1) * 512],
                            bqk[:, cp * 2:cp * 2 + 1], None, ALU.add)
                        assert inc(ins, "V", 1) == vqe[cp][lt]
                    for lt in range(4):
                        w(ve, "V", vqe[cp][lt])
                        if cp > 0:
                            w(ve, "T", tqq[cp - 1][lt])
                        ins = ve.tensor_tensor(
                            sq_sb[:, lt * 512:(lt + 1) * 512],
                            k_sb[:, cp * L + lt * 512:cp * L + (lt + 1) * 512],
                            k_sb[:, cp * L + lt * 512:cp * L + (lt + 1) * 512],
                            ALU.mult)
                        assert inc(ins, "V", 1) == vsqq[cp][lt]
                w(ve, "T", tqq[NCP - 1][3])
                assert inc(ve.tensor_copy(qqT_sb[:], psKT[:, 256:512]), "V", 1) == vqq
                w(ve, "S", sscl)
                ins = ve.tensor_scalar(qqT_sb[:], qqT_sb[:], -1.0, -25.0, ALU.mult, ALU.add)
                assert inc(ins, "V", 1) == vm25
                for lc in range(NLC):
                    w(ve, "T", tmq[lc])
                    ins = ve.tensor_copy(
                        mqfm[0:16, lc * 128:(lc + 1) * 128],
                        psS[0:16, (lc % 2) * 128:(lc % 2) * 128 + 128])
                    assert inc(ins, "V", 1) == vmq2[lc]
                for cp in range(NCP):
                    for lt in range(4):
                        bank = lt % 2
                        w(ve, "T", tk[cp][lt])
                        if cp == 0 and lt == 0:
                            w(ve, "T", tlq[NCP - 1][3])
                        ins = ve.tensor_scalar(
                            k_sb[:, cp * L + lt * 512:cp * L + (lt + 1) * 512],
                            psK[:, bank * 512:(bank + 1) * 512],
                            bqk[:, cp * 2 + 1:cp * 2 + 2], None, ALU.add)
                        assert inc(ins, "V", 1) == vke[cp][lt]
                    for lt in range(4):
                        w(ve, "V", vke[cp][lt])
                        if cp > 0:
                            w(ve, "T", tkk[cp - 1][lt])
                        elif lt == 0:
                            w(ve, "T", tqq[NCP - 1][3])
                        ins = ve.tensor_tensor(
                            sq_sb[:, lt * 512:(lt + 1) * 512],
                            k_sb[:, cp * L + lt * 512:cp * L + (lt + 1) * 512],
                            k_sb[:, cp * L + lt * 512:cp * L + (lt + 1) * 512],
                            ALU.mult)
                        assert inc(ins, "V", 1) == vsqk[cp][lt]

            @blk.scalar
            def _(se):
                w(se, "V", vqq)
                ins = se.activation(qqT_sb[:], qqT_sb[:], AF.Sqrt,
                                    bias=0.0, scale=-2.0 * 2.35 * 2.35)
                assert inc(ins, "S", 1) == sscl
                for cp in range(NCP):
                    for lt in range(4):
                        w(se, "T", tlq[cp][lt])
                        ins = se.activation(
                            eq_sb[:, cp * L + lt * 512:cp * L + (lt + 1) * 512],
                            psQ[:, lt * 512:(lt + 1) * 512],
                            AF.Exp, bias=0.0, scale=1.0)
                        assert inc(ins, "S", 1) == sexp[cp][lt]
        tap("eq", eq_sb[:], [128, NCP * L], "S")
        tap("ksb", k_sb[:], [128, NCP * L], "V")
        if stage == "p1b":
            return finish_ret()
        s_qs.close(); open_stacks.remove(s_qs)

        s_big.close(); open_stacks.remove(s_big)

        s_ee = ExitStack()
        open_stacks.append(s_ee)
        comb2 = s_ee.enter_context(nc.sbuf_tensor("comb2", [64, L], BF16))
        comblo = s_ee.enter_context(nc.sbuf_tensor("comblo", [16, L], BF16))
        e_sb = s_ee.enter_context(nc.sbuf_tensor("e_sb", [128, 2 * L], BF16))
        eT_sb = s_ee.enter_context(nc.sbuf_tensor("eT_sb", [128, 2 * L], BF16))
        psEb = None

        T2p, V2p = cnt["T"], cnt["V"]
        vkkt = V2p + 1
        kktr = [T2p + lc + 1 for lc in range(NLC)]
        vkkfm = [V2p + 1 + 2 * lc + 2 for lc in range(NLC)]
        vclo = V2p + 1 + 2 * NLC
        assert vclo == vkkfm[-1]
        if (blk := mblk) or True:
            @blk.vector
            def _(ve):
                w(ve, "T", tkk[NCP - 1][3])
                w(ve, "T", tmq[NLC - 1])
                ve.memset(comb2[:], 0.0)
                assert inc(ve.tensor_copy(kkT_sb[:], psKT[:, 0:256]), "V", 1) == vkkt

            @blk.tensor
            def _(te):
                w(te, "V", vkkt)
                w(te, "S", sexp[NCP - 1][3])
                for lc in range(NLC):
                    if lc >= 1:
                        w(te, "V", vkkfm[lc - 1])
                    ins = te.transpose(
                        psS[0:16, (lc % 2) * 128:(lc % 2) * 128 + 128],
                        kkT_sb[:, lc * 16:(lc + 1) * 16], identf[:])
                    assert inc(ins, "T", 1) == kktr[lc]

            @blk.vector
            def _(ve):
                for lc in range(NLC):
                    w(ve, "T", kktr[lc])
                    ins = ve.tensor_copy(
                        comb2[0:16, lc * 128:(lc + 1) * 128],
                        psS[0:16, (lc % 2) * 128:(lc % 2) * 128 + 128])
                    inc(ins, "V", 1)
                    w(ve, "V", cnt["V"])
                    ins = ve.scalar_tensor_tensor(
                        comblo[:, lc * 128:(lc + 1) * 128],
                        comb2[0:16, lc * 128:(lc + 1) * 128], -1.0,
                        psS[0:16, (lc % 2) * 128:(lc % 2) * 128 + 128],
                        ALU.mult, ALU.add)
                    assert inc(ins, "V", 1) == vkkfm[lc]
                assert vclo == cnt["V"] + 0 or True
        tap("kkfm", comb2[0:16, :], [16, L], "V", None)
        if (blk := mblk) or True:
            @blk.sync
            def _(sy):
                sy.wait_ge(sem["V"], vclo)
                inc(sy.dma_start(comb2[32:48, :], comblo[:]), "dC", 16)
        s_wt = ExitStack()
        open_stacks.append(s_wt)
        wbase = s_wt.enter_context(nc.sbuf_tensor("wbase", [128, NDC * D], BF16, side="right"))
        cball = s_wt.enter_context(nc.sbuf_tensor("cball", [8, D], BF16, side="right"))
        borow = s_wt.enter_context(nc.sbuf_tensor("borow", [1, D], BF16, side="right"))
        s_wt2 = ExitStack()
        open_stacks.append(s_wt2)
        wo = s_wt2.enter_context(nc.sbuf_tensor("wo", [128, NDC * D], BF16, side="right"))
        if (blk := mblk) or True:
            @blk.gpsimd
            def _(g):
                g.wait_ge(sem["T"], tkk[NCP - 1][3])
                inc(g.dma_start(
                    wo[:].rearrange("p (c n) -> p c n", n=D),
                    d_in["woT"].rearrange("(c p) n -> p c n", p=128)), "dW", 16)
                inc(g.dma_start(
                    wbase[:].rearrange("p (c n) -> p c n", n=D),
                    d_in["baseT"].rearrange("(c p) n -> p c n", p=128)), "dW", 16)
                inc(g.dma_start(cball[:], d_in["cball"]), "dW", 16)
                inc(g.dma_start(borow[:], d_in["borow"]), "dW", 16)
        wo_dL = cnt["dW"] - 48
        wbase_dL = cnt["dW"]
        s_kt.close(); open_stacks.remove(s_kt)
        s_aps2 = ExitStack()
        open_stacks.append(s_aps2)
        psEb = s_aps2.enter_context(nc.psum_tensor("psEb", [128, 512], BF16))

        if (blk := mblk) or True:
            @blk.vector
            def _(ve):
                w(ve, "V", cnt["V"])
                for c in range(NCP):
                    ve.memset(numS[c][:], 0.0)
                    ve.memset(denS[c][:], 0.0)
                inc(ve.nop(), "V", 1)
        memset_V = cnt["V"]

        _pt, _pv, _ps = cnt["T"], cnt["V"], cnt["S"]
        tLh = [[0, 0] for _ in range(NCP)]
        ttr2 = [0] * NCP
        tbias2 = [0] * NCP
        tetr = [[0] * 4 for _ in range(NCP)]
        tN = [0] * NCP
        vcm2 = [0] * NCP
        vscr2 = [0] * NCP
        vmk = [0] * NCP
        vmb2 = [0] * NCP
        vet = [[0] * 4 for _ in range(NCP)]
        vnum = [0] * NCP
        sexp2h = [[0, 0] for _ in range(NCP)]
        for i in range(NCP):
            tLh[i][0] = _pt + 1; tLh[i][1] = _pt + 2
            _pt += 2
            if i >= 1:
                for g in range(4):
                    tetr[i - 1][g] = _pt + 1 + g
                tN[i - 1] = _pt + 5
                _pt += 5
            ttr2[i] = _pt + 1; tbias2[i] = _pt + 2
            _pt += 2
            vcm2[i] = _pv + 1
            _pv += 1
            if i >= 1:
                for g in range(4):
                    vet[i - 1][g] = _pv + 1 + g
                vnum[i - 1] = _pv + 5
                _pv += 5
            vscr2[i] = _pv + 1; vmk[i] = _pv + 2; vmb2[i] = _pv + 3
            _pv += 3
            sexp2h[i][0] = _ps + 1; sexp2h[i][1] = _ps + 2
            _ps += 2
        for g in range(4):
            tetr[NCP - 1][g] = _pt + 1 + g
            vet[NCP - 1][g] = _pv + 1 + g
        tN[NCP - 1] = _pt + 5
        vnum[NCP - 1] = _pv + 5

        def _emit_tr_nh(te, cp):
            for g4 in range(4):
                if cp > 0 or g4 > 0:
                    pc, pg = (cp, g4 - 1) if g4 > 0 else (cp - 1, 3)
                    w(te, "V", vet[pc][pg])
                w(te, "S", sexp2h[cp][g4 // 2])
                for j in range(4):
                    lc = g4 * 4 + j
                    ins = te.transpose(
                        psEb[:, j * 128:(j + 1) * 128],
                        e_sb[:, (cp % 2) * L + lc * 128:(cp % 2) * L + (lc + 1) * 128],
                        identb[:])
                assert inc(ins, "T", 1) == tetr[cp][g4]
            if cp == 0:
                w(te, "V", memset_V)
            if cp > 0:
                w(te, "V", vnum[cp - 1])
            for lc in range(NLC):
                g4 = lc // 4
                w(te, "V", vet[cp][g4])
                ins = te.matmul(
                    psK[:, 512:642],
                    eT_sb[:, (cp % 2) * L + lc * 128:(cp % 2) * L + (lc + 1) * 128],
                    v_sb[:, lc * 1040 + cp * 130:lc * 1040 + cp * 130 + 130],
                    start=(lc == 0), stop=(lc == NLC - 1))
            assert inc(ins, "T", 1) == tN[cp]

        if (blk := mblk) or True:
            @blk.tensor
            def _(te):
                for cp in range(NCP):
                    w(te, "V", vclo)
                    if cp == 0:
                        w(te, "dC", 16)
                    for h in range(2):
                        if cp > 0:
                            w(te, "S", sexp2h[cp - 1][h])
                        for lt in (2 * h, 2 * h + 1):
                            te.matmul(psQ[:, lt * 512:(lt + 1) * 512],
                                      w0blk[:, cp * 128:(cp + 1) * 128],
                                      k_sb[:, cp * L + lt * 512:cp * L + (lt + 1) * 512],
                                      start=True, stop=False)
                            ins = te.matmul(psQ[:, lt * 512:(lt + 1) * 512],
                                            dsel64[:, cp * 128:(cp + 1) * 128],
                                            comb2[:, lt * 512:(lt + 1) * 512],
                                            start=False, stop=True)
                        assert inc(ins, "T", 1) == tLh[cp][h]
                    if cp >= 1:
                        _emit_tr_nh(te, cp - 1)
                    w(te, "V", vcm2[cp])
                    if cp > 0:
                        w(te, "V", vmb2[cp - 1])
                    ins = te.transpose(psS[0:1, 0:128], colmax[:], identf[:])
                    assert inc(ins, "T", 1) == ttr2[cp]
                    w(te, "V", vmk[cp])
                    te.matmul(psS[:, 128:129], parab[:, 0:128],
                              mkA[:, cp:cp + 1], start=True, stop=False)
                    ins = te.matmul(psS[:, 128:129], parab[:, 128:256],
                                    mkB[:, cp:cp + 1], start=False, stop=True)
                    assert inc(ins, "T", 1) == tbias2[cp]
                _emit_tr_nh(te, NCP - 1)

            def _emit_evacs(ve, pc):
                for g4 in range(4):
                    w(ve, "T", tetr[pc][g4])
                    if pc >= 2:
                        w(ve, "T", tN[pc - 2])
                    ins = ve.tensor_copy(
                        eT_sb[:, (pc % 2) * L + g4 * 512:(pc % 2) * L + (g4 + 1) * 512],
                        psEb[:, 0:512])
                    assert inc(ins, "V", 1) == vet[pc][g4]
                w(ve, "T", tN[pc])
                ve.tensor_copy(numS[pc][0:64, 0:64], psK[0:64, 512:576])
                ve.tensor_copy(numS[pc][64:128, 64:128], psK[64:128, 577:641])
                ve.tensor_copy(d16[0:64, pc:pc + 1], psK[0:64, 576:577])
                ins = ve.tensor_copy(d16[64:128, pc:pc + 1], psK[64:128, 641:642])
                assert inc(ins, "V", 1) == vnum[pc]

            @blk.vector
            def _(ve):
                for cp in range(NCP):
                    w(ve, "T", tLh[cp][0])
                    ins = ve.reduce_max(colmax[:], psQ[:, 0:1024:4], axis=AX.X)
                    assert inc(ins, "V", 1) == vcm2[cp]
                    if cp >= 1:
                        _emit_evacs(ve, cp - 1)
                    w(ve, "T", ttr2[cp])
                    ins = ve.tensor_copy(scr3[:], psS[0:1, 0:128])
                    assert inc(ins, "V", 1) == vscr2[cp]
                    w(ve, "V", vscr2[cp])
                    ve.reduce_max(mkA[:, cp:cp + 1], scr3[:, 0:64],
                                  axis=AX.X, negate=True)
                    ins = ve.reduce_max(mkB[:, cp:cp + 1], scr3[:, 64:128],
                                        axis=AX.X, negate=True)
                    assert inc(ins, "V", 1) == vmk[cp]
                    w(ve, "T", tbias2[cp])
                    if cp > 0:
                        w(ve, "S", sexp2h[cp - 1][1])
                    ins = ve.tensor_copy(mbias[:], psS[:, 128:129])
                    assert inc(ins, "V", 1) == vmb2[cp]
                _emit_evacs(ve, NCP - 1)

            @blk.scalar
            def _(se):
                for cp in range(NCP):
                    for h in range(2):
                        w(se, "T", tLh[cp][h])
                        w(se, "V", vmb2[cp])
                        if cp >= 2:
                            w(se, "T", tetr[cp - 2][3])
                        ins = se.activation(
                            e_sb[:, (cp % 2) * L + h * 1024:(cp % 2) * L + (h + 1) * 1024],
                            psQ[:, h * 1024:(h + 1) * 1024],
                            AF.Exp, bias=mbias[:], scale=1.0)
                        assert inc(ins, "S", 1) == sexp2h[cp][h]
        assert cnt["T"] == _pt + 5 and cnt["V"] == _pv + 5 and cnt["S"] == _ps
        sexp2 = [sexp2h[c][1] for c in range(NCP)]
        tap("d16", d16[:], [128, 8], "V")
        tap("e", e_sb[:], [128, 2 * L], "S")
        if stage == "p2":
            return finish_ret()
        s_ee.close(); open_stacks.remove(s_ee)

        TZ, VZ = cnt["T"], cnt["V"]
        tzd = TZ + 1
        tzab = TZ + 2
        tze8 = TZ + 3
        vzd8 = VZ + 1
        vzcols = VZ + 2
        vzrows = VZ + 3
        vzeb8 = VZ + 4
        vzeall = VZ + 5

        if (blk := mblk) or True:
            @blk.tensor
            def _(te):
                w(te, "V", vnum[NCP - 1])
                w(te, "S", sexp2[NCP - 1])
                ins = te.transpose(psS[0:8, 0:128], d16[:], identf[:])
                assert inc(ins, "T", 1) == tzd
                w(te, "V", vzcols)
                te.transpose(psS[0:1, 128:136], dcolA[:], identf[0:8, 0:8])
                ins = te.transpose(psS[0:1, 136:144], dcolB[:], identf[0:8, 0:8])
                assert inc(ins, "T", 1) == tzab
                w(te, "V", vzrows)
                te.matmul(psS[:, 144:152], parab[:, 0:128],
                          zArow[:, 0:8], start=True, stop=False)
                ins = te.matmul(psS[:, 144:152], parab[:, 128:256],
                                zBrow[:, 0:8], start=False, stop=True)
                assert inc(ins, "T", 1) == tze8

            @blk.vector
            def _(ve):
                w(ve, "T", tzd)
                ins = ve.tensor_copy(zd8[:], psS[0:8, 0:128])
                assert inc(ins, "V", 1) == vzd8
                w(ve, "V", vzd8)
                ve.reduce_sum(dcolA[:], zd8[:, 0:64], axis=AX.X)
                ins = ve.reduce_sum(dcolB[:], zd8[:, 64:128], axis=AX.X)
                assert inc(ins, "V", 1) == vzcols
                w(ve, "T", tzab)
                ve.tensor_scalar(zArow[:], psS[0:1, 128:136], EPS_ATT, None, ALU.mult)
                ins = ve.tensor_scalar(zBrow[:], psS[0:1, 136:144], EPS_ATT, None, ALU.mult)
                assert inc(ins, "V", 1) == vzrows
                w(ve, "T", tze8)
                ins = ve.tensor_copy(ebias[:], psS[:, 144:152])
                assert inc(ins, "V", 1) == vzeb8
                w(ve, "V", vzeb8)
                for cp in range(NCP):
                    ve.tensor_tensor(denS[cp][0:64, 2 * cp:2 * cp + 1],
                                     d16[0:64, cp:cp + 1], ebias[0:64, cp:cp + 1],
                                     ALU.add)
                    ins = ve.tensor_tensor(denS[cp][64:128, 2 * cp + 1:2 * cp + 2],
                                           d16[64:128, cp:cp + 1],
                                           ebias[64:128, cp:cp + 1], ALU.add)
                assert inc(ins, "V", 1) == vzeall
        denS_V = cnt["V"]

        s_att1.close(); open_stacks.remove(s_att1)
        s_rp = ExitStack()
        open_stacks.append(s_rp)
        recipf = s_rp.enter_context(nc.sbuf_tensor("recipf", [16, L], F32))
        recip2 = s_rp.enter_context(nc.sbuf_tensor("recip2", [64, L], BF16))
        reciplo16 = s_rp.enter_context(nc.sbuf_tensor("reciplo16", [16, L], BF16))

        TB, VB = cnt["T"], cnt["V"]
        tden = [TB + lt + 1 for lt in range(4)]
        vrecf = [VB + 3 * lt + 1 for lt in range(4)]
        vrech = [VB + 3 * lt + 2 for lt in range(4)]
        vrlo = [VB + 3 * lt + 3 for lt in range(4)]
        if (blk := mblk) or True:
            @blk.tensor
            def _(te):
                w(te, "V", denS_V)
                for lt in range(4):
                    for cp in range(NCP):
                        ins = te.matmul(
                            psQ[0:16, lt * 512:(lt + 1) * 512],
                            denS[cp][:],
                            eq_sb[:, cp * L + lt * 512:cp * L + (lt + 1) * 512],
                            start=(cp == 0), stop=(cp == NCP - 1))
                    assert inc(ins, "T", 1) == tden[lt]

            @blk.vector
            def _(ve):
                ve.memset(recip2[:], 0.0)
                for lt in range(4):
                    sl = slice(lt * 512, (lt + 1) * 512)
                    w(ve, "T", tden[lt])
                    ins = ve.reciprocal(recipf[:, sl], psQ[0:16, sl])
                    assert inc(ins, "V", 1) == vrecf[lt]
                    w(ve, "V", vrecf[lt])
                    ins = ve.tensor_copy(recip2[0:16, sl], recipf[:, sl])
                    assert inc(ins, "V", 1) == vrech[lt]
                    w(ve, "V", vrech[lt])
                    ins = ve.scalar_tensor_tensor(
                        reciplo16[:, sl], recip2[0:16, sl], -1.0, recipf[:, sl],
                        ALU.mult, ALU.add)
                    assert inc(ins, "V", 1) == vrlo[lt]

            @blk.sync
            def _(sy):
                for lt in range(4):
                    sl = slice(lt * 512, (lt + 1) * 512)
                    sy.wait_ge(sem["V"], vrlo[lt])
                    inc(sy.dma_start(recip2[32:48, sl], reciplo16[:, sl]),
                        ("dRa", "dRb", "dRc", "dRd")[lt], 16)
        tap("recipf", recipf[:], [16, L], "V")
        tap("densall", denS[0][:], [128, 16], "V")
        if stage == "pz":
            return finish_ret()
        s_aps2.close(); open_stacks.remove(s_aps2)
        s_apq.close(); open_stacks.remove(s_apq)
        s_aps.close(); open_stacks.remove(s_aps)

        s_ctx = ExitStack()
        open_stacks.append(s_ctx)
        ctx_sb = s_ctx.enter_context(
            nc.sbuf_tensor("ctx_sb", [128, NCP * L], BF16, side="right"))
        s_b2 = ExitStack()
        open_stacks.append(s_b2)
        psBC = s_b2.enter_context(nc.psum_tensor("psBC", [128, L], F32))
        psNum = s_b2.enter_context(nc.psum_tensor("psNum", [128, L], F32))
        phi_sb = s_b2.enter_context(nc.sbuf_tensor("phi_sb", [128, L], BF16))
        TB2, VB2, SB2 = cnt["T"], cnt["V"], cnt["S"]
        tbc = [[TB2 + 4 * c + h + 1 for h in range(2)] for c in range(NCP)]
        tnum = [[TB2 + 4 * c + 2 + h + 1 for h in range(2)] for c in range(NCP)]
        vphi = [[VB2 + 2 * c + h + 1 for h in range(2)] for c in range(NCP)]
        sctx = [[SB2 + 2 * c + h + 1 for h in range(2)] for c in range(NCP)]
        if (blk := mblk) or True:
            @blk.tensor
            def _(te):
                for cp in range(NCP):
                    for h in range(2):
                        if cp == 0:
                            w(te, ("dRa", "dRc")[h], 16)
                            w(te, ("dRb", "dRd")[h], 16)
                            if h == 0:
                                w(te, "dC", 16)
                        if cp >= 1:
                            w(te, "V", vphi[cp - 1][h])
                        for lt in (2 * h, 2 * h + 1):
                            ins = te.matmul(psBC[:, lt * 512:(lt + 1) * 512],
                                            dsel64[:, cp * 128:(cp + 1) * 128],
                                            recip2[:, lt * 512:(lt + 1) * 512],
                                            start=True, stop=True)
                        assert inc(ins, "T", 1) == tbc[cp][h]
                    for h in range(2):
                        w(te, "V", vphi[cp][h])
                        if cp >= 1:
                            w(te, "S", sctx[cp - 1][h])
                        for lt in (2 * h, 2 * h + 1):
                            ins = te.matmul(psNum[:, lt * 512:(lt + 1) * 512],
                                            numS[cp][:],
                                            phi_sb[:, lt * 512:(lt + 1) * 512],
                                            start=True, stop=True)
                        assert inc(ins, "T", 1) == tnum[cp][h]

            @blk.vector
            def _(ve):
                for cp in range(NCP):
                    for h in range(2):
                        w(ve, "T", tbc[cp][h])
                        if cp >= 1:
                            w(ve, "T", tnum[cp - 1][h])
                        for lt in (2 * h, 2 * h + 1):
                            ins = ve.tensor_tensor(
                                phi_sb[:, lt * 512:(lt + 1) * 512],
                                psBC[:, lt * 512:(lt + 1) * 512],
                                eq_sb[:, cp * L + lt * 512:cp * L + (lt + 1) * 512],
                                ALU.mult)
                        assert inc(ins, "V", 1) == vphi[cp][h]

            @blk.scalar
            def _(se):
                for cp in range(NCP):
                    for h in range(2):
                        w(se, "T", tnum[cp][h])
                        for lt in (2 * h, 2 * h + 1):
                            ins = se.activation(
                                ctx_sb[:, cp * L + lt * 512:cp * L + (lt + 1) * 512],
                                psNum[:, lt * 512:(lt + 1) * 512],
                                AF.Copy, bias=0.0, scale=1.0)
                        assert inc(ins, "S", 1) == sctx[cp][h]
        ctx_S = cnt["S"]
        assert ctx_S == sctx[NCP - 1][1]
        tap("ctx", ctx_sb[:], [128, NCP * L], "S")
        if stage == "pb2":
            return finish_ret()
        s_b2.close(); open_stacks.remove(s_b2)
        s_rp.close(); open_stacks.remove(s_rp)
        s_att.close(); open_stacks.remove(s_att)

        s1bf = SB("s1bf", [128, NLC * D], BF16)
        s1fm = SB("s1fm", [128, NDC * L], BF16)
        stat = SB("stat", [128, 24], F32)
        scrap = SB("scrap", [128, L], BF16)
        s2t = SB("s2t", [128, 2 * D], F32)
        m1q = SB("m1q", [128, 32], F32)
        m2q = SB("m2q", [128, 32], F32)
        m2scr = SB("m2scr", [128, 1024], BF16)
        s_wo = ExitStack()
        open_stacks.append(s_wo)
        xr = s_wo.enter_context(nc.sbuf_tensor("xr", [128, 2 * D], F32))
        u1buf = s_wo.enter_context(nc.sbuf_tensor("u1buf", [128, 4 * D], F32))
        psO = s_wo.enter_context(nc.psum_tensor("psO", [128, 2 * D], F32))
        psT = s_wo.enter_context(nc.psum_tensor("psT", [128, 2048], BF16))

        TW, VW, SW, PW = cnt["T"], cnt["V"], cnt["S"], cnt["P"]
        def _msched():
            sched = [[] for _ in range(NLC)]
            pend = []
            for lc in range(NLC):
                if lc >= 6 and (lc - 6) % 4 == 0 and (lc - 6) // 4 < 4:
                    pend.extend(((lc - 6) // 4, dc) for dc in range(NDC))
                for _ in range(3):
                    if pend:
                        sched[lc].append(pend.pop(0))
            pend.extend((3, dc) for dc in range(NDC))
            return sched, pend
        m1sched, m1tail = _msched()
        m2sched, m2tail = _msched()
        te_ops = []
        for lc in range(NLC):
            te_ops.append(("mm", lc))
            if lc >= 3:
                te_ops.append(("tr", lc - 3))
        te_ops += [("tr", 13), ("tr", 14), ("tr", 15)]
        ve_ops = []
        for lc in range(NLC):
            ve_ops.append(("vu", lc))
            if lc >= 1:
                ve_ops += [("vms", lc - 1), ("vm2", lc - 1), ("vst", lc - 1)]
            if lc >= 2:
                ve_ops += [("vrc", lc - 2), ("vnm", lc - 2)]
            if lc >= 3:
                ve_ops.append(("vfev", lc - 3))
            ve_ops += [("m1", q, dc) for q, dc in m1sched[lc]]
        ve_ops += [("vms", 15), ("vm2", 15), ("vst", 15),
                   ("vrc", 14), ("vnm", 14), ("vfev", 13),
                   ("vrc", 15), ("vnm", 15), ("vfev", 14), ("vfev", 15)]
        ve_ops += [("m1", q, dc) for q, dc in m1tail]
        se_ops = []
        for lc in range(NLC):
            se_ops.append(("ssq", lc))
            if lc >= 1:
                se_ops.append(("ssd", lc - 1))
            if lc >= 2:
                se_ops.append(("sapply", lc - 2))
            se_ops += [("m2", q, dc) for q, dc in m2sched[lc]]
        se_ops += [("ssd", 15), ("sapply", 14), ("sapply", 15)]
        se_ops += [("m2", q, dc) for q, dc in m2tail]
        tmk = {op: TW + i + 1 for i, op in enumerate(te_ops)}
        vmk_ = {op: VW + i + 1 for i, op in enumerate(ve_ops)}
        smk = {op: SW + i + 1 for i, op in enumerate(se_ops)}
        two = [tmk[("mm", lc)] for lc in range(NLC)]
        tftr = [tmk[("tr", lc)] for lc in range(NLC)]
        vu = [vmk_[("vu", lc)] for lc in range(NLC)]
        vms = [vmk_[("vms", lc)] for lc in range(NLC)]
        vm2 = [vmk_[("vm2", lc)] for lc in range(NLC)]
        vst = [vmk_[("vst", lc)] for lc in range(NLC)]
        vrc = [vmk_[("vrc", lc)] for lc in range(NLC)]
        vnm = [vmk_[("vnm", lc)] for lc in range(NLC)]
        vfev = [vmk_[("vfev", lc)] for lc in range(NLC)]
        vm1_all = VW + len(ve_ops)
        ssq = [smk[("ssq", lc)] for lc in range(NLC)]
        ssd = [smk[("ssd", lc)] for lc in range(NLC)]
        sapply = [smk[("sapply", lc)] for lc in range(NLC)]
        sm2_all = SW + len(se_ops)
        xr_dL = [16 * (lc // 2 + 1) for lc in range(NLC)]
        s1fm3 = s1fm[:].rearrange("p (c l) -> p c l", l=L)
        if (blk := mblk) or True:
            @blk.sync
            def _(sy):
                sy.wait_ge(sem["S"], ctx_S)
                for lc in range(NLC):
                    if lc >= 2:
                        sy.wait_ge(sem["V"], vu[lc - 2])
                    inc(sy.dma_start(xr[:, (lc % 2) * D:(lc % 2 + 1) * D], x_t[lc]),
                        "dXa" if lc % 2 == 0 else "dXb", 16)

            @blk.tensor
            def _(te):
                w(te, "dW", wbase_dL)
                w(te, "S", ctx_S)
                for op in te_ops:
                    kind, lc = op
                    if kind == "mm":
                        if lc >= 2:
                            w(te, "V", vu[lc - 2])
                        for ot in range(2):
                            for c8 in range(NCP):
                                te.matmul(
                                    psO[:, (lc % 2) * D + ot * 512:(lc % 2) * D + (ot + 1) * 512],
                                    ctx_sb[:, c8 * L + lc * 128:c8 * L + (lc + 1) * 128],
                                    wo[:, c8 * D + ot * 512:c8 * D + (ot + 1) * 512],
                                    start=(c8 == 0), stop=False)
                            ins = te.matmul(
                                psO[:, (lc % 2) * D + ot * 512:(lc % 2) * D + (ot + 1) * 512],
                                onesrow[:], borow[:, ot * 512:(ot + 1) * 512],
                                start=False, stop=True)
                        assert inc(ins, "T", 1) == two[lc]
                    else:
                        w(te, "S", sapply[lc])
                        if lc >= 2:
                            w(te, "V", vfev[lc - 2])
                        for dc in range(NDC):
                            ins = te.transpose(
                                psT[:, (lc % 2) * 1024 + dc * 128:(lc % 2) * 1024 + (dc + 1) * 128],
                                s1bf[:, lc * D + dc * 128:lc * D + (dc + 1) * 128],
                                identb[:])
                        assert inc(ins, "T", 1) == tftr[lc]

            @blk.vector
            def _(ve):
                for op in ve_ops:
                    kind, a = op[0], op[1]
                    p2 = a % 2
                    if kind == "vu":
                        w(ve, "T", two[a])
                        w(ve, "dXa" if a % 2 == 0 else "dXb", xr_dL[a])
                        if a >= 2:
                            w(ve, "V", vms[a - 2])
                        if a >= 4:
                            w(ve, "S", sapply[a - 4])
                        ins = ve.scalar_tensor_tensor(
                            u1buf[:, (a % 4) * D:(a % 4 + 1) * D],
                            psO[:, p2 * D:(p2 + 1) * D], 1.0,
                            xr[:, p2 * D:(p2 + 1) * D],
                            ALU.mult, ALU.add, accum_out=stat[:, p2:p2 + 1])
                        assert inc(ins, "V", 1) == vu[a]
                    elif kind == "vms":
                        w(ve, "S", ssq[a])
                        if a >= 2:
                            w(ve, "V", vnm[a - 2])
                        ve.tensor_scalar(stat[:, 4 + p2:5 + p2], stat[:, p2:p2 + 1],
                                         1.0 / D, None, ALU.mult)
                        ins = ve.tensor_scalar(stat[:, 6 + p2:7 + p2],
                                               stat[:, 2 + p2:3 + p2],
                                               1.0 / D, None, ALU.mult)
                        assert inc(ins, "V", 1) == vms[a]
                    elif kind == "vm2":
                        w(ve, "V", vms[a])
                        ins = ve.tensor_tensor(stat[:, 8 + p2:9 + p2],
                                               stat[:, 4 + p2:5 + p2],
                                               stat[:, 4 + p2:5 + p2], ALU.mult)
                        assert inc(ins, "V", 1) == vm2[a]
                    elif kind == "vst":
                        w(ve, "V", vm2[a])
                        ins = ve.scalar_tensor_tensor(stat[:, 10 + p2:11 + p2],
                                                      stat[:, 6 + p2:7 + p2], EPS_LN,
                                                      stat[:, 8 + p2:9 + p2],
                                                      ALU.add, ALU.subtract)
                        assert inc(ins, "V", 1) == vst[a]
                    elif kind == "vrc":
                        w(ve, "S", ssd[a])
                        if a >= 4:
                            w(ve, "S", sapply[a - 4])
                        ins = ve.reciprocal(stat[:, 14 + a % 4:15 + a % 4],
                                            stat[:, 12 + p2:13 + p2])
                        assert inc(ins, "V", 1) == vrc[a]
                    elif kind == "vnm":
                        w(ve, "V", vrc[a])
                        ins = ve.scalar_tensor_tensor(
                            stat[:, 18 + a % 4:19 + a % 4],
                            stat[:, 4 + p2:5 + p2], -1.0,
                            stat[:, 14 + a % 4:15 + a % 4], ALU.mult, ALU.mult)
                        assert inc(ins, "V", 1) == vnm[a]
                    elif kind == "vfev":
                        w(ve, "T", tftr[a])
                        ins = ve.tensor_copy(
                            s1fm3[:, :, a * 128:(a + 1) * 128],
                            psT[:, p2 * 1024:p2 * 1024 + 1024]
                            .rearrange("p (c k) -> p c k", k=128))
                        assert inc(ins, "V", 1) == vfev[a]
                    else:
                        q, dc = op[1], op[2]
                        w(ve, "V", vfev[4 * q + 3])
                        ins = ve.reduce_sum(
                            m1q[:, q * 8 + dc:q * 8 + dc + 1],
                            s1fm[:, dc * L + q * 512:dc * L + (q + 1) * 512],
                            axis=AX.X)
                        inc(ins, "V", 1)
                assert cnt["V"] == vm1_all

            @blk.scalar
            def _(se):
                for op in se_ops:
                    kind, a = op[0], op[1]
                    p2 = a % 2
                    if kind == "ssq":
                        w(se, "V", vu[a])
                        if a >= 1:
                            w(se, "S", ssq[a - 1])
                        ins = se.activation(scrap[:, 0:D],
                                            u1buf[:, (a % 4) * D:(a % 4 + 1) * D],
                                            AF.Square,
                                            accum_out=stat[:, 2 + p2:3 + p2])
                        assert inc(ins, "S", 1) == ssq[a]
                    elif kind == "ssd":
                        w(se, "V", vst[a])
                        ins = se.activation(stat[:, 12 + p2:13 + p2],
                                            stat[:, 10 + p2:11 + p2],
                                            AF.Sqrt, bias=0.0, scale=1.0)
                        assert inc(ins, "S", 1) == ssd[a]
                    elif kind == "sapply":
                        w(se, "V", vnm[a])
                        ins = se.activation(s1bf[:, a * D:(a + 1) * D],
                                            u1buf[:, (a % 4) * D:(a % 4 + 1) * D],
                                            AF.Identity,
                                            bias=stat[:, 18 + a % 4:19 + a % 4],
                                            scale=stat[:, 14 + a % 4:15 + a % 4])
                        assert inc(ins, "S", 1) == sapply[a]
                    else:
                        q, dc = op[1], op[2]
                        w(se, "S", cnt["S"])
                        w(se, "V", vfev[4 * q + 3])
                        ins = se.activation(
                            m2scr[:, (dc % 2) * 512:(dc % 2) * 512 + 512],
                            s1fm[:, dc * L + q * 512:dc * L + (q + 1) * 512],
                            AF.Square, accum_out=m2q[:, q * 8 + dc:q * 8 + dc + 1])
                        inc(ins, "S", 1)
                assert cnt["S"] == sm2_all
        tap("s1", s1bf[:], [128, NLC * D], "S")
        tap("s1fm", s1fm[:], [128, NDC * L], "V")
        if stage in ("pwo", "ft"):
            return finish_ret()
        s_wo.close(); open_stacks.remove(s_wo)
        s_ctx.close(); open_stacks.remove(s_ctx)
        s_wt2.close(); open_stacks.remove(s_wt2)

        s_fn = ExitStack()
        open_stacks.append(s_fn)
        psH = s_fn.enter_context(nc.psum_tensor("psH", [64, 1024], F32))
        psA2 = s_fn.enter_context(nc.psum_tensor("psA2", [128, 1024], F32))
        psO2 = s_fn.enter_context(nc.psum_tensor("psO2", [128, 2 * D], F32))
        psF1 = psA2

        if (blk := mblk) or True:
            @blk.tensor
            def _(te):
                w(te, "V", vfev[NLC - 1])
                for lc in range(2):
                    for ot in range(2):
                        for dc in range(NDC):
                            ins = te.matmul(
                                psO2[:, lc * D + ot * 512:lc * D + (ot + 1) * 512],
                                s1fm[:, dc * L + lc * 128:dc * L + (lc + 1) * 128],
                                wbase[:, dc * D + ot * 512:dc * D + (ot + 1) * 512],
                                start=(dc == 0), stop=False)
                inc(ins, "T", 1)
        base_pre_T = cnt["T"]


        convsel = SB("convsel", [128, NDC * D], BF16)
        biasrow = SB("biasrow", [1, D], BF16)
        dwsel = SB("dwsel", [128, NDC * R], BF16)
        upsel = SB("upsel", [64, D], BF16)
        h_sb = SB("h_sb", [64, L], BF16)
        a_sb = SB("a_sb", [128, NDC * L], BF16)

        fl_dH = consts_dH

        if (blk := mblk) or True:
            @blk.vector
            def _(ve):
                w(ve, "S", sm2_all)
                w(ve, "V", vm1_all)
                inc(ve.tensor_tensor(m1c[:], m1q[:, 0:8], m1q[:, 8:16], ALU.add),
                    "V", 1)
                inc(ve.tensor_tensor(m2c[:], m2q[:, 0:8], m2q[:, 8:16], ALU.add),
                    "V", 1)
                w(ve, "V", cnt["V"])
                inc(ve.tensor_tensor(m1c[:], m1c[:], m1q[:, 16:24], ALU.add), "V", 1)
                inc(ve.tensor_tensor(m2c[:], m2c[:], m2q[:, 16:24], ALU.add), "V", 1)
                w(ve, "V", cnt["V"])
                inc(ve.tensor_tensor(m1c[:], m1c[:], m1q[:, 24:32], ALU.add), "V", 1)
                inc(ve.tensor_tensor(m2c[:], m2c[:], m2q[:, 24:32], ALU.add), "V", 1)
        mom_V, mom_S = cnt["V"], cnt["S"]

        V3, T3 = cnt["V"], cnt["T"]
        tfl = [T3 + dc + 1 for dc in range(NDC)]
        if (blk := mblk) or True:
            @blk.vector
            def _(ve):
                w(ve, "dH", fl_dH)
                w(ve, "dL", cnt["dL"])
                inc(ve.memset(momdc8[:].rearrange("p (c k) -> p c k", k=5)[:, :, 4:5],
                              float(L)), "V", 1)
                ins = ve.tensor_tensor(sctA[:], sigT[:], sigT[:], ALU.mult)
                inc(ins, "V", 1)
                w(ve, "V", cnt["V"])
                inc(ve.reciprocal(ivA[:], sctA[:]), "V", 1)
                w(ve, "V", cnt["V"])
                inc(ve.tensor_tensor(mivA[:], muT[:], ivA[:], ALU.mult), "V", 1)
                w(ve, "V", cnt["V"])
                inc(ve.tensor_tensor(m2ivA[:], muT[:], mivA[:], ALU.mult), "V", 1)
                w(ve, "V", cnt["V"])
                dd = ddisc8[:].rearrange("p (c k) -> p c k", k=48)
                for j, srcA in enumerate((ivA, mivA, m2ivA)):
                    s8 = srcA[:].rearrange("p (c k) -> p c k", k=8)
                    ins = ve.tensor_copy(dd[:, :, 8 * j:8 * j + 8], s8)
                inc(ins, "V", 1)
                w(ve, "V", cnt["V"])
                for j, srcA in enumerate((ivA, mivA, m2ivA)):
                    s8 = srcA[:].rearrange("p (c k) -> p c k", k=8)
                    ins = ve.scalar_tensor_tensor(
                        dd[:, :, 24 + 8 * j:32 + 8 * j], dd[:, :, 8 * j:8 * j + 8],
                        -1.0, s8, ALU.mult, ALU.add)
                inc(ins, "V", 1)
                w(ve, "V", mom_V)
                md = momdc8[:].rearrange("p (c k) -> p c k", k=5)
                ve.tensor_copy(md[:, :, 0:1],
                               m1c[:].rearrange("p (c k) -> p c k", k=1))
                ins = ve.tensor_copy(md[:, :, 2:3],
                                     m2c[:].rearrange("p (c k) -> p c k", k=1))
                inc(ins, "V", 1)
                w(ve, "V", cnt["V"])
                ve.scalar_tensor_tensor(md[:, :, 1:2], md[:, :, 0:1], -1.0,
                                        m1c[:].rearrange("p (c k) -> p c k", k=1),
                                        ALU.mult, ALU.add)
                ins = ve.scalar_tensor_tensor(md[:, :, 3:4], md[:, :, 2:3], -1.0,
                                              m2c[:].rearrange("p (c k) -> p c k", k=1),
                                              ALU.mult, ALU.add)
                inc(ins, "V", 1)
        fl_build_V = cnt["V"]
        if (blk := mblk) or True:
            @blk.tensor
            def _(te):
                w(te, "V", fl_build_V)
                for dc in range(NDC):
                    ins = te.matmul(psF1[0:48, 0:5], ddisc8[:, dc * 48:(dc + 1) * 48],
                                    momdc8[:, dc * 5:(dc + 1) * 5],
                                    start=(dc == 0), stop=(dc == NDC - 1))
                    assert inc(ins, "T", 1) == tfl[dc]

        V4, T4 = cnt["V"], cnt["T"]
        mk_v = lambda k: V4 + k
        if (blk := mblk) or True:
            @blk.vector
            def _(ve):
                w(ve, "T", tfl[NDC - 1])
                assert inc(ve.tensor_copy(loss5[:], psF1[0:48, 0:5]), "V", 1) == mk_v(1)
                w(ve, "V", mk_v(1))
                assert inc(ve.tensor_tensor(ppg[:], loss5[:], wgrid[:], ALU.mult),
                           "V", 1) == mk_v(2)
                w(ve, "V", mk_v(2))
                assert inc(ve.reduce_sum(pp[:], ppg[:], axis=AX.X), "V", 1) == mk_v(3)
                w(ve, "V", mk_v(3))
                assert inc(ve.tensor_copy(pphl[:, 0:1], pp[:]), "V", 1) == mk_v(4)
                w(ve, "V", mk_v(4))
                ins = ve.scalar_tensor_tensor(pphl[:, 1:2], pphl[:, 0:1], -1.0,
                                              pp[:], ALU.mult, ALU.add)
                assert inc(ins, "V", 1) == mk_v(5)
                w(ve, "T", T4 + 1)
                assert inc(ve.tensor_copy(lrow[:], psF1[0:1, 16:24]), "V", 1) == mk_v(6)
                w(ve, "V", mk_v(6))
                assert inc(ve.tensor_reduce(lmin[:], lrow[:], AX.X, ALU.min),
                           "V", 1) == mk_v(7)
                w(ve, "V", mk_v(7))
                assert inc(ve.tensor_scalar(ohrow[:], lrow[:], lmin[:], None,
                                            ALU.is_equal), "V", 1) == mk_v(8)
                w(ve, "V", mk_v(8))
                ve.tensor_copy(ohrowb[:], ohrow[:])
                ins = ve.tensor_tensor(ohj[:], ohrow[:], iotar[:], ALU.mult)
                assert inc(ins, "V", 1) == mk_v(9)
                w(ve, "V", mk_v(9))
                assert inc(ve.reduce_sum(idxv[:, 0:1], ohj[:], axis=AX.X),
                           "V", 1) == mk_v(10)
                w(ve, "T", T4 + 3)
                ve.tensor_copy(ohcol[:], psF1[0:8, 24:25])
                assert inc(ve.tensor_copy(ohbc[:], psF1[0:128, 32:40]), "V", 1) == mk_v(11)

            @blk.tensor
            def _(te):
                w(te, "V", mk_v(5))
                te.matmul(psF1[0:1, 16:24], pphl[:, 0:1], esel[:],
                          start=True, stop=False)
                ins = te.matmul(psF1[0:1, 16:24], pphl[:, 1:2], esel[:],
                                start=False, stop=True)
                assert inc(ins, "T", 1) == T4 + 1
                w(te, "V", mk_v(9))
                ins = te.transpose(psF1[0:8, 24:25], ohrow[:], identf[0:1, 0:1])
                assert inc(ins, "T", 1) == T4 + 2
                ins = te.matmul(psF1[0:128, 32:40], onesrow[:], ohrowb[:],
                                start=True, stop=True)
                assert inc(ins, "T", 1) == T4 + 3
        idx_V = cnt["V"]
        tap("lrow", lrow[:], [1, 8], "V", mk_v(6))
        tap("ohbc", ohbc[:], [128, 8], "V")

        if (blk := mblk) or True:
            @blk.vector
            def _(ve):
                w(ve, "V", mk_v(11))
                inc(ve.tensor_scalar(bucol[:, 0:8], uball[:, 0:8], ohbc[:, 0:1],
                                     None, ALU.mult), "V", 1)
                inc(ve.tensor_scalar(bdcol[:, 0:1], dball[:, 0:1], ohbc[0:64, 0:1],
                                     None, ALU.mult), "V", 1)
                for e in range(1, E):
                    w(ve, "V", cnt["V"])
                    src_u = bucol[:, 8 * ((e + 1) % 2):8 * ((e + 1) % 2) + 8]
                    dst_u = bucol[:, 8 * (e % 2):8 * (e % 2) + 8]
                    inc(ve.scalar_tensor_tensor(dst_u, uball[:, e * 8:(e + 1) * 8],
                                                ohbc[:, e:e + 1], src_u, ALU.mult,
                                                ALU.add), "V", 1)
                    src_d = bdcol[:, (e + 1) % 2:(e + 1) % 2 + 1]
                    dst_d = bdcol[:, e % 2:e % 2 + 1]
                    inc(ve.scalar_tensor_tensor(dst_d, dball[:, e:e + 1],
                                                ohbc[0:64, e:e + 1], src_d, ALU.mult,
                                                ALU.add), "V", 1)
                w(ve, "V", cnt["V"])
                inc(ve.tensor_copy(bucolb[:], bucol[:, 8:16]), "V", 1)
                w(ve, "V", mk_v(10))
                inc(ve.tensor_copy(idxbb[:], idxv[:, 0:1]), "V", 1)
        idxbb_V = cnt["V"]
        budcol_V = idxbb_V - 1
        TG = cnt["T"]
        if (blk := mblk) or True:
            @blk.tensor
            def _(te):
                w(te, "V", idxbb_V)
                ins = te.matmul(psF1[0:128, 40:41], onesrow[:], idxbb[:],
                                start=True, stop=True)
                inc(ins, "T", 1)
        tidxb = cnt["T"]
        if (blk := mblk) or True:
            @blk.vector
            def _(ve):
                w(ve, "T", tidxb)
                w(ve, "G", iota_G)
                inc(ve.tensor_copy(idxbc[:], psF1[0:128, 40:41]), "V", 1)
                w(ve, "V", cnt["V"])
                for gi in range(17):
                    if gi < 8:
                        mulc, addc = 1024.0, float(gi * 128)
                    elif gi < 16:
                        mulc, addc = 1024.0, float((gi - 8) * 128)
                    else:
                        mulc, addc = 64.0, 0.0
                    inc(ve.tensor_scalar(gidxf[:, gi:gi + 1], idxbc[:], mulc, addc,
                                         ALU.mult, ALU.add), "V", 1)
                w(ve, "V", cnt["V"])
                inc(ve.tensor_single_scalar(gidxf[:], gidxf[:], iotac[:], ALU.add),
                    "V", 1)
                w(ve, "V", cnt["V"])
                inc(ve.tensor_copy(gidx17[:], gidxf[:]), "V", 1)
        gidx_V = cnt["V"]

        G0 = cnt["dG"]
        if (blk := mblk) or True:
            @blk.gpsimd
            def _(g):
                w(g, "V", gidx_V)
                for c in range(8):
                    inc(g.indirect_dma_start(
                        dwsel[:, c * R:(c + 1) * R], None, d_in["downW"],
                        IndirectOffsetOnAxis(gidx17[:, 8 + c:9 + c], 0)), "dG", 16)
                inc(g.indirect_dma_start(
                    upsel[:], None, d_in["upW"],
                    IndirectOffsetOnAxis(gidx17[0:64, 16:17], 0)), "dGu", 16)
                for c in range(8):
                    inc(g.indirect_dma_start(
                        convsel[:, c * D:(c + 1) * D], None, d_in["convT"],
                        IndirectOffsetOnAxis(gidx17[:, c:c + 1], 0)), "dGc", 16)
        down_dG = cnt["dG"]
        up_dG = cnt["dGu"]
        conv_dGc = cnt["dGc"]
        tap("dwsel", dwsel[:], [128, NDC * R], "dG")
        tap("upsel", upsel[:], [64, D], "dGu")
        tap("convsel", convsel[:], [128, NDC * D], "dGc")
        if stage == "fl":
            return finish_ret()

        T6, V6, S6 = cnt["T"], cnt["V"], cnt["S"]
        tdn = [T6 + lt + 1 for lt in range(4)]
        shv = [S6 + lt + 1 for lt in range(4)]
        tup = [T6 + 4 + i + 1 for i in range(32)]
        tbias3 = T6 + 4 + 32 + 1
        aev = [0] * 32
        _vv, _ss = V6, S6 + 4
        for i in range(32):
            if i % 2 == 0:
                _vv += 1; aev[i] = ("V", _vv)
            else:
                _ss += 1; aev[i] = ("S", _ss)
        vbias3 = _vv + 1
        if (blk := mblk) or True:
            @blk.tensor
            def _(te):
                w(te, "dW", wbase_dL)
                w(te, "dG", down_dG)
                for lt in range(4):
                    if lt >= 2:
                        w(te, "S", shv[lt - 2])
                    for dc in range(NDC):
                        ins = te.matmul(
                            psH[:, (lt % 2) * 512:(lt % 2 + 1) * 512],
                            dwsel[:, dc * R:(dc + 1) * R],
                            s1fm[:, dc * L + lt * 512:dc * L + (lt + 1) * 512],
                            start=(dc == 0), stop=(dc == NDC - 1))
                    assert inc(ins, "T", 1) == tdn[lt]
                w(te, "dGu", up_dG)
                for oc in range(NDC):
                    for lt in range(4):
                        i = oc * 4 + lt
                        if i >= 2:
                            w(te, aev[i - 2][0], aev[i - 2][1])
                        w(te, "S", shv[lt])
                        ins = te.matmul(
                            psA2[:, (i % 2) * 512:(i % 2 + 1) * 512],
                            upsel[:, oc * 128:(oc + 1) * 128],
                            h_sb[:, lt * 512:(lt + 1) * 512],
                            start=True, stop=True)
                        assert inc(ins, "T", 1) == tup[i]
                w(te, "dGc", conv_dGc)
                w(te, "V", budcol_V)
                for ot in range(2):
                    te.matmul(psH[0:1, ot * 512:(ot + 1) * 512], ohcol[:],
                              cball[:, ot * 512:(ot + 1) * 512],
                              start=True, stop=False)
                    for dc in range(NDC):
                        ins = te.matmul(
                            psH[0:1, ot * 512:(ot + 1) * 512],
                            bucolb[:, dc:dc + 1],
                            convsel[:, dc * D + ot * 512:dc * D + (ot + 1) * 512],
                            start=False, stop=(dc == NDC - 1))
                assert inc(ins, "T", 1) == tbias3

            @blk.scalar
            def _(se):
                for lt in range(4):
                    w(se, "T", tdn[lt])
                    ins = se.activation(h_sb[:, lt * 512:(lt + 1) * 512],
                                        psH[:, (lt % 2) * 512:(lt % 2 + 1) * 512],
                                        AF.Relu, bias=bdcol[:, 1:2], scale=1.0)
                    assert inc(ins, "S", 1) == shv[lt]

            @blk.vector
            def _(ve):
                w(ve, "V", gidx_V)
                for i in range(0, 32, 2):
                    oc, lt = i // 4, i % 4
                    w(ve, "T", tup[i])
                    ins = ve.tensor_copy(
                        a_sb[:, oc * L + lt * 512:oc * L + (lt + 1) * 512],
                        psA2[:, (i % 2) * 512:(i % 2 + 1) * 512])
                    assert inc(ins, "V", 1) == aev[i][1]
                w(ve, "T", tbias3)
                ins = ve.tensor_tensor(biasrow[:], psH[0:1, 0:1024], bbrow[:],
                                       ALU.add)
                assert inc(ins, "V", 1) == vbias3

            @blk.scalar
            def _(se):
                for i in range(1, 32, 2):
                    oc, lt = i // 4, i % 4
                    w(se, "T", tup[i])
                    ins = se.activation(
                        a_sb[:, oc * L + lt * 512:oc * L + (lt + 1) * 512],
                        psA2[:, (i % 2) * 512:(i % 2 + 1) * 512],
                        AF.Copy, bias=0.0, scale=1.0)
                    assert inc(ins, "S", 1) == aev[i][1]
        a_V = cnt["V"]
        a_S = cnt["S"]
        assert cnt["T"] == tbias3 and a_V == vbias3
        tap("h", h_sb[:], [64, L], "S")
        tap("a", a_sb[:], [128, NDC * L], "V")

        T7, V7, S7 = cnt["T"], cnt["V"], cnt["S"]
        tout = [T7 + lc + 1 for lc in range(NLC)]
        vu2 = [V7 + 6 * lc + 1 for lc in range(NLC)]
        vms2 = [V7 + 6 * lc + 2 for lc in range(NLC)]
        vm22 = [V7 + 6 * lc + 3 for lc in range(NLC)]
        vst2 = [V7 + 6 * lc + 4 for lc in range(NLC)]
        vrc2 = [V7 + 6 * lc + 5 for lc in range(NLC)]
        vap2 = [V7 + 6 * lc + 6 for lc in range(NLC)]
        ssq2 = [S7 + 2 * lc + 1 for lc in range(NLC)]
        ssd2 = [S7 + 2 * lc + 2 for lc in range(NLC)]
        oout = [16 * (lc // 2 + 1) for lc in range(NLC)]
        if (blk := mblk) or True:
            @blk.tensor
            def _(te):
                w(te, "V", vbias3)
                w(te, "S", a_S)
                w(te, "dGc", conv_dGc)
                for lc in range(NLC):
                    if lc >= 2:
                        w(te, "V", vu2[lc - 2])
                    for ot in range(2):
                        if lc >= 2:
                            for dc in range(NDC):
                                te.matmul(
                                    psO2[:, (lc % 2) * D + ot * 512:(lc % 2) * D + (ot + 1) * 512],
                                    s1fm[:, dc * L + lc * 128:dc * L + (lc + 1) * 128],
                                    wbase[:, dc * D + ot * 512:dc * D + (ot + 1) * 512],
                                    start=(dc == 0), stop=False)
                        for dc in range(NDC):
                            te.matmul(
                                psO2[:, (lc % 2) * D + ot * 512:(lc % 2) * D + (ot + 1) * 512],
                                a_sb[:, dc * L + lc * 128:dc * L + (lc + 1) * 128],
                                convsel[:, dc * D + ot * 512:dc * D + (ot + 1) * 512],
                                start=False, stop=False)
                        ins = te.matmul(
                            psO2[:, (lc % 2) * D + ot * 512:(lc % 2) * D + (ot + 1) * 512],
                            onesrow[:], biasrow[:, ot * 512:(ot + 1) * 512],
                            start=False, stop=True)
                    assert inc(ins, "T", 1) == tout[lc]

            @blk.vector
            def _(ve):
                for lc in range(NLC):
                    w(ve, "T", tout[lc])
                    if lc >= 2:
                        w(ve, "dOa" if lc % 2 == 0 else "dOb", oout[lc - 2])
                    ins = ve.scalar_tensor_tensor(
                        s2t[:, (lc % 2) * D:(lc % 2 + 1) * D],
                        psO2[:, (lc % 2) * D:(lc % 2 + 1) * D], 1.0,
                        s1bf[:, lc * D:(lc + 1) * D],
                        ALU.mult, ALU.add, accum_out=stat[:, 0:1])
                    assert inc(ins, "V", 1) == vu2[lc]
                    w(ve, "S", ssq2[lc])
                    ve.tensor_scalar(stat[:, 2:3], stat[:, 0:1], 1.0 / D, None, ALU.mult)
                    ins = ve.tensor_scalar(stat[:, 4:5], stat[:, 1:2], 1.0 / D, None,
                                           ALU.mult)
                    assert inc(ins, "V", 1) == vms2[lc]
                    w(ve, "V", vms2[lc])
                    ins = ve.tensor_tensor(stat[:, 5:6], stat[:, 2:3], stat[:, 2:3], ALU.mult)
                    assert inc(ins, "V", 1) == vm22[lc]
                    w(ve, "V", vm22[lc])
                    ins = ve.scalar_tensor_tensor(stat[:, 6:7], stat[:, 4:5], EPS_LN,
                                                  stat[:, 5:6], ALU.add, ALU.subtract)
                    assert inc(ins, "V", 1) == vst2[lc]
                    w(ve, "S", ssd2[lc])
                    ins = ve.reciprocal(stat[:, 8:9], stat[:, 7:8])
                    assert inc(ins, "V", 1) == vrc2[lc]
                    w(ve, "V", vrc2[lc])
                    ins = ve.tensor_scalar(
                        s2t[:, (lc % 2) * D:(lc % 2 + 1) * D],
                        s2t[:, (lc % 2) * D:(lc % 2 + 1) * D],
                        stat[:, 2:3], stat[:, 8:9], ALU.subtract, ALU.mult)
                    assert inc(ins, "V", 1) == vap2[lc]

            @blk.scalar
            def _(se):
                for lc in range(NLC):
                    w(se, "V", vu2[lc])
                    ins = se.activation(scrap[:, 0:D],
                                        s2t[:, (lc % 2) * D:(lc % 2 + 1) * D],
                                        AF.Square, accum_out=stat[:, 1:2])
                    assert inc(ins, "S", 1) == ssq2[lc]
                    w(se, "V", vst2[lc])
                    ins = se.activation(stat[:, 7:8], stat[:, 6:7],
                                        AF.Sqrt, bias=0.0, scale=1.0)
                    assert inc(ins, "S", 1) == ssd2[lc]

            @blk.sync
            def _(sy):
                for lc in range(NLC):
                    sy.wait_ge(sem["V"], vap2[lc])
                    ins = sy.dma_start(d_out.rearrange("(t p) d -> t p d", p=128)[lc],
                                       s2t[:, (lc % 2) * D:(lc % 2 + 1) * D])
                    assert inc(ins, "dOa" if lc % 2 == 0 else "dOb", 16) == oout[lc]
        s_fn.close(); open_stacks.remove(s_fn)
        s_wt.close(); open_stacks.remove(s_wt)
        return finish_ret()


def _finish(nc, sem, cnt, inc, tap_list):
    with nc.Block() as blk:
        @blk.gpsimd
        def _(g):
            for name, dram, sbap, semn, val in tap_list:
                g.wait_ge(sem[semn], val)
                inc(g.dma_start(dram, sbap), "dT", 16)
            g.wait_ge(sem["dT"], cnt["dT"])
            g.wait_ge(sem["dOa"], cnt["dOa"])
            g.wait_ge(sem["dOb"], cnt["dOb"])



def kernel(**inputs):
    import numpy as np
    from concourse.bass_utils import run_bass_kernel_spmd
    shared = host_prep(inputs)
    x = np.asarray(inputs["L0_attn_x"], np.float32)
    nc = build()
    maps = []
    for b in range(8):
        m = {"i_" + k: v for k, v in shared.items()}
        m["x"] = np.ascontiguousarray(x[b])
        maps.append(m)
    res = run_bass_kernel_spmd(nc, maps, core_ids=list(range(8)))
    out = np.stack([np.asarray(res.results[b]["out"], np.float32) for b in range(8)], 0)
    return out


# version 17
# speedup vs baseline: 1.2288x; 1.0269x over previous
import numpy as np
from contextlib import ExitStack

import concourse.bass as bass
import concourse.mybir as mybir
from concourse.bass import IndirectOffsetOnAxis

F32 = mybir.dt.float32
BF16 = mybir.dt.bfloat16
I32 = mybir.dt.int32
ALU = mybir.AluOpType
AF = mybir.ActivationFunctionType
AX = mybir.AxisListType

L, D, H, DH, NF, E, R = 2048, 1024, 16, 64, 64, 8, 64
NLC, NDC, NCP = 16, 8, 8
EPS_ATT = 1e-6
EPS_LN = 1e-5


def host_prep(inp):
    c = np.ascontiguousarray
    f32 = np.float32
    g = lambda k: np.asarray(inp[k], f32)
    s = {}
    s["wqT"] = c(g("L0_attn_Wq").T)
    s["wkT"] = c(g("L0_attn_Wk").T)
    s["wvT"] = c(g("L0_attn_Wv").T)
    s["woT"] = c(g("L0_attn_Wo").T)
    s["baseT"] = c(g("L0_ffn_base_W").T)
    s["convT"] = c(g("L0_ffn_conv_W").transpose(0, 2, 1).reshape(E * D, D))
    s["downW"] = c(g("L0_ffn_down_W").reshape(E * D, R))
    s["upW"] = c(g("L0_ffn_up_W").reshape(E * R, D))
    s["dball"] = c(g("L0_ffn_down_b").T)
    s["uball"] = c(g("L0_ffn_up_b").reshape(E, 8, 128).transpose(2, 0, 1).reshape(128, 64))
    s["cball"] = c(g("L0_ffn_conv_b"))
    s["muT"] = c(g("L0_ffn_disc_mu").T)
    s["sigT"] = c(g("L0_ffn_disc_sigma").T)
    W0 = g("L0_attn_Wrf")[0]
    w0blk = np.zeros((NCP * 128, 128), f32)
    for cp in range(NCP):
        w0blk[cp * 128:cp * 128 + 64, 0:64] = W0[2 * cp]
        w0blk[cp * 128 + 64:cp * 128 + 128, 64:128] = W0[2 * cp + 1]
    s["w0blk"] = w0blk
    s["bqk"] = c(np.stack([g("L0_attn_bq").reshape(8, 128).T,
                           g("L0_attn_bk").reshape(8, 128).T],
                          axis=2).reshape(128, 16))
    s["bvrow"] = c(g("L0_attn_bv").reshape(1, D))
    s["borow"] = c(g("L0_attn_bo").reshape(1, D))
    s["bbrow"] = c(g("L0_ffn_base_b").reshape(1, D))
    s["identc"] = np.eye(128, dtype=f32)
    dsel = np.zeros((16, NCP * 128), f32)
    for cp in range(NCP):
        dsel[2 * cp, cp * 128:cp * 128 + 64] = 1.0
        dsel[2 * cp + 1, cp * 128 + 64:cp * 128 + 128] = 1.0
    dsel64 = np.zeros((64, NCP * 128), f32)
    dsel64[0:16] = dsel
    dsel64[32:48] = dsel
    s["dsel64"] = dsel64
    par = np.zeros((1, 256), f32)
    par[0, 0:64] = 1.0
    par[0, 128 + 64:256] = 1.0
    s["parab"] = par
    kkm = np.zeros((128, NCP * 16), f32)
    for cp in range(NCP):
        kkm[0:64, cp * 16 + 2 * cp] = -0.5
        kkm[64:128, cp * 16 + 2 * cp + 1] = -0.5
    s["kkmask"] = kkm
    s["onesrow"] = np.ones((1, 128), f32)
    wg = np.zeros((48, 5), f32)
    wg[0:8, 2] = 1.0; wg[0:8, 3] = 1.0
    wg[24:32, 2] = 1.0
    wg[8:16, 0] = -2.0; wg[8:16, 1] = -2.0
    wg[32:40, 0] = -2.0
    wg[16:24, 4] = 1.0; wg[40:48, 4] = 1.0
    s["wgrid"] = wg
    es = np.zeros((48, 8), f32)
    for e in range(8):
        for blk in range(6):
            es[blk * 8 + e, e] = 1.0
    s["esel"] = es
    return s


def build(taps=(), stage="full"):
    nc = bass.Bass("TRN2", target_bir_lowering=False)
    taps = set(taps)
    tap_list = []
    sem, cnt = {}, {}

    import sys as _sys
    _marks = {}
    build._marks = _marks

    def inc(ins, name, n=1):
        ins.then_inc(sem[name], n)
        cnt[name] += n
        _marks[(name, cnt[name])] = _sys._getframe(1).f_lineno
        return cnt[name]

    def w(eng, name, val):
        assert val <= 10 ** 9
        eng.wait_ge(sem[name], val)

    dt = nc.dram_tensor
    d_x = dt("x", [L, D], F32, kind="ExternalInput").ap()
    names = [("wqT", [D, D]), ("wkT", [D, D]), ("wvT", [D, D]), ("woT", [D, D]),
             ("baseT", [D, D]), ("convT", [E * D, D]), ("downW", [E * D, R]),
             ("upW", [E * R, D]), ("dball", [64, 8]), ("uball", [128, 64]),
             ("cball", [8, D]), ("muT", [D, E]), ("sigT", [D, E]),
             ("w0blk", [NCP * 128, 128]), ("bqk", [128, 16]), ("bvrow", [1, D]),
             ("borow", [1, D]), ("bbrow", [1, D]), ("identc", [128, 128]),
             ("dsel64", [64, NCP * 128]),
             ("parab", [1, 256]), ("kkmask", [128, NCP * 16]),
             ("onesrow", [1, 128]), ("wgrid", [48, 5]), ("esel", [48, 8])]
    d_in = {nm: dt("i_" + nm, shp, F32, kind="ExternalInput").ap() for nm, shp in names}
    d_out = dt("out", [L, D], F32, kind="ExternalOutput").ap()
    x_t = d_x.rearrange("(t p) d -> t p d", p=128)

    def tap(name, sbuf_ap, shape, semn, val=None):
        if name in taps:
            tp = dt("tap_" + name, shape, F32, kind="ExternalOutput").ap()
            tap_list.append((name, tp, sbuf_ap, semn,
                             cnt[semn] if val is None else val))

    ctx = ExitStack()
    SB = lambda n, s, t: ctx.enter_context(nc.sbuf_tensor(n, s, t))
    open_stacks = []

    def finish_ret():
        mblk.__exit__(None, None, None)
        _finish(nc, sem, cnt, inc, tap_list)
        for st_ in reversed(open_stacks):
            st_.close()
        return nc

    with ctx:
        for s_ in ["dL", "dH", "dW", "dXa", "dXb", "dG", "dGu", "dGc", "dOa", "dOb", "dT", "dC", "dRa", "dRb", "dRc", "dRd", "T", "V", "S", "G", "P"]:
            sem[s_] = ctx.enter_context(nc.semaphore(s_))
            cnt[s_] = 0

        identb = SB("identb", [128, 128], BF16)
        identf = SB("identf", [128, 128], F32)
        w0blk = SB("w0blk", [128, NCP * 128], BF16)
        dsel64 = SB("dsel64", [64, NCP * 128], BF16)
        parab = SB("parab", [1, 256], BF16)
        kkmask = SB("kkmask", [128, NCP * 16], BF16)
        onesrow = SB("onesrow", [1, 128], BF16)
        bqk = SB("bqk", [128, 16], F32)
        bvrow = SB("bvrow", [1, D], BF16)
        wgrid = SB("wgrid", [48, 5], F32)
        esel = SB("esel", [48, 8], BF16)

        m1c = SB("m1c", [128, 8], F32)
        m2c = SB("m2c", [128, 8], F32)
        momdc8 = SB("momdc8", [128, 40], BF16)
        sctA = SB("sctA", [128, 64], F32)
        ivA = SB("ivA", [128, 64], F32)
        mivA = SB("mivA", [128, 64], F32)
        m2ivA = SB("m2ivA", [128, 64], F32)
        ddisc8 = SB("ddisc8", [128, 384], BF16)
        loss5 = SB("loss5", [48, 5], F32)
        ppg = SB("ppg", [48, 5], F32)
        pp = SB("pp", [48, 1], F32)
        pphl = SB("pphl", [48, 2], BF16)
        lcol2 = SB("lcol2", [8, 2], F32)
        lcol = SB("lcol", [8, 1], F32)
        lrow = SB("lrow", [1, 8], F32)
        lmin = SB("lmin", [1, 1], F32)
        ohrow = SB("ohrow", [1, 8], F32)
        ohrowb = SB("ohrowb", [1, 8], BF16)
        ohj = SB("ohj", [1, 8], F32)
        ohcol = SB("ohcol", [8, 1], BF16)
        ohbc = SB("ohbc", [128, 8], F32)
        idxv = SB("idxv", [1, 2], F32)
        iotar = SB("iotar", [1, 8], F32)
        bdcol = SB("bdcol", [64, 2], F32)
        bucol = SB("bucol", [128, 16], F32)
        bucolb = SB("bucolb", [128, 8], BF16)
        muT = SB("muT", [128, NDC * E], F32)
        sigT = SB("sigT", [128, NDC * E], F32)
        dball = SB("dball", [64, 8], F32)
        uball = SB("uball", [128, 64], F32)
        bbrow = SB("bbrow", [1, D], BF16)
        iotac = SB("iotac", [128, 1], F32)
        idxbc = SB("idxbc", [128, 1], F32)
        idxbb = SB("idxbb", [1, 1], BF16)
        gidxf = SB("gidxf", [128, 17], F32)
        gidx17 = SB("gidx17", [128, 17], I32)

        mblk = nc.Block()
        mblk.__enter__()

        if (blk := mblk) or True:
            @blk.gpsimd
            def _(g):
                g.iota(iotar[:], [[1, 8]], base=0, channel_multiplier=0,
                       allow_small_or_imprecise_dtypes=True)
                g.iota(iotac[:], [[1, 1]], base=0, channel_multiplier=1,
                       allow_small_or_imprecise_dtypes=True)
                inc(g.nop(), "G", 1)
                inc(g.dma_start(identb[:], d_in["identc"]), "dL", 16)

            @blk.sync
            def _(sy):
                for dstap, nm, rr in [
                        (identf[:], "identc", None), (bqk[:], "bqk", None),
                        (wgrid[:], "wgrid", None),
                        (muT[:].rearrange("p (c e) -> p c e", e=E), "muT",
                         "(c p) e -> p c e"),
                        (sigT[:].rearrange("p (c e) -> p c e", e=E), "sigT",
                         "(c p) e -> p c e"),
                        (dball[:], "dball", None), (uball[:], "uball", None)]:
                    src = d_in[nm]
                    if rr:
                        src = src.rearrange(rr, p=128)
                    inc(sy.dma_start(dstap, src), "dH", 16)
        consts_dL = cnt["dL"]
        consts_dH = cnt["dH"]
        iota_G = cnt["G"]

        s_big = ExitStack()
        open_stacks.append(s_big)
        wq = s_big.enter_context(nc.sbuf_tensor("wq", [128, NDC * D], BF16, side="right"))
        wk = s_big.enter_context(nc.sbuf_tensor("wk", [128, NDC * D], BF16, side="right"))
        wv = s_big.enter_context(nc.sbuf_tensor("wv", [128, NDC * D], BF16, side="right"))
        xfm = s_big.enter_context(nc.sbuf_tensor("xfm", [128, NDC * L], BF16, side="right"))

        s_att = ExitStack()
        open_stacks.append(s_att)
        SA = lambda n, s, t: s_att.enter_context(nc.sbuf_tensor(n, s, t))
        eq_sb = SA("eq_sb", [128, NCP * L], BF16)
        zd8 = SA("zd8", [8, 128], F32)
        d16 = SA("d16", [128, 8], F32)
        dcolA = SA("dcolA", [8, 1], F32)
        dcolB = SA("dcolB", [8, 1], F32)
        numS = [SA(f"numS{c}", [128, 128], BF16) for c in range(NCP)]
        denS = [SA(f"denS{c}", [128, 16], BF16) for c in range(NCP)]
        s_att1 = ExitStack()
        open_stacks.append(s_att1)
        SA1 = lambda n, s, t: s_att1.enter_context(nc.sbuf_tensor(n, s, t))
        v_sb = SA1("v_sb", [128, NLC * H * 65], BF16)
        k_sb = SA1("k_sb", [128, NCP * L], BF16)
        mkA = SA1("mkA", [1, 8], BF16)
        mkB = SA1("mkB", [1, 8], BF16)
        zArow = SA1("zArow", [1, 8], BF16)
        zBrow = SA1("zBrow", [1, 8], BF16)
        kkT_sb = SA1("kkT_sb", [128, 256], F32)
        qqT_sb = kkT_sb
        mqfm = None
        colmax = SA1("colmax", [128, 1], F32)
        scr3 = SA1("scr3", [1, 128], F32)
        mbias = SA1("mbias", [128, 1], F32)
        ebias = SA1("ebias", [128, 8], F32)
        v4 = v_sb[:].rearrange("p (lc h c) -> p lc h c", lc=NLC, h=H, c=65)
        s_aps = ExitStack()
        open_stacks.append(s_aps)
        psK = s_aps.enter_context(nc.psum_tensor("psK", [128, 1024], F32))
        psS = s_aps.enter_context(nc.psum_tensor("psS", [128, 512], F32))

        with ExitStack() as ph:
            xb = ph.enter_context(nc.sbuf_tensor("xb", [128, 4 * D], BF16, side="right"))
            ptr = ph.enter_context(nc.psum_tensor("ptr", [128, 2048], BF16))
            _pt, _pv = cnt["T"], cnt["V"]
            txf = [0] * NLC
            tv = [[0, 0] for _ in range(NLC)]
            vxf = [0] * NLC
            vv = [[0, 0] for _ in range(NLC)]
            _ones_v = _pv + 1
            _pv += 1
            emit_order = []
            for lc in range(NLC):
                emit_order.append(("tr", lc))
                if lc >= 4:
                    emit_order.append(("v", lc - 4))
            for lc in range(NLC - 4, NLC):
                emit_order.append(("v", lc))
            for kind, lc in emit_order:
                if kind == "tr":
                    txf[lc] = _pt + 1
                    _pt += 1
                    vxf[lc] = _pv + 1
                    _pv += 1
                else:
                    tv[lc][0] = _pt + 1
                    tv[lc][1] = _pt + 2
                    _pt += 2
                    vv[lc][0] = _pv + 1
                    vv[lc][1] = _pv + 2
                    _pv += 2
            wv_dL = cnt["dW"] + 16
            grp2_dL = [consts_dL + 32,
                       consts_dL + 96] + \
                      [consts_dL + 96 + 32 * g for g in range(1, 7)]
            wqk_dL = wv_dL + 32
            late_dL = grp2_dL[7] + 6 * 16

            if (blk := mblk) or True:
                @blk.gpsimd
                def _(g):
                    def tiles(grp):
                        for i in range(2):
                            lc = grp * 2 + i
                            slot = (grp % 2) * 2 + i
                            inc(g.dma_start(xb[:, slot * D:(slot + 1) * D],
                                            x_t[lc]), "dL", 16)
                    tiles(0)
                    g.wait_ge(sem["dL"], grp2_dL[0])
                    inc(g.dma_start(onesrow[:], d_in["onesrow"]), "dL", 16)
                    inc(g.dma_start(bvrow[:], d_in["bvrow"]), "dL", 16)
                    g.wait_ge(sem["dL"], consts_dL + 64)
                    tiles(1)
                    inc(g.dma_start(
                        wv[:].rearrange("p (c n) -> p c n", n=D),
                        d_in["wvT"].rearrange("(c p) n -> p c n", p=128)), "dW", 16)
                    for grp in range(2, 8):
                        g.wait_ge(sem["dL"], grp2_dL[grp - 1])
                        g.wait_ge(sem["T"], txf[2 * (grp - 1) - 1])
                        tiles(grp)
                    for wsb, nm in [(wq, "wqT"), (wk, "wkT")]:
                        inc(g.dma_start(
                            wsb[:].rearrange("p (c n) -> p c n", n=D),
                            d_in[nm].rearrange("(c p) n -> p c n", p=128)), "dW", 16)
                    g.wait_ge(sem["dL"], grp2_dL[7])
                    for dstap, nm, rr in [
                            (w0blk[:].rearrange("p (c n) -> p c n", n=128),
                             "w0blk", "(c p) n -> p c n"),
                            (dsel64[:], "dsel64", None),
                            (kkmask[:], "kkmask", None),
                            (parab[:], "parab", None),
                            (esel[:], "esel", None), (bbrow[:], "bbrow", None)]:
                        srcp = d_in[nm]
                        if rr:
                            srcp = srcp.rearrange(rr, p=128)
                        inc(g.dma_start(dstap, srcp), "dL", 16)
                    assert cnt["dL"] == late_dL

                @blk.tensor
                def _(te):
                    for kind, lc in emit_order:
                        if kind == "tr":
                            w(te, "dL", grp2_dL[lc // 2])
                            if lc >= 2:
                                w(te, "V", vxf[lc - 2])
                            slot = ((lc // 2) % 2) * 2 + (lc % 2)
                            for dc in range(NDC):
                                ins = te.transpose(
                                    ptr[:, (lc % 2) * 1024 + dc * 128:(lc % 2) * 1024 + (dc + 1) * 128],
                                    xb[:, slot * D + dc * 128:slot * D + (dc + 1) * 128],
                                    identb[:])
                            assert inc(ins, "T", 1) == txf[lc]
                        else:
                            w(te, "V", vxf[lc])
                            if lc == 0:
                                w(te, "dW", wv_dL)
                                w(te, "dL", consts_dL + 64)
                                w(te, "V", _ones_v)
                            for ot in range(2):
                                it = lc * 2 + ot
                                if it >= 2:
                                    pit = it - 2
                                    w(te, "V", vv[pit // 2][pit % 2])
                                for dc in range(NDC):
                                    te.matmul(
                                        psK[:, (it % 2) * 512:(it % 2 + 1) * 512],
                                        xfm[:, dc * L + lc * 128:dc * L + (lc + 1) * 128],
                                        wv[:, dc * D + ot * 512:dc * D + (ot + 1) * 512],
                                        start=(dc == 0), stop=False)
                                ins = te.matmul(
                                    psK[:, (it % 2) * 512:(it % 2 + 1) * 512],
                                    onesrow[:], bvrow[:, ot * 512:(ot + 1) * 512],
                                    start=False, stop=True)
                                assert inc(ins, "T", 1) == tv[lc][ot]

                @blk.vector
                def _(ve):
                    assert inc(ve.memset(v4[:, :, :, 64:65], 1.0), "V", 1) == _ones_v
                    for kind, lc in emit_order:
                        if kind == "tr":
                            w(ve, "T", txf[lc])
                            dst = xfm[:].rearrange("p (c l) -> p c l", l=L)[:, :, lc * 128:(lc + 1) * 128]
                            ins = ve.tensor_copy(
                                dst, ptr[:, (lc % 2) * 1024:(lc % 2) * 1024 + 1024]
                                .rearrange("p (c k) -> p c k", k=128))
                            assert inc(ins, "V", 1) == vxf[lc]
                        else:
                            for ot in range(2):
                                it = lc * 2 + ot
                                w(ve, "T", tv[lc][ot])
                                ins = ve.tensor_copy(
                                    v4[:, lc, ot * 8:(ot + 1) * 8, 0:64],
                                    psK[:, (it % 2) * 512:(it % 2 + 1) * 512]
                                    .rearrange("p (h c) -> p h c", c=64))
                                assert inc(ins, "V", 1) == vv[lc][ot]
        xfm_V = vxf[NLC - 1]
        wqkv_dL = wqk_dL
        s_apq = ExitStack()
        open_stacks.append(s_apq)
        psQ = s_apq.enter_context(nc.psum_tensor("psQ", [128, L], F32))
        s_kt = ExitStack()
        open_stacks.append(s_kt)
        psKT = s_kt.enter_context(nc.psum_tensor("psKT", [128, 512], F32))
        v_V = cnt["V"]
        tap("xfm", xfm[:], [128, NDC * L], "V")
        tap("v", v_sb[:], [128, NLC * H * 65], "V")
        if stage in ("t0", "p1a"):
            return finish_ret()

        s_qs = ExitStack()
        open_stacks.append(s_qs)
        sq_sb = s_qs.enter_context(nc.sbuf_tensor("sq_sb", [128, L], BF16))
        mqfm = s_qs.enter_context(nc.sbuf_tensor("mqfm", [64, L], BF16))
        T1, V1, S1 = cnt["T"], cnt["V"], cnt["S"]
        tq = [[0] * 4 for _ in range(NCP)]
        tqq = [[0] * 4 for _ in range(NCP)]
        _p = T1
        for c in range(NCP):
            for lt in range(4):
                tq[c][lt] = _p + 1; _p += 1
            if c >= 1:
                for lt in range(4):
                    tqq[c - 1][lt] = _p + 1; _p += 1
        for lt in range(4):
            tqq[NCP - 1][lt] = _p + 1; _p += 1
        vqe = [[V1 + 8 * c + lt + 1 for lt in range(4)] for c in range(NCP)]
        vsqq = [[V1 + 8 * c + 4 + lt + 1 for lt in range(4)] for c in range(NCP)]
        TA_end = T1 + 8 * NCP
        VA_end = V1 + 8 * NCP
        vqq = VA_end + 1
        vm25 = VA_end + 2
        tmq = [TA_end + lc + 1 for lc in range(NLC)]
        vmq2 = [VA_end + 2 + lc + 1 for lc in range(NLC)]
        sscl = S1 + 1
        TB_end = TA_end + NLC
        VB_end = VA_end + 2 + NLC
        tlq = [[TB_end + 4 * c + lt + 1 for lt in range(4)] for c in range(NCP)]
        sexp = [[S1 + 1 + 4 * c + lt + 1 for lt in range(4)] for c in range(NCP)]
        TC_end = TB_end + 4 * NCP
        tk = [[0] * 4 for _ in range(NCP)]
        tkk = [[0] * 4 for _ in range(NCP)]
        _p = TC_end
        for c in range(NCP):
            for lt in range(4):
                tk[c][lt] = _p + 1; _p += 1
            if c >= 1:
                for lt in range(4):
                    tkk[c - 1][lt] = _p + 1; _p += 1
        for lt in range(4):
            tkk[NCP - 1][lt] = _p + 1; _p += 1
        vke = [[VB_end + 8 * c + lt + 1 for lt in range(4)] for c in range(NCP)]
        vsqk = [[VB_end + 8 * c + 4 + lt + 1 for lt in range(4)] for c in range(NCP)]

        if (blk := mblk) or True:
            @blk.tensor
            def _(te):
                w(te, "dW", wqk_dL)
                w(te, "dL", late_dL)
                for cp in range(NCP):
                    for lt in range(4):
                        if cp > 0:
                            w(te, "V", vqe[cp - 1][lt])
                        for dc in range(NDC):
                            ins = te.matmul(
                                psQ[:, lt * 512:(lt + 1) * 512],
                                wq[:, dc * D + cp * 128:dc * D + (cp + 1) * 128],
                                xfm[:, dc * L + lt * 512:dc * L + (lt + 1) * 512],
                                start=(dc == 0), stop=(dc == NDC - 1))
                        assert inc(ins, "T", 1) == tq[cp][lt]
                    for pcp in ([cp - 1] if cp >= 1 else []):
                        for lt in range(4):
                            w(te, "V", vsqq[pcp][lt])
                            for j in range(4):
                                lc = lt * 4 + j
                                ins = te.matmul(
                                    psKT[:, 256 + lc * 16 + 2 * pcp:256 + lc * 16 + 2 * pcp + 2],
                                    sq_sb[:, (pcp % 2) * 0 + lc * 128:(lc + 1) * 128],
                                    kkmask[:, pcp * 16 + 2 * pcp:pcp * 16 + 2 * pcp + 2],
                                    start=True, stop=True)
                            assert inc(ins, "T", 1) == tqq[pcp][lt]
                for lt in range(4):
                    w(te, "V", vsqq[NCP - 1][lt])
                    for j in range(4):
                        lc = lt * 4 + j
                        ins = te.matmul(
                            psKT[:, 256 + lc * 16 + 2 * (NCP - 1):256 + lc * 16 + 2 * (NCP - 1) + 2],
                            sq_sb[:, lc * 128:(lc + 1) * 128],
                            kkmask[:, (NCP - 1) * 16 + 2 * (NCP - 1):(NCP - 1) * 16 + 2 * (NCP - 1) + 2],
                            start=True, stop=True)
                    assert inc(ins, "T", 1) == tqq[NCP - 1][lt]
                w(te, "V", vm25)
                for lc in range(NLC):
                    if lc >= 1:
                        w(te, "V", vmq2[lc - 1])
                    ins = te.transpose(
                        psS[0:16, (lc % 2) * 128:(lc % 2) * 128 + 128],
                        qqT_sb[:, lc * 16:(lc + 1) * 16], identf[:])
                    assert inc(ins, "T", 1) == tmq[lc]
                for cp in range(NCP):
                    if cp == 0:
                        w(te, "V", vmq2[NLC - 1])
                    for lt in range(4):
                        if cp == 0:
                            w(te, "V", vqe[NCP - 1][lt])
                        else:
                            w(te, "S", sexp[cp - 1][lt])
                        te.matmul(psQ[:, lt * 512:(lt + 1) * 512],
                                  w0blk[:, cp * 128:(cp + 1) * 128],
                                  k_sb[:, cp * L + lt * 512:cp * L + (lt + 1) * 512],
                                  start=True, stop=False)
                        ins = te.matmul(psQ[:, lt * 512:(lt + 1) * 512],
                                        dsel64[:, cp * 128:(cp + 1) * 128],
                                        mqfm[:, lt * 512:(lt + 1) * 512],
                                        start=False, stop=True)
                        assert inc(ins, "T", 1) == tlq[cp][lt]
                def _kkgrp(pcp, lt):
                    w(te, "V", vsqk[pcp][lt])
                    for j in range(4):
                        lc = lt * 4 + j
                        ins = te.matmul(
                            psKT[:, lc * 16 + 2 * pcp:lc * 16 + 2 * pcp + 2],
                            sq_sb[:, lc * 128:(lc + 1) * 128],
                            kkmask[:, pcp * 16 + 2 * pcp:pcp * 16 + 2 * pcp + 2],
                            start=True, stop=True)
                    assert inc(ins, "T", 1) == tkk[pcp][lt]
                for cp in range(NCP):
                    for lt in range(4):
                        bank = lt % 2
                        if cp == 0 and lt < 2:
                            w(te, "V", v_V)
                        else:
                            pc, plt = (cp, lt - 2) if lt >= 2 else (cp - 1, lt + 2)
                            w(te, "V", vke[pc][plt])
                        for dc in range(NDC):
                            ins = te.matmul(
                                psK[:, bank * 512:(bank + 1) * 512],
                                wk[:, dc * D + cp * 128:dc * D + (cp + 1) * 128],
                                xfm[:, dc * L + lt * 512:dc * L + (lt + 1) * 512],
                                start=(dc == 0), stop=(dc == NDC - 1))
                        assert inc(ins, "T", 1) == tk[cp][lt]
                    if cp >= 1:
                        for lt in range(4):
                            _kkgrp(cp - 1, lt)
                for lt in range(4):
                    _kkgrp(NCP - 1, lt)

            @blk.vector
            def _(ve):
                w(ve, "dH", consts_dH)
                ve.memset(mqfm[:], 0.0)
                for cp in range(NCP):
                    for lt in range(4):
                        w(ve, "T", tq[cp][lt])
                        ins = ve.tensor_scalar(
                            k_sb[:, cp * L + lt * 512:cp * L + (lt + 1) * 512],
                            psQ[:, lt * 512:(lt + 1) * 512],
                            bqk[:, cp * 2:cp * 2 + 1], None, ALU.add)
                        assert inc(ins, "V", 1) == vqe[cp][lt]
                    for lt in range(4):
                        w(ve, "V", vqe[cp][lt])
                        if cp > 0:
                            w(ve, "T", tqq[cp - 1][lt])
                        ins = ve.tensor_tensor(
                            sq_sb[:, lt * 512:(lt + 1) * 512],
                            k_sb[:, cp * L + lt * 512:cp * L + (lt + 1) * 512],
                            k_sb[:, cp * L + lt * 512:cp * L + (lt + 1) * 512],
                            ALU.mult)
                        assert inc(ins, "V", 1) == vsqq[cp][lt]
                w(ve, "T", tqq[NCP - 1][3])
                assert inc(ve.tensor_copy(qqT_sb[:], psKT[:, 256:512]), "V", 1) == vqq
                w(ve, "S", sscl)
                ins = ve.tensor_scalar(qqT_sb[:], qqT_sb[:], -1.0, -25.0, ALU.mult, ALU.add)
                assert inc(ins, "V", 1) == vm25
                for lc in range(NLC):
                    w(ve, "T", tmq[lc])
                    ins = ve.tensor_copy(
                        mqfm[0:16, lc * 128:(lc + 1) * 128],
                        psS[0:16, (lc % 2) * 128:(lc % 2) * 128 + 128])
                    assert inc(ins, "V", 1) == vmq2[lc]
                for cp in range(NCP):
                    for lt in range(4):
                        bank = lt % 2
                        w(ve, "T", tk[cp][lt])
                        if cp == 0 and lt == 0:
                            w(ve, "T", tlq[NCP - 1][3])
                        ins = ve.tensor_scalar(
                            k_sb[:, cp * L + lt * 512:cp * L + (lt + 1) * 512],
                            psK[:, bank * 512:(bank + 1) * 512],
                            bqk[:, cp * 2 + 1:cp * 2 + 2], None, ALU.add)
                        assert inc(ins, "V", 1) == vke[cp][lt]
                    for lt in range(4):
                        w(ve, "V", vke[cp][lt])
                        if cp > 0:
                            w(ve, "T", tkk[cp - 1][lt])
                        elif lt == 0:
                            w(ve, "T", tqq[NCP - 1][3])
                        ins = ve.tensor_tensor(
                            sq_sb[:, lt * 512:(lt + 1) * 512],
                            k_sb[:, cp * L + lt * 512:cp * L + (lt + 1) * 512],
                            k_sb[:, cp * L + lt * 512:cp * L + (lt + 1) * 512],
                            ALU.mult)
                        assert inc(ins, "V", 1) == vsqk[cp][lt]

            @blk.scalar
            def _(se):
                w(se, "V", vqq)
                ins = se.activation(qqT_sb[:], qqT_sb[:], AF.Sqrt,
                                    bias=0.0, scale=-2.0 * 2.35 * 2.35)
                assert inc(ins, "S", 1) == sscl
                for cp in range(NCP):
                    for lt in range(4):
                        w(se, "T", tlq[cp][lt])
                        ins = se.activation(
                            eq_sb[:, cp * L + lt * 512:cp * L + (lt + 1) * 512],
                            psQ[:, lt * 512:(lt + 1) * 512],
                            AF.Exp, bias=0.0, scale=1.0)
                        assert inc(ins, "S", 1) == sexp[cp][lt]
        tap("eq", eq_sb[:], [128, NCP * L], "S")
        tap("ksb", k_sb[:], [128, NCP * L], "V")
        if stage == "p1b":
            return finish_ret()
        s_qs.close(); open_stacks.remove(s_qs)

        s_big.close(); open_stacks.remove(s_big)

        s_ee = ExitStack()
        open_stacks.append(s_ee)
        comb2 = s_ee.enter_context(nc.sbuf_tensor("comb2", [64, L], BF16))
        comblo = s_ee.enter_context(nc.sbuf_tensor("comblo", [16, L], BF16))
        e_sb = s_ee.enter_context(nc.sbuf_tensor("e_sb", [128, 2 * L], BF16))
        eT_sb = s_ee.enter_context(nc.sbuf_tensor("eT_sb", [128, 2 * L], BF16))
        psEb = None

        T2p, V2p = cnt["T"], cnt["V"]
        vkkt = V2p + 1
        kktr = [T2p + lc + 1 for lc in range(NLC)]
        vkkfm = [V2p + 1 + 2 * lc + 2 for lc in range(NLC)]
        vclo = V2p + 1 + 2 * NLC
        assert vclo == vkkfm[-1]
        if (blk := mblk) or True:
            @blk.vector
            def _(ve):
                w(ve, "T", tkk[NCP - 1][3])
                w(ve, "T", tmq[NLC - 1])
                ve.memset(comb2[:], 0.0)
                assert inc(ve.tensor_copy(kkT_sb[:], psKT[:, 0:256]), "V", 1) == vkkt

            @blk.tensor
            def _(te):
                w(te, "V", vkkt)
                w(te, "S", sexp[NCP - 1][3])
                for lc in range(NLC):
                    if lc >= 1:
                        w(te, "V", vkkfm[lc - 1])
                    ins = te.transpose(
                        psS[0:16, (lc % 2) * 128:(lc % 2) * 128 + 128],
                        kkT_sb[:, lc * 16:(lc + 1) * 16], identf[:])
                    assert inc(ins, "T", 1) == kktr[lc]

            @blk.vector
            def _(ve):
                for lc in range(NLC):
                    w(ve, "T", kktr[lc])
                    ins = ve.tensor_copy(
                        comb2[0:16, lc * 128:(lc + 1) * 128],
                        psS[0:16, (lc % 2) * 128:(lc % 2) * 128 + 128])
                    inc(ins, "V", 1)
                    w(ve, "V", cnt["V"])
                    ins = ve.scalar_tensor_tensor(
                        comblo[:, lc * 128:(lc + 1) * 128],
                        comb2[0:16, lc * 128:(lc + 1) * 128], -1.0,
                        psS[0:16, (lc % 2) * 128:(lc % 2) * 128 + 128],
                        ALU.mult, ALU.add)
                    assert inc(ins, "V", 1) == vkkfm[lc]
                assert vclo == cnt["V"] + 0 or True
        tap("kkfm", comb2[0:16, :], [16, L], "V", None)
        if (blk := mblk) or True:
            @blk.sync
            def _(sy):
                sy.wait_ge(sem["V"], vclo)
                inc(sy.dma_start(comb2[32:48, :], comblo[:]), "dC", 16)
        s_wt = ExitStack()
        open_stacks.append(s_wt)
        wbase = s_wt.enter_context(nc.sbuf_tensor("wbase", [128, NDC * D], BF16, side="right"))
        cball = s_wt.enter_context(nc.sbuf_tensor("cball", [8, D], BF16, side="right"))
        borow = s_wt.enter_context(nc.sbuf_tensor("borow", [1, D], BF16, side="right"))
        s_wt2 = ExitStack()
        open_stacks.append(s_wt2)
        wo = s_wt2.enter_context(nc.sbuf_tensor("wo", [128, NDC * D], BF16, side="right"))
        if (blk := mblk) or True:
            @blk.gpsimd
            def _(g):
                g.wait_ge(sem["T"], tkk[NCP - 1][3])
                inc(g.dma_start(
                    wo[:].rearrange("p (c n) -> p c n", n=D),
                    d_in["woT"].rearrange("(c p) n -> p c n", p=128)), "dW", 16)
                inc(g.dma_start(
                    wbase[:].rearrange("p (c n) -> p c n", n=D),
                    d_in["baseT"].rearrange("(c p) n -> p c n", p=128)), "dW", 16)
                inc(g.dma_start(cball[:], d_in["cball"]), "dW", 16)
                inc(g.dma_start(borow[:], d_in["borow"]), "dW", 16)
        wo_dL = cnt["dW"] - 48
        wbase_dL = cnt["dW"]
        s_kt.close(); open_stacks.remove(s_kt)
        s_aps2 = ExitStack()
        open_stacks.append(s_aps2)
        psEb = s_aps2.enter_context(nc.psum_tensor("psEb", [128, 512], BF16))

        if (blk := mblk) or True:
            @blk.vector
            def _(ve):
                w(ve, "V", cnt["V"])
                for c in range(NCP):
                    ve.memset(numS[c][:], 0.0)
                    ve.memset(denS[c][:], 0.0)
                inc(ve.nop(), "V", 1)
        memset_V = cnt["V"]

        _pt, _pv, _ps = cnt["T"], cnt["V"], cnt["S"]
        tLh = [[0, 0] for _ in range(NCP)]
        ttr2 = [0] * NCP
        tbias2 = [0] * NCP
        tetr = [[0] * 4 for _ in range(NCP)]
        tN = [0] * NCP
        vcm2 = [0] * NCP
        vscr2 = [0] * NCP
        vmk = [0] * NCP
        vmb2 = [0] * NCP
        vet = [[0] * 4 for _ in range(NCP)]
        vnum = [0] * NCP
        sexp2h = [[0, 0] for _ in range(NCP)]
        for i in range(NCP):
            tLh[i][0] = _pt + 1; tLh[i][1] = _pt + 2
            _pt += 2
            if i >= 1:
                tetr[i - 1][0] = _pt + 1
                ttr2[i] = _pt + 2
                tetr[i - 1][1] = _pt + 3
                tetr[i - 1][2] = _pt + 4
                tbias2[i] = _pt + 5
                tetr[i - 1][3] = _pt + 6
                tN[i - 1] = _pt + 7
                _pt += 7
            else:
                ttr2[i] = _pt + 1; tbias2[i] = _pt + 2
                _pt += 2
            vcm2[i] = _pv + 1
            if i >= 1:
                vet[i - 1][0] = _pv + 2
                vscr2[i] = _pv + 3
                vet[i - 1][1] = _pv + 4
                vmk[i] = _pv + 5
                vet[i - 1][2] = _pv + 6
                vmb2[i] = _pv + 7
                vet[i - 1][3] = _pv + 8
                vnum[i - 1] = _pv + 9
                _pv += 9
            else:
                vscr2[i] = _pv + 2; vmk[i] = _pv + 3; vmb2[i] = _pv + 4
                _pv += 4
            sexp2h[i][0] = _ps + 1; sexp2h[i][1] = _ps + 2
            _ps += 2
        for g in range(4):
            tetr[NCP - 1][g] = _pt + 1 + g
            vet[NCP - 1][g] = _pv + 1 + g
        tN[NCP - 1] = _pt + 5
        vnum[NCP - 1] = _pv + 5

        def _emit_tr_g(te, cp, g4):
            if cp > 0 or g4 > 0:
                pc, pg = (cp, g4 - 1) if g4 > 0 else (cp - 1, 3)
                w(te, "V", vet[pc][pg])
            w(te, "S", sexp2h[cp][g4 // 2])
            for j in range(4):
                lc = g4 * 4 + j
                ins = te.transpose(
                    psEb[:, j * 128:(j + 1) * 128],
                    e_sb[:, (cp % 2) * L + lc * 128:(cp % 2) * L + (lc + 1) * 128],
                    identb[:])
            assert inc(ins, "T", 1) == tetr[cp][g4]

        def _emit_nh(te, cp):
            if cp == 0:
                w(te, "V", memset_V)
            if cp > 0:
                w(te, "V", vnum[cp - 1])
            for lc in range(NLC):
                g4 = lc // 4
                w(te, "V", vet[cp][g4])
                ins = te.matmul(
                    psK[:, 512:642],
                    eT_sb[:, (cp % 2) * L + lc * 128:(cp % 2) * L + (lc + 1) * 128],
                    v_sb[:, lc * 1040 + cp * 130:lc * 1040 + cp * 130 + 130],
                    start=(lc == 0), stop=(lc == NLC - 1))
            assert inc(ins, "T", 1) == tN[cp]

        if (blk := mblk) or True:
            @blk.tensor
            def _(te):
                for cp in range(NCP):
                    w(te, "V", vclo)
                    if cp == 0:
                        w(te, "dC", 16)
                    for h in range(2):
                        if cp > 0:
                            w(te, "S", sexp2h[cp - 1][h])
                        for lt in (2 * h, 2 * h + 1):
                            te.matmul(psQ[:, lt * 512:(lt + 1) * 512],
                                      w0blk[:, cp * 128:(cp + 1) * 128],
                                      k_sb[:, cp * L + lt * 512:cp * L + (lt + 1) * 512],
                                      start=True, stop=False)
                            ins = te.matmul(psQ[:, lt * 512:(lt + 1) * 512],
                                            dsel64[:, cp * 128:(cp + 1) * 128],
                                            comb2[:, lt * 512:(lt + 1) * 512],
                                            start=False, stop=True)
                        assert inc(ins, "T", 1) == tLh[cp][h]
                    if cp >= 1:
                        _emit_tr_g(te, cp - 1, 0)
                    w(te, "V", vcm2[cp])
                    if cp > 0:
                        w(te, "V", vmb2[cp - 1])
                    ins = te.transpose(psS[0:1, 0:128], colmax[:], identf[:])
                    assert inc(ins, "T", 1) == ttr2[cp]
                    if cp >= 1:
                        _emit_tr_g(te, cp - 1, 1)
                        _emit_tr_g(te, cp - 1, 2)
                    w(te, "V", vmk[cp])
                    te.matmul(psS[:, 128:129], parab[:, 0:128],
                              mkA[:, cp:cp + 1], start=True, stop=False)
                    ins = te.matmul(psS[:, 128:129], parab[:, 128:256],
                                    mkB[:, cp:cp + 1], start=False, stop=True)
                    assert inc(ins, "T", 1) == tbias2[cp]
                    if cp >= 1:
                        _emit_tr_g(te, cp - 1, 3)
                        _emit_nh(te, cp - 1)
                for g in range(4):
                    _emit_tr_g(te, NCP - 1, g)
                _emit_nh(te, NCP - 1)

            def _emit_ev_g(ve, pc, g4):
                w(ve, "T", tetr[pc][g4])
                if pc >= 2:
                    w(ve, "T", tN[pc - 2])
                ins = ve.tensor_copy(
                    eT_sb[:, (pc % 2) * L + g4 * 512:(pc % 2) * L + (g4 + 1) * 512],
                    psEb[:, 0:512])
                assert inc(ins, "V", 1) == vet[pc][g4]

            def _emit_numev(ve, pc):
                w(ve, "T", tN[pc])
                ve.tensor_copy(numS[pc][0:64, 0:64], psK[0:64, 512:576])
                ve.tensor_copy(numS[pc][64:128, 64:128], psK[64:128, 577:641])
                ve.tensor_copy(d16[0:64, pc:pc + 1], psK[0:64, 576:577])
                ins = ve.tensor_copy(d16[64:128, pc:pc + 1], psK[64:128, 641:642])
                assert inc(ins, "V", 1) == vnum[pc]

            @blk.vector
            def _(ve):
                for cp in range(NCP):
                    w(ve, "T", tLh[cp][0])
                    ins = ve.reduce_max(colmax[:], psQ[:, 0:1024:4], axis=AX.X)
                    assert inc(ins, "V", 1) == vcm2[cp]
                    if cp >= 1:
                        _emit_ev_g(ve, cp - 1, 0)
                    w(ve, "T", ttr2[cp])
                    ins = ve.tensor_copy(scr3[:], psS[0:1, 0:128])
                    assert inc(ins, "V", 1) == vscr2[cp]
                    if cp >= 1:
                        _emit_ev_g(ve, cp - 1, 1)
                    w(ve, "V", vscr2[cp])
                    ve.reduce_max(mkA[:, cp:cp + 1], scr3[:, 0:64],
                                  axis=AX.X, negate=True)
                    ins = ve.reduce_max(mkB[:, cp:cp + 1], scr3[:, 64:128],
                                        axis=AX.X, negate=True)
                    assert inc(ins, "V", 1) == vmk[cp]
                    if cp >= 1:
                        _emit_ev_g(ve, cp - 1, 2)
                    w(ve, "T", tbias2[cp])
                    if cp > 0:
                        w(ve, "S", sexp2h[cp - 1][1])
                    ins = ve.tensor_copy(mbias[:], psS[:, 128:129])
                    assert inc(ins, "V", 1) == vmb2[cp]
                    if cp >= 1:
                        _emit_ev_g(ve, cp - 1, 3)
                        _emit_numev(ve, cp - 1)
                for g in range(4):
                    _emit_ev_g(ve, NCP - 1, g)
                _emit_numev(ve, NCP - 1)

            @blk.scalar
            def _(se):
                for cp in range(NCP):
                    for h in range(2):
                        w(se, "T", tLh[cp][h])
                        w(se, "V", vmb2[cp])
                        if cp >= 2:
                            w(se, "T", tetr[cp - 2][3])
                        ins = se.activation(
                            e_sb[:, (cp % 2) * L + h * 1024:(cp % 2) * L + (h + 1) * 1024],
                            psQ[:, h * 1024:(h + 1) * 1024],
                            AF.Exp, bias=mbias[:], scale=1.0)
                        assert inc(ins, "S", 1) == sexp2h[cp][h]
        assert cnt["T"] == _pt + 5 and cnt["V"] == _pv + 5 and cnt["S"] == _ps
        sexp2 = [sexp2h[c][1] for c in range(NCP)]
        tap("d16", d16[:], [128, 8], "V")
        tap("e", e_sb[:], [128, 2 * L], "S")
        if stage == "p2":
            return finish_ret()
        s_ee.close(); open_stacks.remove(s_ee)

        TZ, VZ = cnt["T"], cnt["V"]
        tzd = TZ + 1
        tzab = TZ + 2
        tze8 = TZ + 3
        vzd8 = VZ + 1
        vzcols = VZ + 2
        vzrows = VZ + 3
        vzeb8 = VZ + 4
        vzeall = VZ + 5

        if (blk := mblk) or True:
            @blk.tensor
            def _(te):
                w(te, "V", vnum[NCP - 1])
                w(te, "S", sexp2[NCP - 1])
                ins = te.transpose(psS[0:8, 0:128], d16[:], identf[:])
                assert inc(ins, "T", 1) == tzd
                w(te, "V", vzcols)
                te.transpose(psS[0:1, 128:136], dcolA[:], identf[0:8, 0:8])
                ins = te.transpose(psS[0:1, 136:144], dcolB[:], identf[0:8, 0:8])
                assert inc(ins, "T", 1) == tzab
                w(te, "V", vzrows)
                te.matmul(psS[:, 144:152], parab[:, 0:128],
                          zArow[:, 0:8], start=True, stop=False)
                ins = te.matmul(psS[:, 144:152], parab[:, 128:256],
                                zBrow[:, 0:8], start=False, stop=True)
                assert inc(ins, "T", 1) == tze8

            @blk.vector
            def _(ve):
                w(ve, "T", tzd)
                ins = ve.tensor_copy(zd8[:], psS[0:8, 0:128])
                assert inc(ins, "V", 1) == vzd8
                w(ve, "V", vzd8)
                ve.reduce_sum(dcolA[:], zd8[:, 0:64], axis=AX.X)
                ins = ve.reduce_sum(dcolB[:], zd8[:, 64:128], axis=AX.X)
                assert inc(ins, "V", 1) == vzcols
                w(ve, "T", tzab)
                ve.tensor_scalar(zArow[:], psS[0:1, 128:136], EPS_ATT, None, ALU.mult)
                ins = ve.tensor_scalar(zBrow[:], psS[0:1, 136:144], EPS_ATT, None, ALU.mult)
                assert inc(ins, "V", 1) == vzrows
                w(ve, "T", tze8)
                ins = ve.tensor_copy(ebias[:], psS[:, 144:152])
                assert inc(ins, "V", 1) == vzeb8
                w(ve, "V", vzeb8)
                for cp in range(NCP):
                    ve.tensor_tensor(denS[cp][0:64, 2 * cp:2 * cp + 1],
                                     d16[0:64, cp:cp + 1], ebias[0:64, cp:cp + 1],
                                     ALU.add)
                    ins = ve.tensor_tensor(denS[cp][64:128, 2 * cp + 1:2 * cp + 2],
                                           d16[64:128, cp:cp + 1],
                                           ebias[64:128, cp:cp + 1], ALU.add)
                assert inc(ins, "V", 1) == vzeall
        denS_V = cnt["V"]

        s_att1.close(); open_stacks.remove(s_att1)
        s_rp = ExitStack()
        open_stacks.append(s_rp)
        recipf = s_rp.enter_context(nc.sbuf_tensor("recipf", [16, L], F32))
        recip2 = s_rp.enter_context(nc.sbuf_tensor("recip2", [64, L], BF16))
        reciplo16 = s_rp.enter_context(nc.sbuf_tensor("reciplo16", [16, L], BF16))

        TB, VB = cnt["T"], cnt["V"]
        tden = [TB + lt + 1 for lt in range(4)]
        vrecf = [VB + 3 * lt + 1 for lt in range(4)]
        vrech = [VB + 3 * lt + 2 for lt in range(4)]
        vrlo = [VB + 3 * lt + 3 for lt in range(4)]
        if (blk := mblk) or True:
            @blk.tensor
            def _(te):
                w(te, "V", denS_V)
                for lt in range(4):
                    for cp in range(NCP):
                        ins = te.matmul(
                            psQ[0:16, lt * 512:(lt + 1) * 512],
                            denS[cp][:],
                            eq_sb[:, cp * L + lt * 512:cp * L + (lt + 1) * 512],
                            start=(cp == 0), stop=(cp == NCP - 1))
                    assert inc(ins, "T", 1) == tden[lt]

            @blk.vector
            def _(ve):
                ve.memset(recip2[:], 0.0)
                for lt in range(4):
                    sl = slice(lt * 512, (lt + 1) * 512)
                    w(ve, "T", tden[lt])
                    ins = ve.reciprocal(recipf[:, sl], psQ[0:16, sl])
                    assert inc(ins, "V", 1) == vrecf[lt]
                    w(ve, "V", vrecf[lt])
                    ins = ve.tensor_copy(recip2[0:16, sl], recipf[:, sl])
                    assert inc(ins, "V", 1) == vrech[lt]
                    w(ve, "V", vrech[lt])
                    ins = ve.scalar_tensor_tensor(
                        reciplo16[:, sl], recip2[0:16, sl], -1.0, recipf[:, sl],
                        ALU.mult, ALU.add)
                    assert inc(ins, "V", 1) == vrlo[lt]

            @blk.sync
            def _(sy):
                for lt in range(4):
                    sl = slice(lt * 512, (lt + 1) * 512)
                    sy.wait_ge(sem["V"], vrlo[lt])
                    inc(sy.dma_start(recip2[32:48, sl], reciplo16[:, sl]),
                        ("dRa", "dRb", "dRc", "dRd")[lt], 16)
        tap("recipf", recipf[:], [16, L], "V")
        tap("densall", denS[0][:], [128, 16], "V")
        if stage == "pz":
            return finish_ret()
        s_aps2.close(); open_stacks.remove(s_aps2)
        s_apq.close(); open_stacks.remove(s_apq)
        s_aps.close(); open_stacks.remove(s_aps)

        s_ctx = ExitStack()
        open_stacks.append(s_ctx)
        ctx_sb = s_ctx.enter_context(
            nc.sbuf_tensor("ctx_sb", [128, NCP * L], BF16, side="right"))
        s_b2 = ExitStack()
        open_stacks.append(s_b2)
        psBC = s_b2.enter_context(nc.psum_tensor("psBC", [128, L], F32))
        psNum = s_b2.enter_context(nc.psum_tensor("psNum", [128, L], F32))
        phi_sb = s_b2.enter_context(nc.sbuf_tensor("phi_sb", [128, L], BF16))
        TB2, VB2, SB2 = cnt["T"], cnt["V"], cnt["S"]
        tbc = [[TB2 + 4 * c + h + 1 for h in range(2)] for c in range(NCP)]
        tnum = [[TB2 + 4 * c + 2 + h + 1 for h in range(2)] for c in range(NCP)]
        vphi = [[VB2 + 2 * c + h + 1 for h in range(2)] for c in range(NCP)]
        sctx = [[SB2 + 2 * c + h + 1 for h in range(2)] for c in range(NCP)]
        if (blk := mblk) or True:
            @blk.tensor
            def _(te):
                for cp in range(NCP):
                    for h in range(2):
                        if cp == 0:
                            w(te, ("dRa", "dRc")[h], 16)
                            w(te, ("dRb", "dRd")[h], 16)
                            if h == 0:
                                w(te, "dC", 16)
                        if cp >= 1:
                            w(te, "V", vphi[cp - 1][h])
                        for lt in (2 * h, 2 * h + 1):
                            ins = te.matmul(psBC[:, lt * 512:(lt + 1) * 512],
                                            dsel64[:, cp * 128:(cp + 1) * 128],
                                            recip2[:, lt * 512:(lt + 1) * 512],
                                            start=True, stop=True)
                        assert inc(ins, "T", 1) == tbc[cp][h]
                    for h in range(2):
                        w(te, "V", vphi[cp][h])
                        if cp >= 1:
                            w(te, "S", sctx[cp - 1][h])
                        for lt in (2 * h, 2 * h + 1):
                            ins = te.matmul(psNum[:, lt * 512:(lt + 1) * 512],
                                            numS[cp][:],
                                            phi_sb[:, lt * 512:(lt + 1) * 512],
                                            start=True, stop=True)
                        assert inc(ins, "T", 1) == tnum[cp][h]

            @blk.vector
            def _(ve):
                for cp in range(NCP):
                    for h in range(2):
                        w(ve, "T", tbc[cp][h])
                        if cp >= 1:
                            w(ve, "T", tnum[cp - 1][h])
                        for lt in (2 * h, 2 * h + 1):
                            ins = ve.tensor_tensor(
                                phi_sb[:, lt * 512:(lt + 1) * 512],
                                psBC[:, lt * 512:(lt + 1) * 512],
                                eq_sb[:, cp * L + lt * 512:cp * L + (lt + 1) * 512],
                                ALU.mult)
                        assert inc(ins, "V", 1) == vphi[cp][h]

            @blk.scalar
            def _(se):
                for cp in range(NCP):
                    for h in range(2):
                        w(se, "T", tnum[cp][h])
                        for lt in (2 * h, 2 * h + 1):
                            ins = se.activation(
                                ctx_sb[:, cp * L + lt * 512:cp * L + (lt + 1) * 512],
                                psNum[:, lt * 512:(lt + 1) * 512],
                                AF.Copy, bias=0.0, scale=1.0)
                        assert inc(ins, "S", 1) == sctx[cp][h]
        ctx_S = cnt["S"]
        assert ctx_S == sctx[NCP - 1][1]
        tap("ctx", ctx_sb[:], [128, NCP * L], "S")
        if stage == "pb2":
            return finish_ret()
        s_b2.close(); open_stacks.remove(s_b2)
        s_rp.close(); open_stacks.remove(s_rp)
        s_att.close(); open_stacks.remove(s_att)

        s1bf = SB("s1bf", [128, NLC * D], BF16)
        s1fm = SB("s1fm", [128, NDC * L], BF16)
        stat = SB("stat", [128, 24], F32)
        scrap = SB("scrap", [128, L], BF16)
        s2t = SB("s2t", [128, 2 * D], F32)
        m1q = SB("m1q", [128, 32], F32)
        m2q = SB("m2q", [128, 32], F32)
        m2scr = SB("m2scr", [128, 1024], BF16)
        s_wo = ExitStack()
        open_stacks.append(s_wo)
        xr = s_wo.enter_context(nc.sbuf_tensor("xr", [128, 2 * D], F32))
        u1buf = s_wo.enter_context(nc.sbuf_tensor("u1buf", [128, 4 * D], F32))
        psO = s_wo.enter_context(nc.psum_tensor("psO", [128, 2 * D], F32))
        psT = s_wo.enter_context(nc.psum_tensor("psT", [128, 2048], BF16))

        TW, VW, SW, PW = cnt["T"], cnt["V"], cnt["S"], cnt["P"]
        def _msched():
            sched = [[] for _ in range(NLC)]
            pend = []
            for lc in range(NLC):
                if lc >= 6 and (lc - 6) % 4 == 0 and (lc - 6) // 4 < 4:
                    pend.extend(((lc - 6) // 4, dc) for dc in range(NDC))
                for _ in range(3):
                    if pend:
                        sched[lc].append(pend.pop(0))
            pend.extend((3, dc) for dc in range(NDC))
            return sched, pend
        m1sched, m1tail = _msched()
        m2sched, m2tail = _msched()
        te_ops = []
        for lc in range(NLC):
            te_ops.append(("mm", lc))
            if lc >= 3:
                te_ops.append(("tr", lc - 3))
        te_ops += [("tr", 13), ("tr", 14), ("tr", 15)]
        ve_ops = []
        for lc in range(NLC):
            ve_ops.append(("vu", lc))
            if lc >= 1:
                ve_ops += [("vms", lc - 1), ("vm2", lc - 1), ("vst", lc - 1)]
            if lc >= 2:
                ve_ops += [("vrc", lc - 2), ("vnm", lc - 2)]
            if lc >= 3:
                ve_ops.append(("vfev", lc - 3))
            ve_ops += [("m1", q, dc) for q, dc in m1sched[lc]]
        ve_ops += [("vms", 15), ("vm2", 15), ("vst", 15),
                   ("vrc", 14), ("vnm", 14), ("vfev", 13),
                   ("vrc", 15), ("vnm", 15), ("vfev", 14), ("vfev", 15)]
        ve_ops += [("m1", q, dc) for q, dc in m1tail]
        se_ops = []
        for lc in range(NLC):
            se_ops.append(("ssq", lc))
            if lc >= 1:
                se_ops.append(("ssd", lc - 1))
            if lc >= 2:
                se_ops.append(("sapply", lc - 2))
            se_ops += [("m2", q, dc) for q, dc in m2sched[lc]]
        se_ops += [("ssd", 15), ("sapply", 14), ("sapply", 15)]
        se_ops += [("m2", q, dc) for q, dc in m2tail]
        tmk = {op: TW + i + 1 for i, op in enumerate(te_ops)}
        vmk_ = {op: VW + i + 1 for i, op in enumerate(ve_ops)}
        smk = {op: SW + i + 1 for i, op in enumerate(se_ops)}
        two = [tmk[("mm", lc)] for lc in range(NLC)]
        tftr = [tmk[("tr", lc)] for lc in range(NLC)]
        vu = [vmk_[("vu", lc)] for lc in range(NLC)]
        vms = [vmk_[("vms", lc)] for lc in range(NLC)]
        vm2 = [vmk_[("vm2", lc)] for lc in range(NLC)]
        vst = [vmk_[("vst", lc)] for lc in range(NLC)]
        vrc = [vmk_[("vrc", lc)] for lc in range(NLC)]
        vnm = [vmk_[("vnm", lc)] for lc in range(NLC)]
        vfev = [vmk_[("vfev", lc)] for lc in range(NLC)]
        vm1_all = VW + len(ve_ops)
        ssq = [smk[("ssq", lc)] for lc in range(NLC)]
        ssd = [smk[("ssd", lc)] for lc in range(NLC)]
        sapply = [smk[("sapply", lc)] for lc in range(NLC)]
        sm2_all = SW + len(se_ops)
        xr_dL = [16 * (lc // 2 + 1) for lc in range(NLC)]
        s1fm3 = s1fm[:].rearrange("p (c l) -> p c l", l=L)
        if (blk := mblk) or True:
            @blk.sync
            def _(sy):
                sy.wait_ge(sem["S"], ctx_S)
                for lc in range(NLC):
                    if lc >= 2:
                        sy.wait_ge(sem["V"], vu[lc - 2])
                    inc(sy.dma_start(xr[:, (lc % 2) * D:(lc % 2 + 1) * D], x_t[lc]),
                        "dXa" if lc % 2 == 0 else "dXb", 16)

            @blk.tensor
            def _(te):
                w(te, "dW", wbase_dL)
                w(te, "S", ctx_S)
                for op in te_ops:
                    kind, lc = op
                    if kind == "mm":
                        if lc >= 2:
                            w(te, "V", vu[lc - 2])
                        for ot in range(2):
                            for c8 in range(NCP):
                                te.matmul(
                                    psO[:, (lc % 2) * D + ot * 512:(lc % 2) * D + (ot + 1) * 512],
                                    ctx_sb[:, c8 * L + lc * 128:c8 * L + (lc + 1) * 128],
                                    wo[:, c8 * D + ot * 512:c8 * D + (ot + 1) * 512],
                                    start=(c8 == 0), stop=False)
                            ins = te.matmul(
                                psO[:, (lc % 2) * D + ot * 512:(lc % 2) * D + (ot + 1) * 512],
                                onesrow[:], borow[:, ot * 512:(ot + 1) * 512],
                                start=False, stop=True)
                        assert inc(ins, "T", 1) == two[lc]
                    else:
                        w(te, "S", sapply[lc])
                        if lc >= 2:
                            w(te, "V", vfev[lc - 2])
                        for dc in range(NDC):
                            ins = te.transpose(
                                psT[:, (lc % 2) * 1024 + dc * 128:(lc % 2) * 1024 + (dc + 1) * 128],
                                s1bf[:, lc * D + dc * 128:lc * D + (dc + 1) * 128],
                                identb[:])
                        assert inc(ins, "T", 1) == tftr[lc]

            @blk.vector
            def _(ve):
                for op in ve_ops:
                    kind, a = op[0], op[1]
                    p2 = a % 2
                    if kind == "vu":
                        w(ve, "T", two[a])
                        w(ve, "dXa" if a % 2 == 0 else "dXb", xr_dL[a])
                        if a >= 2:
                            w(ve, "V", vms[a - 2])
                        if a >= 4:
                            w(ve, "S", sapply[a - 4])
                        ins = ve.scalar_tensor_tensor(
                            u1buf[:, (a % 4) * D:(a % 4 + 1) * D],
                            psO[:, p2 * D:(p2 + 1) * D], 1.0,
                            xr[:, p2 * D:(p2 + 1) * D],
                            ALU.mult, ALU.add, accum_out=stat[:, p2:p2 + 1])
                        assert inc(ins, "V", 1) == vu[a]
                    elif kind == "vms":
                        w(ve, "S", ssq[a])
                        if a >= 2:
                            w(ve, "V", vnm[a - 2])
                        ve.tensor_scalar(stat[:, 4 + p2:5 + p2], stat[:, p2:p2 + 1],
                                         1.0 / D, None, ALU.mult)
                        ins = ve.tensor_scalar(stat[:, 6 + p2:7 + p2],
                                               stat[:, 2 + p2:3 + p2],
                                               1.0 / D, None, ALU.mult)
                        assert inc(ins, "V", 1) == vms[a]
                    elif kind == "vm2":
                        w(ve, "V", vms[a])
                        ins = ve.tensor_tensor(stat[:, 8 + p2:9 + p2],
                                               stat[:, 4 + p2:5 + p2],
                                               stat[:, 4 + p2:5 + p2], ALU.mult)
                        assert inc(ins, "V", 1) == vm2[a]
                    elif kind == "vst":
                        w(ve, "V", vm2[a])
                        ins = ve.scalar_tensor_tensor(stat[:, 10 + p2:11 + p2],
                                                      stat[:, 6 + p2:7 + p2], EPS_LN,
                                                      stat[:, 8 + p2:9 + p2],
                                                      ALU.add, ALU.subtract)
                        assert inc(ins, "V", 1) == vst[a]
                    elif kind == "vrc":
                        w(ve, "S", ssd[a])
                        if a >= 4:
                            w(ve, "S", sapply[a - 4])
                        ins = ve.reciprocal(stat[:, 14 + a % 4:15 + a % 4],
                                            stat[:, 12 + p2:13 + p2])
                        assert inc(ins, "V", 1) == vrc[a]
                    elif kind == "vnm":
                        w(ve, "V", vrc[a])
                        ins = ve.scalar_tensor_tensor(
                            stat[:, 18 + a % 4:19 + a % 4],
                            stat[:, 4 + p2:5 + p2], -1.0,
                            stat[:, 14 + a % 4:15 + a % 4], ALU.mult, ALU.mult)
                        assert inc(ins, "V", 1) == vnm[a]
                    elif kind == "vfev":
                        w(ve, "T", tftr[a])
                        ins = ve.tensor_copy(
                            s1fm3[:, :, a * 128:(a + 1) * 128],
                            psT[:, p2 * 1024:p2 * 1024 + 1024]
                            .rearrange("p (c k) -> p c k", k=128))
                        assert inc(ins, "V", 1) == vfev[a]
                    else:
                        q, dc = op[1], op[2]
                        w(ve, "V", vfev[4 * q + 3])
                        ins = ve.reduce_sum(
                            m1q[:, q * 8 + dc:q * 8 + dc + 1],
                            s1fm[:, dc * L + q * 512:dc * L + (q + 1) * 512],
                            axis=AX.X)
                        inc(ins, "V", 1)
                assert cnt["V"] == vm1_all

            @blk.scalar
            def _(se):
                for op in se_ops:
                    kind, a = op[0], op[1]
                    p2 = a % 2
                    if kind == "ssq":
                        w(se, "V", vu[a])
                        if a >= 1:
                            w(se, "S", ssq[a - 1])
                        ins = se.activation(scrap[:, 0:D],
                                            u1buf[:, (a % 4) * D:(a % 4 + 1) * D],
                                            AF.Square,
                                            accum_out=stat[:, 2 + p2:3 + p2])
                        assert inc(ins, "S", 1) == ssq[a]
                    elif kind == "ssd":
                        w(se, "V", vst[a])
                        ins = se.activation(stat[:, 12 + p2:13 + p2],
                                            stat[:, 10 + p2:11 + p2],
                                            AF.Sqrt, bias=0.0, scale=1.0)
                        assert inc(ins, "S", 1) == ssd[a]
                    elif kind == "sapply":
                        w(se, "V", vnm[a])
                        ins = se.activation(s1bf[:, a * D:(a + 1) * D],
                                            u1buf[:, (a % 4) * D:(a % 4 + 1) * D],
                                            AF.Identity,
                                            bias=stat[:, 18 + a % 4:19 + a % 4],
                                            scale=stat[:, 14 + a % 4:15 + a % 4])
                        assert inc(ins, "S", 1) == sapply[a]
                    else:
                        q, dc = op[1], op[2]
                        w(se, "S", cnt["S"])
                        w(se, "V", vfev[4 * q + 3])
                        ins = se.activation(
                            m2scr[:, (dc % 2) * 512:(dc % 2) * 512 + 512],
                            s1fm[:, dc * L + q * 512:dc * L + (q + 1) * 512],
                            AF.Square, accum_out=m2q[:, q * 8 + dc:q * 8 + dc + 1])
                        inc(ins, "S", 1)
                assert cnt["S"] == sm2_all
        tap("s1", s1bf[:], [128, NLC * D], "S")
        tap("s1fm", s1fm[:], [128, NDC * L], "V")
        if stage in ("pwo", "ft"):
            return finish_ret()
        s_wo.close(); open_stacks.remove(s_wo)
        s_ctx.close(); open_stacks.remove(s_ctx)
        s_wt2.close(); open_stacks.remove(s_wt2)

        s_fn = ExitStack()
        open_stacks.append(s_fn)
        psH = s_fn.enter_context(nc.psum_tensor("psH", [64, 1024], F32))
        psA2 = s_fn.enter_context(nc.psum_tensor("psA2", [128, 1024], F32))
        psO2 = s_fn.enter_context(nc.psum_tensor("psO2", [128, 2 * D], F32))
        psF1 = psA2

        if (blk := mblk) or True:
            @blk.tensor
            def _(te):
                w(te, "V", vfev[NLC - 1])
                for lc in range(2):
                    for ot in range(2):
                        for dc in range(NDC):
                            ins = te.matmul(
                                psO2[:, lc * D + ot * 512:lc * D + (ot + 1) * 512],
                                s1fm[:, dc * L + lc * 128:dc * L + (lc + 1) * 128],
                                wbase[:, dc * D + ot * 512:dc * D + (ot + 1) * 512],
                                start=(dc == 0), stop=False)
                inc(ins, "T", 1)
        base_pre_T = cnt["T"]


        convsel = SB("convsel", [128, NDC * D], BF16)
        biasrow = SB("biasrow", [1, D], BF16)
        dwsel = SB("dwsel", [128, NDC * R], BF16)
        upsel = SB("upsel", [64, D], BF16)
        h_sb = SB("h_sb", [64, L], BF16)
        a_sb = SB("a_sb", [128, NDC * L], BF16)

        fl_dH = consts_dH

        if (blk := mblk) or True:
            @blk.vector
            def _(ve):
                w(ve, "S", sm2_all)
                w(ve, "V", vm1_all)
                inc(ve.tensor_tensor(m1c[:], m1q[:, 0:8], m1q[:, 8:16], ALU.add),
                    "V", 1)
                inc(ve.tensor_tensor(m2c[:], m2q[:, 0:8], m2q[:, 8:16], ALU.add),
                    "V", 1)
                w(ve, "V", cnt["V"])
                inc(ve.tensor_tensor(m1c[:], m1c[:], m1q[:, 16:24], ALU.add), "V", 1)
                inc(ve.tensor_tensor(m2c[:], m2c[:], m2q[:, 16:24], ALU.add), "V", 1)
                w(ve, "V", cnt["V"])
                inc(ve.tensor_tensor(m1c[:], m1c[:], m1q[:, 24:32], ALU.add), "V", 1)
                inc(ve.tensor_tensor(m2c[:], m2c[:], m2q[:, 24:32], ALU.add), "V", 1)
        mom_V, mom_S = cnt["V"], cnt["S"]

        V3, T3 = cnt["V"], cnt["T"]
        tfl = [T3 + dc + 1 for dc in range(NDC)]
        if (blk := mblk) or True:
            @blk.vector
            def _(ve):
                w(ve, "dH", fl_dH)
                w(ve, "dL", cnt["dL"])
                inc(ve.memset(momdc8[:].rearrange("p (c k) -> p c k", k=5)[:, :, 4:5],
                              float(L)), "V", 1)
                ins = ve.tensor_tensor(sctA[:], sigT[:], sigT[:], ALU.mult)
                inc(ins, "V", 1)
                w(ve, "V", cnt["V"])
                inc(ve.reciprocal(ivA[:], sctA[:]), "V", 1)
                w(ve, "V", cnt["V"])
                inc(ve.tensor_tensor(mivA[:], muT[:], ivA[:], ALU.mult), "V", 1)
                w(ve, "V", cnt["V"])
                inc(ve.tensor_tensor(m2ivA[:], muT[:], mivA[:], ALU.mult), "V", 1)
                w(ve, "V", cnt["V"])
                dd = ddisc8[:].rearrange("p (c k) -> p c k", k=48)
                for j, srcA in enumerate((ivA, mivA, m2ivA)):
                    s8 = srcA[:].rearrange("p (c k) -> p c k", k=8)
                    ins = ve.tensor_copy(dd[:, :, 8 * j:8 * j + 8], s8)
                inc(ins, "V", 1)
                w(ve, "V", cnt["V"])
                for j, srcA in enumerate((ivA, mivA, m2ivA)):
                    s8 = srcA[:].rearrange("p (c k) -> p c k", k=8)
                    ins = ve.scalar_tensor_tensor(
                        dd[:, :, 24 + 8 * j:32 + 8 * j], dd[:, :, 8 * j:8 * j + 8],
                        -1.0, s8, ALU.mult, ALU.add)
                inc(ins, "V", 1)
                w(ve, "V", mom_V)
                md = momdc8[:].rearrange("p (c k) -> p c k", k=5)
                ve.tensor_copy(md[:, :, 0:1],
                               m1c[:].rearrange("p (c k) -> p c k", k=1))
                ins = ve.tensor_copy(md[:, :, 2:3],
                                     m2c[:].rearrange("p (c k) -> p c k", k=1))
                inc(ins, "V", 1)
                w(ve, "V", cnt["V"])
                ve.scalar_tensor_tensor(md[:, :, 1:2], md[:, :, 0:1], -1.0,
                                        m1c[:].rearrange("p (c k) -> p c k", k=1),
                                        ALU.mult, ALU.add)
                ins = ve.scalar_tensor_tensor(md[:, :, 3:4], md[:, :, 2:3], -1.0,
                                              m2c[:].rearrange("p (c k) -> p c k", k=1),
                                              ALU.mult, ALU.add)
                inc(ins, "V", 1)
        fl_build_V = cnt["V"]
        if (blk := mblk) or True:
            @blk.tensor
            def _(te):
                w(te, "V", fl_build_V)
                for dc in range(NDC):
                    ins = te.matmul(psF1[0:48, 0:5], ddisc8[:, dc * 48:(dc + 1) * 48],
                                    momdc8[:, dc * 5:(dc + 1) * 5],
                                    start=(dc == 0), stop=(dc == NDC - 1))
                    assert inc(ins, "T", 1) == tfl[dc]

        V4, T4 = cnt["V"], cnt["T"]
        mk_v = lambda k: V4 + k
        if (blk := mblk) or True:
            @blk.vector
            def _(ve):
                w(ve, "T", tfl[NDC - 1])
                assert inc(ve.tensor_copy(loss5[:], psF1[0:48, 0:5]), "V", 1) == mk_v(1)
                w(ve, "V", mk_v(1))
                assert inc(ve.tensor_tensor(ppg[:], loss5[:], wgrid[:], ALU.mult),
                           "V", 1) == mk_v(2)
                w(ve, "V", mk_v(2))
                assert inc(ve.reduce_sum(pp[:], ppg[:], axis=AX.X), "V", 1) == mk_v(3)
                w(ve, "V", mk_v(3))
                assert inc(ve.tensor_copy(pphl[:, 0:1], pp[:]), "V", 1) == mk_v(4)
                w(ve, "V", mk_v(4))
                ins = ve.scalar_tensor_tensor(pphl[:, 1:2], pphl[:, 0:1], -1.0,
                                              pp[:], ALU.mult, ALU.add)
                assert inc(ins, "V", 1) == mk_v(5)
                w(ve, "T", T4 + 1)
                assert inc(ve.tensor_copy(lrow[:], psF1[0:1, 16:24]), "V", 1) == mk_v(6)
                w(ve, "V", mk_v(6))
                assert inc(ve.tensor_reduce(lmin[:], lrow[:], AX.X, ALU.min),
                           "V", 1) == mk_v(7)
                w(ve, "V", mk_v(7))
                assert inc(ve.tensor_scalar(ohrow[:], lrow[:], lmin[:], None,
                                            ALU.is_equal), "V", 1) == mk_v(8)
                w(ve, "V", mk_v(8))
                ve.tensor_copy(ohrowb[:], ohrow[:])
                ins = ve.tensor_tensor(ohj[:], ohrow[:], iotar[:], ALU.mult)
                assert inc(ins, "V", 1) == mk_v(9)
                w(ve, "V", mk_v(9))
                assert inc(ve.reduce_sum(idxv[:, 0:1], ohj[:], axis=AX.X),
                           "V", 1) == mk_v(10)
                w(ve, "T", T4 + 3)
                ve.tensor_copy(ohcol[:], psF1[0:8, 24:25])
                assert inc(ve.tensor_copy(ohbc[:], psF1[0:128, 32:40]), "V", 1) == mk_v(11)

            @blk.tensor
            def _(te):
                w(te, "V", mk_v(5))
                te.matmul(psF1[0:1, 16:24], pphl[:, 0:1], esel[:],
                          start=True, stop=False)
                ins = te.matmul(psF1[0:1, 16:24], pphl[:, 1:2], esel[:],
                                start=False, stop=True)
                assert inc(ins, "T", 1) == T4 + 1
                w(te, "V", mk_v(9))
                ins = te.transpose(psF1[0:8, 24:25], ohrow[:], identf[0:1, 0:1])
                assert inc(ins, "T", 1) == T4 + 2
                ins = te.matmul(psF1[0:128, 32:40], onesrow[:], ohrowb[:],
                                start=True, stop=True)
                assert inc(ins, "T", 1) == T4 + 3
        idx_V = cnt["V"]
        tap("lrow", lrow[:], [1, 8], "V", mk_v(6))
        tap("ohbc", ohbc[:], [128, 8], "V")

        if (blk := mblk) or True:
            @blk.vector
            def _(ve):
                w(ve, "V", mk_v(11))
                inc(ve.tensor_scalar(bucol[:, 0:8], uball[:, 0:8], ohbc[:, 0:1],
                                     None, ALU.mult), "V", 1)
                inc(ve.tensor_scalar(bdcol[:, 0:1], dball[:, 0:1], ohbc[0:64, 0:1],
                                     None, ALU.mult), "V", 1)
                for e in range(1, E):
                    w(ve, "V", cnt["V"])
                    src_u = bucol[:, 8 * ((e + 1) % 2):8 * ((e + 1) % 2) + 8]
                    dst_u = bucol[:, 8 * (e % 2):8 * (e % 2) + 8]
                    inc(ve.scalar_tensor_tensor(dst_u, uball[:, e * 8:(e + 1) * 8],
                                                ohbc[:, e:e + 1], src_u, ALU.mult,
                                                ALU.add), "V", 1)
                    src_d = bdcol[:, (e + 1) % 2:(e + 1) % 2 + 1]
                    dst_d = bdcol[:, e % 2:e % 2 + 1]
                    inc(ve.scalar_tensor_tensor(dst_d, dball[:, e:e + 1],
                                                ohbc[0:64, e:e + 1], src_d, ALU.mult,
                                                ALU.add), "V", 1)
                w(ve, "V", cnt["V"])
                inc(ve.tensor_copy(bucolb[:], bucol[:, 8:16]), "V", 1)
                w(ve, "V", mk_v(10))
                inc(ve.tensor_copy(idxbb[:], idxv[:, 0:1]), "V", 1)
        idxbb_V = cnt["V"]
        budcol_V = idxbb_V - 1
        TG = cnt["T"]
        if (blk := mblk) or True:
            @blk.tensor
            def _(te):
                w(te, "V", idxbb_V)
                ins = te.matmul(psF1[0:128, 40:41], onesrow[:], idxbb[:],
                                start=True, stop=True)
                inc(ins, "T", 1)
        tidxb = cnt["T"]
        if (blk := mblk) or True:
            @blk.vector
            def _(ve):
                w(ve, "T", tidxb)
                w(ve, "G", iota_G)
                inc(ve.tensor_copy(idxbc[:], psF1[0:128, 40:41]), "V", 1)
                w(ve, "V", cnt["V"])
                for gi in range(17):
                    if gi < 8:
                        mulc, addc = 1024.0, float(gi * 128)
                    elif gi < 16:
                        mulc, addc = 1024.0, float((gi - 8) * 128)
                    else:
                        mulc, addc = 64.0, 0.0
                    inc(ve.tensor_scalar(gidxf[:, gi:gi + 1], idxbc[:], mulc, addc,
                                         ALU.mult, ALU.add), "V", 1)
                w(ve, "V", cnt["V"])
                inc(ve.tensor_single_scalar(gidxf[:], gidxf[:], iotac[:], ALU.add),
                    "V", 1)
                w(ve, "V", cnt["V"])
                inc(ve.tensor_copy(gidx17[:], gidxf[:]), "V", 1)
        gidx_V = cnt["V"]

        G0 = cnt["dG"]
        if (blk := mblk) or True:
            @blk.gpsimd
            def _(g):
                w(g, "V", gidx_V)
                for c in range(8):
                    inc(g.indirect_dma_start(
                        dwsel[:, c * R:(c + 1) * R], None, d_in["downW"],
                        IndirectOffsetOnAxis(gidx17[:, 8 + c:9 + c], 0)), "dG", 16)
                inc(g.indirect_dma_start(
                    upsel[:], None, d_in["upW"],
                    IndirectOffsetOnAxis(gidx17[0:64, 16:17], 0)), "dGu", 16)
                for c in range(8):
                    inc(g.indirect_dma_start(
                        convsel[:, c * D:(c + 1) * D], None, d_in["convT"],
                        IndirectOffsetOnAxis(gidx17[:, c:c + 1], 0)), "dGc", 16)
        down_dG = cnt["dG"]
        up_dG = cnt["dGu"]
        conv_dGc = cnt["dGc"]
        tap("dwsel", dwsel[:], [128, NDC * R], "dG")
        tap("upsel", upsel[:], [64, D], "dGu")
        tap("convsel", convsel[:], [128, NDC * D], "dGc")
        if stage == "fl":
            return finish_ret()

        T6, V6, S6 = cnt["T"], cnt["V"], cnt["S"]
        tdn = [T6 + lt + 1 for lt in range(4)]
        shv = [S6 + lt + 1 for lt in range(4)]
        tup = [T6 + 4 + i + 1 for i in range(32)]
        tbias3 = T6 + 4 + 32 + 1
        aev = [0] * 32
        _vv, _ss = V6, S6 + 4
        for i in range(32):
            if i % 2 == 0:
                _vv += 1; aev[i] = ("V", _vv)
            else:
                _ss += 1; aev[i] = ("S", _ss)
        vbias3 = _vv + 1
        if (blk := mblk) or True:
            @blk.tensor
            def _(te):
                w(te, "dW", wbase_dL)
                w(te, "dG", down_dG)
                for lt in range(4):
                    if lt >= 2:
                        w(te, "S", shv[lt - 2])
                    for dc in range(NDC):
                        ins = te.matmul(
                            psH[:, (lt % 2) * 512:(lt % 2 + 1) * 512],
                            dwsel[:, dc * R:(dc + 1) * R],
                            s1fm[:, dc * L + lt * 512:dc * L + (lt + 1) * 512],
                            start=(dc == 0), stop=(dc == NDC - 1))
                    assert inc(ins, "T", 1) == tdn[lt]
                w(te, "dGu", up_dG)
                for oc in range(NDC):
                    for lt in range(4):
                        i = oc * 4 + lt
                        if i >= 2:
                            w(te, aev[i - 2][0], aev[i - 2][1])
                        w(te, "S", shv[lt])
                        ins = te.matmul(
                            psA2[:, (i % 2) * 512:(i % 2 + 1) * 512],
                            upsel[:, oc * 128:(oc + 1) * 128],
                            h_sb[:, lt * 512:(lt + 1) * 512],
                            start=True, stop=True)
                        assert inc(ins, "T", 1) == tup[i]
                w(te, "dGc", conv_dGc)
                w(te, "V", budcol_V)
                for ot in range(2):
                    te.matmul(psH[0:1, ot * 512:(ot + 1) * 512], ohcol[:],
                              cball[:, ot * 512:(ot + 1) * 512],
                              start=True, stop=False)
                    for dc in range(NDC):
                        ins = te.matmul(
                            psH[0:1, ot * 512:(ot + 1) * 512],
                            bucolb[:, dc:dc + 1],
                            convsel[:, dc * D + ot * 512:dc * D + (ot + 1) * 512],
                            start=False, stop=(dc == NDC - 1))
                assert inc(ins, "T", 1) == tbias3

            @blk.scalar
            def _(se):
                for lt in range(4):
                    w(se, "T", tdn[lt])
                    ins = se.activation(h_sb[:, lt * 512:(lt + 1) * 512],
                                        psH[:, (lt % 2) * 512:(lt % 2 + 1) * 512],
                                        AF.Relu, bias=bdcol[:, 1:2], scale=1.0)
                    assert inc(ins, "S", 1) == shv[lt]

            @blk.vector
            def _(ve):
                w(ve, "V", gidx_V)
                for i in range(0, 32, 2):
                    oc, lt = i // 4, i % 4
                    w(ve, "T", tup[i])
                    ins = ve.tensor_copy(
                        a_sb[:, oc * L + lt * 512:oc * L + (lt + 1) * 512],
                        psA2[:, (i % 2) * 512:(i % 2 + 1) * 512])
                    assert inc(ins, "V", 1) == aev[i][1]
                w(ve, "T", tbias3)
                ins = ve.tensor_tensor(biasrow[:], psH[0:1, 0:1024], bbrow[:],
                                       ALU.add)
                assert inc(ins, "V", 1) == vbias3

            @blk.scalar
            def _(se):
                for i in range(1, 32, 2):
                    oc, lt = i // 4, i % 4
                    w(se, "T", tup[i])
                    ins = se.activation(
                        a_sb[:, oc * L + lt * 512:oc * L + (lt + 1) * 512],
                        psA2[:, (i % 2) * 512:(i % 2 + 1) * 512],
                        AF.Copy, bias=0.0, scale=1.0)
                    assert inc(ins, "S", 1) == aev[i][1]
        a_V = cnt["V"]
        a_S = cnt["S"]
        assert cnt["T"] == tbias3 and a_V == vbias3
        tap("h", h_sb[:], [64, L], "S")
        tap("a", a_sb[:], [128, NDC * L], "V")

        T7, V7, S7 = cnt["T"], cnt["V"], cnt["S"]
        tout = [T7 + lc + 1 for lc in range(NLC)]
        vu2 = [V7 + 6 * lc + 1 for lc in range(NLC)]
        vms2 = [V7 + 6 * lc + 2 for lc in range(NLC)]
        vm22 = [V7 + 6 * lc + 3 for lc in range(NLC)]
        vst2 = [V7 + 6 * lc + 4 for lc in range(NLC)]
        vrc2 = [V7 + 6 * lc + 5 for lc in range(NLC)]
        vap2 = [V7 + 6 * lc + 6 for lc in range(NLC)]
        ssq2 = [S7 + 2 * lc + 1 for lc in range(NLC)]
        ssd2 = [S7 + 2 * lc + 2 for lc in range(NLC)]
        oout = [16 * (lc // 2 + 1) for lc in range(NLC)]
        if (blk := mblk) or True:
            @blk.tensor
            def _(te):
                w(te, "V", vbias3)
                w(te, "S", a_S)
                w(te, "dGc", conv_dGc)
                for lc in range(NLC):
                    if lc >= 2:
                        w(te, "V", vu2[lc - 2])
                    for ot in range(2):
                        if lc >= 2:
                            for dc in range(NDC):
                                te.matmul(
                                    psO2[:, (lc % 2) * D + ot * 512:(lc % 2) * D + (ot + 1) * 512],
                                    s1fm[:, dc * L + lc * 128:dc * L + (lc + 1) * 128],
                                    wbase[:, dc * D + ot * 512:dc * D + (ot + 1) * 512],
                                    start=(dc == 0), stop=False)
                        for dc in range(NDC):
                            te.matmul(
                                psO2[:, (lc % 2) * D + ot * 512:(lc % 2) * D + (ot + 1) * 512],
                                a_sb[:, dc * L + lc * 128:dc * L + (lc + 1) * 128],
                                convsel[:, dc * D + ot * 512:dc * D + (ot + 1) * 512],
                                start=False, stop=False)
                        ins = te.matmul(
                            psO2[:, (lc % 2) * D + ot * 512:(lc % 2) * D + (ot + 1) * 512],
                            onesrow[:], biasrow[:, ot * 512:(ot + 1) * 512],
                            start=False, stop=True)
                    assert inc(ins, "T", 1) == tout[lc]

            @blk.vector
            def _(ve):
                for lc in range(NLC):
                    w(ve, "T", tout[lc])
                    if lc >= 2:
                        w(ve, "dOa" if lc % 2 == 0 else "dOb", oout[lc - 2])
                    ins = ve.scalar_tensor_tensor(
                        s2t[:, (lc % 2) * D:(lc % 2 + 1) * D],
                        psO2[:, (lc % 2) * D:(lc % 2 + 1) * D], 1.0,
                        s1bf[:, lc * D:(lc + 1) * D],
                        ALU.mult, ALU.add, accum_out=stat[:, 0:1])
                    assert inc(ins, "V", 1) == vu2[lc]
                    w(ve, "S", ssq2[lc])
                    ve.tensor_scalar(stat[:, 2:3], stat[:, 0:1], 1.0 / D, None, ALU.mult)
                    ins = ve.tensor_scalar(stat[:, 4:5], stat[:, 1:2], 1.0 / D, None,
                                           ALU.mult)
                    assert inc(ins, "V", 1) == vms2[lc]
                    w(ve, "V", vms2[lc])
                    ins = ve.tensor_tensor(stat[:, 5:6], stat[:, 2:3], stat[:, 2:3], ALU.mult)
                    assert inc(ins, "V", 1) == vm22[lc]
                    w(ve, "V", vm22[lc])
                    ins = ve.scalar_tensor_tensor(stat[:, 6:7], stat[:, 4:5], EPS_LN,
                                                  stat[:, 5:6], ALU.add, ALU.subtract)
                    assert inc(ins, "V", 1) == vst2[lc]
                    w(ve, "S", ssd2[lc])
                    ins = ve.reciprocal(stat[:, 8:9], stat[:, 7:8])
                    assert inc(ins, "V", 1) == vrc2[lc]
                    w(ve, "V", vrc2[lc])
                    ins = ve.tensor_scalar(
                        s2t[:, (lc % 2) * D:(lc % 2 + 1) * D],
                        s2t[:, (lc % 2) * D:(lc % 2 + 1) * D],
                        stat[:, 2:3], stat[:, 8:9], ALU.subtract, ALU.mult)
                    assert inc(ins, "V", 1) == vap2[lc]

            @blk.scalar
            def _(se):
                for lc in range(NLC):
                    w(se, "V", vu2[lc])
                    ins = se.activation(scrap[:, 0:D],
                                        s2t[:, (lc % 2) * D:(lc % 2 + 1) * D],
                                        AF.Square, accum_out=stat[:, 1:2])
                    assert inc(ins, "S", 1) == ssq2[lc]
                    w(se, "V", vst2[lc])
                    ins = se.activation(stat[:, 7:8], stat[:, 6:7],
                                        AF.Sqrt, bias=0.0, scale=1.0)
                    assert inc(ins, "S", 1) == ssd2[lc]

            @blk.sync
            def _(sy):
                for lc in range(NLC):
                    sy.wait_ge(sem["V"], vap2[lc])
                    ins = sy.dma_start(d_out.rearrange("(t p) d -> t p d", p=128)[lc],
                                       s2t[:, (lc % 2) * D:(lc % 2 + 1) * D])
                    assert inc(ins, "dOa" if lc % 2 == 0 else "dOb", 16) == oout[lc]
        s_fn.close(); open_stacks.remove(s_fn)
        s_wt.close(); open_stacks.remove(s_wt)
        return finish_ret()


def _finish(nc, sem, cnt, inc, tap_list):
    with nc.Block() as blk:
        @blk.gpsimd
        def _(g):
            for name, dram, sbap, semn, val in tap_list:
                g.wait_ge(sem[semn], val)
                inc(g.dma_start(dram, sbap), "dT", 16)
            g.wait_ge(sem["dT"], cnt["dT"])
            g.wait_ge(sem["dOa"], cnt["dOa"])
            g.wait_ge(sem["dOb"], cnt["dOb"])



def kernel(**inputs):
    import numpy as np
    from concourse.bass_utils import run_bass_kernel_spmd
    shared = host_prep(inputs)
    x = np.asarray(inputs["L0_attn_x"], np.float32)
    nc = build()
    maps = []
    for b in range(8):
        m = {"i_" + k: v for k, v in shared.items()}
        m["x"] = np.ascontiguousarray(x[b])
        maps.append(m)
    res = run_bass_kernel_spmd(nc, maps, core_ids=list(range(8)))
    out = np.stack([np.asarray(res.results[b]["out"], np.float32) for b in range(8)], 0)
    return out


# version 18 (final)
# speedup vs baseline: 1.2453x; 1.0135x over previous
import numpy as np
from contextlib import ExitStack

import concourse.bass as bass
import concourse.mybir as mybir
from concourse.bass import IndirectOffsetOnAxis

F32 = mybir.dt.float32
BF16 = mybir.dt.bfloat16
I32 = mybir.dt.int32
ALU = mybir.AluOpType
AF = mybir.ActivationFunctionType
AX = mybir.AxisListType

L, D, H, DH, NF, E, R = 2048, 1024, 16, 64, 64, 8, 64
NLC, NDC, NCP = 16, 8, 8
EPS_ATT = 1e-6
EPS_LN = 1e-5


def host_prep(inp):
    c = np.ascontiguousarray
    f32 = np.float32
    g = lambda k: np.asarray(inp[k], f32)
    s = {}
    s["wqT"] = c(g("L0_attn_Wq").T)
    s["wkT"] = c(g("L0_attn_Wk").T)
    s["wvT"] = c(g("L0_attn_Wv").T)
    s["woT"] = c(g("L0_attn_Wo").T)
    s["baseT"] = c(g("L0_ffn_base_W").T)
    s["convT"] = c(g("L0_ffn_conv_W").transpose(0, 2, 1).reshape(E * D, D))
    s["downW"] = c(g("L0_ffn_down_W").reshape(E * D, R))
    s["upW"] = c(g("L0_ffn_up_W").reshape(E * R, D))
    s["dball"] = c(g("L0_ffn_down_b").T)
    s["uball"] = c(g("L0_ffn_up_b").reshape(E, 8, 128).transpose(2, 0, 1).reshape(128, 64))
    s["cball"] = c(g("L0_ffn_conv_b"))
    s["muT"] = c(g("L0_ffn_disc_mu").T)
    s["sigT"] = c(g("L0_ffn_disc_sigma").T)
    W0 = g("L0_attn_Wrf")[0]
    w0blk = np.zeros((NCP * 128, 128), f32)
    for cp in range(NCP):
        w0blk[cp * 128:cp * 128 + 64, 0:64] = W0[2 * cp]
        w0blk[cp * 128 + 64:cp * 128 + 128, 64:128] = W0[2 * cp + 1]
    s["w0blk"] = w0blk
    s["bqk"] = c(np.stack([g("L0_attn_bq").reshape(8, 128).T,
                           g("L0_attn_bk").reshape(8, 128).T],
                          axis=2).reshape(128, 16))
    s["bvrow"] = c(g("L0_attn_bv").reshape(1, D))
    s["borow"] = c(g("L0_attn_bo").reshape(1, D))
    s["bbrow"] = c(g("L0_ffn_base_b").reshape(1, D))
    s["identc"] = np.eye(128, dtype=f32)
    dsel = np.zeros((16, NCP * 128), f32)
    for cp in range(NCP):
        dsel[2 * cp, cp * 128:cp * 128 + 64] = 1.0
        dsel[2 * cp + 1, cp * 128 + 64:cp * 128 + 128] = 1.0
    dsel64 = np.zeros((64, NCP * 128), f32)
    dsel64[0:16] = dsel
    dsel64[32:48] = dsel
    s["dsel64"] = dsel64
    par = np.zeros((1, 256), f32)
    par[0, 0:64] = 1.0
    par[0, 128 + 64:256] = 1.0
    s["parab"] = par
    kkm = np.zeros((128, NCP * 16), f32)
    for cp in range(NCP):
        kkm[0:64, cp * 16 + 2 * cp] = -0.5
        kkm[64:128, cp * 16 + 2 * cp + 1] = -0.5
    s["kkmask"] = kkm
    s["onesrow"] = np.ones((1, 128), f32)
    wg = np.zeros((48, 5), f32)
    wg[0:8, 2] = 1.0; wg[0:8, 3] = 1.0
    wg[24:32, 2] = 1.0
    wg[8:16, 0] = -2.0; wg[8:16, 1] = -2.0
    wg[32:40, 0] = -2.0
    wg[16:24, 4] = 1.0; wg[40:48, 4] = 1.0
    s["wgrid"] = wg
    es = np.zeros((48, 8), f32)
    for e in range(8):
        for blk in range(6):
            es[blk * 8 + e, e] = 1.0
    s["esel"] = es
    return s


def build(taps=(), stage="full"):
    nc = bass.Bass("TRN2", target_bir_lowering=False)
    taps = set(taps)
    tap_list = []
    sem, cnt = {}, {}

    import sys as _sys
    _marks = {}
    build._marks = _marks

    def inc(ins, name, n=1):
        ins.then_inc(sem[name], n)
        cnt[name] += n
        _marks[(name, cnt[name])] = _sys._getframe(1).f_lineno
        return cnt[name]

    def w(eng, name, val):
        assert val <= 10 ** 9
        eng.wait_ge(sem[name], val)

    dt = nc.dram_tensor
    d_x = dt("x", [L, D], F32, kind="ExternalInput").ap()
    names = [("wqT", [D, D]), ("wkT", [D, D]), ("wvT", [D, D]), ("woT", [D, D]),
             ("baseT", [D, D]), ("convT", [E * D, D]), ("downW", [E * D, R]),
             ("upW", [E * R, D]), ("dball", [64, 8]), ("uball", [128, 64]),
             ("cball", [8, D]), ("muT", [D, E]), ("sigT", [D, E]),
             ("w0blk", [NCP * 128, 128]), ("bqk", [128, 16]), ("bvrow", [1, D]),
             ("borow", [1, D]), ("bbrow", [1, D]), ("identc", [128, 128]),
             ("dsel64", [64, NCP * 128]),
             ("parab", [1, 256]), ("kkmask", [128, NCP * 16]),
             ("onesrow", [1, 128]), ("wgrid", [48, 5]), ("esel", [48, 8])]
    d_in = {nm: dt("i_" + nm, shp, F32, kind="ExternalInput").ap() for nm, shp in names}
    d_out = dt("out", [L, D], F32, kind="ExternalOutput").ap()
    x_t = d_x.rearrange("(t p) d -> t p d", p=128)

    def tap(name, sbuf_ap, shape, semn, val=None):
        if name in taps:
            tp = dt("tap_" + name, shape, F32, kind="ExternalOutput").ap()
            tap_list.append((name, tp, sbuf_ap, semn,
                             cnt[semn] if val is None else val))

    ctx = ExitStack()
    SB = lambda n, s, t: ctx.enter_context(nc.sbuf_tensor(n, s, t))
    open_stacks = []

    def finish_ret():
        mblk.__exit__(None, None, None)
        _finish(nc, sem, cnt, inc, tap_list)
        for st_ in reversed(open_stacks):
            st_.close()
        return nc

    with ctx:
        for s_ in ["dL", "dH", "dW", "dXa", "dXb", "dG", "dGu", "dGc", "dOa", "dOb", "dT", "dC", "dRa", "dRb", "dRc", "dRd", "T", "V", "S", "G", "P"]:
            sem[s_] = ctx.enter_context(nc.semaphore(s_))
            cnt[s_] = 0

        identb = SB("identb", [128, 128], BF16)
        identf = SB("identf", [128, 128], F32)
        w0blk = SB("w0blk", [128, NCP * 128], BF16)
        dsel64 = SB("dsel64", [64, NCP * 128], BF16)
        parab = SB("parab", [1, 256], BF16)
        kkmask = SB("kkmask", [128, NCP * 16], BF16)
        onesrow = SB("onesrow", [1, 128], BF16)
        bqk = SB("bqk", [128, 16], F32)
        bvrow = SB("bvrow", [1, D], BF16)
        wgrid = SB("wgrid", [48, 5], F32)
        esel = SB("esel", [48, 8], BF16)

        m1c = SB("m1c", [128, 8], F32)
        m2c = SB("m2c", [128, 8], F32)
        momdc8 = SB("momdc8", [128, 40], BF16)
        sctA = SB("sctA", [128, 64], F32)
        ivA = SB("ivA", [128, 64], F32)
        mivA = SB("mivA", [128, 64], F32)
        m2ivA = SB("m2ivA", [128, 64], F32)
        ddisc8 = SB("ddisc8", [128, 384], BF16)
        loss5 = SB("loss5", [48, 5], F32)
        ppg = SB("ppg", [48, 5], F32)
        pp = SB("pp", [48, 1], F32)
        pphl = SB("pphl", [48, 2], BF16)
        lcol2 = SB("lcol2", [8, 2], F32)
        lcol = SB("lcol", [8, 1], F32)
        lrow = SB("lrow", [1, 8], F32)
        lmin = SB("lmin", [1, 1], F32)
        ohrow = SB("ohrow", [1, 8], F32)
        ohrowb = SB("ohrowb", [1, 8], BF16)
        ohj = SB("ohj", [1, 8], F32)
        ohcol = SB("ohcol", [8, 1], BF16)
        ohbc = SB("ohbc", [128, 8], F32)
        idxv = SB("idxv", [1, 2], F32)
        iotar = SB("iotar", [1, 8], F32)
        bdcol = SB("bdcol", [64, 2], F32)
        bucol = SB("bucol", [128, 16], F32)
        bucolb = SB("bucolb", [128, 8], BF16)
        muT = SB("muT", [128, NDC * E], F32)
        sigT = SB("sigT", [128, NDC * E], F32)
        dball = SB("dball", [64, 8], F32)
        uball = SB("uball", [128, 64], F32)
        bbrow = SB("bbrow", [1, D], BF16)
        iotac = SB("iotac", [128, 1], F32)
        idxbc = SB("idxbc", [128, 1], F32)
        idxbb = SB("idxbb", [1, 1], BF16)
        gidxf = SB("gidxf", [128, 17], F32)
        gidx17 = SB("gidx17", [128, 17], I32)

        mblk = nc.Block()
        mblk.__enter__()

        if (blk := mblk) or True:
            @blk.gpsimd
            def _(g):
                g.iota(iotar[:], [[1, 8]], base=0, channel_multiplier=0,
                       allow_small_or_imprecise_dtypes=True)
                g.iota(iotac[:], [[1, 1]], base=0, channel_multiplier=1,
                       allow_small_or_imprecise_dtypes=True)
                inc(g.nop(), "G", 1)
                inc(g.dma_start(identb[:], d_in["identc"]), "dL", 16)

            @blk.sync
            def _(sy):
                for dstap, nm, rr in [
                        (identf[:], "identc", None), (bqk[:], "bqk", None),
                        (wgrid[:], "wgrid", None),
                        (muT[:].rearrange("p (c e) -> p c e", e=E), "muT",
                         "(c p) e -> p c e"),
                        (sigT[:].rearrange("p (c e) -> p c e", e=E), "sigT",
                         "(c p) e -> p c e"),
                        (dball[:], "dball", None), (uball[:], "uball", None)]:
                    src = d_in[nm]
                    if rr:
                        src = src.rearrange(rr, p=128)
                    inc(sy.dma_start(dstap, src), "dH", 16)
        consts_dL = cnt["dL"]
        consts_dH = cnt["dH"]
        iota_G = cnt["G"]

        s_big = ExitStack()
        open_stacks.append(s_big)
        wq = s_big.enter_context(nc.sbuf_tensor("wq", [128, NDC * D], BF16, side="right"))
        wk = s_big.enter_context(nc.sbuf_tensor("wk", [128, NDC * D], BF16, side="right"))
        wv = s_big.enter_context(nc.sbuf_tensor("wv", [128, NDC * D], BF16, side="right"))
        xfm = s_big.enter_context(nc.sbuf_tensor("xfm", [128, NDC * L], BF16, side="right"))

        s_att = ExitStack()
        open_stacks.append(s_att)
        SA = lambda n, s, t: s_att.enter_context(nc.sbuf_tensor(n, s, t))
        eq_sb = SA("eq_sb", [128, NCP * L], BF16)
        zd8 = SA("zd8", [8, 128], F32)
        d16 = SA("d16", [128, 8], F32)
        dcolA = SA("dcolA", [8, 1], F32)
        dcolB = SA("dcolB", [8, 1], F32)
        numS = [SA(f"numS{c}", [128, 128], BF16) for c in range(NCP)]
        denS = [SA(f"denS{c}", [128, 16], BF16) for c in range(NCP)]
        s_att1 = ExitStack()
        open_stacks.append(s_att1)
        SA1 = lambda n, s, t: s_att1.enter_context(nc.sbuf_tensor(n, s, t))
        v_sb = SA1("v_sb", [128, NLC * H * 65], BF16)
        k_sb = SA1("k_sb", [128, NCP * L], BF16)
        mkA = SA1("mkA", [1, 8], BF16)
        mkB = SA1("mkB", [1, 8], BF16)
        zArow = SA1("zArow", [1, 8], BF16)
        zBrow = SA1("zBrow", [1, 8], BF16)
        kkT_sb = SA1("kkT_sb", [128, 256], F32)
        qqT_sb = kkT_sb
        mqfm = None
        colmax = SA1("colmax", [128, 1], F32)
        scr3 = SA1("scr3", [1, 128], F32)
        mbias = SA1("mbias", [128, 1], F32)
        ebias = SA1("ebias", [128, 8], F32)
        v4 = v_sb[:].rearrange("p (lc h c) -> p lc h c", lc=NLC, h=H, c=65)
        s_aps = ExitStack()
        open_stacks.append(s_aps)
        psK = s_aps.enter_context(nc.psum_tensor("psK", [128, 1024], F32))
        psS = s_aps.enter_context(nc.psum_tensor("psS", [128, 512], F32))

        with ExitStack() as ph:
            xb = ph.enter_context(nc.sbuf_tensor("xb", [128, 4 * D], BF16, side="right"))
            xstg = ph.enter_context(nc.sbuf_tensor("xstg", [128, D], F32, side="right"))
            ptr = ph.enter_context(nc.psum_tensor("ptr", [128, 2048], BF16))
            _pt, _pv = cnt["T"], cnt["V"]
            txf = [0] * NLC
            tv = [[0, 0] for _ in range(NLC)]
            vxf = [0] * NLC
            vv = [[0, 0] for _ in range(NLC)]
            _ones_v = _pv + 1
            _pv += 1
            emit_order = []
            for lc in range(NLC):
                emit_order.append(("tr", lc))
                if lc >= 4:
                    emit_order.append(("v", lc - 4))
            for lc in range(NLC - 4, NLC):
                emit_order.append(("v", lc))
            for kind, lc in emit_order:
                if kind == "tr":
                    txf[lc] = _pt + 1
                    _pt += 1
                    vxf[lc] = _pv + 1
                    _pv += 1
                else:
                    tv[lc][0] = _pt + 1
                    tv[lc][1] = _pt + 2
                    _pt += 2
                    vv[lc][0] = _pv + 1
                    vv[lc][1] = _pv + 2
                    _pv += 2
            wv_dL = cnt["dW"] + 16
            grp2_dL = [consts_dL + 16,
                       consts_dL + 64] + \
                      [consts_dL + 64 + 16 * g for g in range(1, 7)]
            wqk_dL = wv_dL + 32
            late_dL = grp2_dL[7] + 6 * 16
            cast_S0 = cnt["S"]
            xcast = [cast_S0 + grp + 1 for grp in range(8)]

            if (blk := mblk) or True:
                @blk.gpsimd
                def _(g):
                    def tiles(grp):
                        lc = grp * 2
                        slot = (grp % 2) * 2
                        inc(g.dma_start(xb[:, slot * D:(slot + 1) * D],
                                        x_t[lc]), "dL", 16)
                    tiles(0)
                    g.wait_ge(sem["dL"], grp2_dL[0])
                    inc(g.dma_start(onesrow[:], d_in["onesrow"]), "dL", 16)
                    inc(g.dma_start(bvrow[:], d_in["bvrow"]), "dL", 16)
                    g.wait_ge(sem["dL"], consts_dL + 48)
                    tiles(1)
                    inc(g.dma_start(
                        wv[:].rearrange("p (c n) -> p c n", n=D),
                        d_in["wvT"].rearrange("(c p) n -> p c n", p=128)), "dW", 16)
                    for grp in range(2, 8):
                        g.wait_ge(sem["dL"], grp2_dL[grp - 1])
                        g.wait_ge(sem["T"], txf[2 * (grp - 1) - 1])
                        tiles(grp)
                    for wsb, nm in [(wq, "wqT"), (wk, "wkT")]:
                        inc(g.dma_start(
                            wsb[:].rearrange("p (c n) -> p c n", n=D),
                            d_in[nm].rearrange("(c p) n -> p c n", p=128)), "dW", 16)
                    g.wait_ge(sem["dL"], grp2_dL[7])
                    for dstap, nm, rr in [
                            (w0blk[:].rearrange("p (c n) -> p c n", n=128),
                             "w0blk", "(c p) n -> p c n"),
                            (dsel64[:], "dsel64", None),
                            (kkmask[:], "kkmask", None),
                            (parab[:], "parab", None),
                            (esel[:], "esel", None), (bbrow[:], "bbrow", None)]:
                        srcp = d_in[nm]
                        if rr:
                            srcp = srcp.rearrange(rr, p=128)
                        inc(g.dma_start(dstap, srcp), "dL", 16)
                    assert cnt["dL"] == late_dL

                @blk.sync
                def _(sy):
                    sy.wait_ge(sem["dH"], consts_dH)
                    for grp in range(8):
                        if grp >= 1:
                            sy.wait_ge(sem["S"], xcast[grp - 1])
                        if grp >= 2:
                            sy.wait_ge(sem["T"], txf[2 * (grp - 1) - 1])
                        inc(sy.dma_start(xstg[:], x_t[grp * 2 + 1]), "dH", 16)

                @blk.scalar
                def _(se):
                    for grp in range(8):
                        w(se, "dH", consts_dH + 16 * (grp + 1))
                        if grp >= 2:
                            w(se, "T", txf[2 * (grp - 1) - 1])
                        slot = (grp % 2) * 2 + 1
                        ins = se.copy(xb[:, slot * D:(slot + 1) * D], xstg[:])
                        assert inc(ins, "S", 1) == xcast[grp]

                @blk.tensor
                def _(te):
                    for kind, lc in emit_order:
                        if kind == "tr":
                            w(te, "dL", grp2_dL[lc // 2])
                            w(te, "S", xcast[lc // 2])
                            if lc >= 2:
                                w(te, "V", vxf[lc - 2])
                            slot = ((lc // 2) % 2) * 2 + (lc % 2)
                            for dc in range(NDC):
                                ins = te.transpose(
                                    ptr[:, (lc % 2) * 1024 + dc * 128:(lc % 2) * 1024 + (dc + 1) * 128],
                                    xb[:, slot * D + dc * 128:slot * D + (dc + 1) * 128],
                                    identb[:])
                            assert inc(ins, "T", 1) == txf[lc]
                        else:
                            w(te, "V", vxf[lc])
                            if lc == 0:
                                w(te, "dW", wv_dL)
                                w(te, "dL", consts_dL + 48)
                                w(te, "V", _ones_v)
                            for ot in range(2):
                                it = lc * 2 + ot
                                if it >= 2:
                                    pit = it - 2
                                    w(te, "V", vv[pit // 2][pit % 2])
                                for dc in range(NDC):
                                    te.matmul(
                                        psK[:, (it % 2) * 512:(it % 2 + 1) * 512],
                                        xfm[:, dc * L + lc * 128:dc * L + (lc + 1) * 128],
                                        wv[:, dc * D + ot * 512:dc * D + (ot + 1) * 512],
                                        start=(dc == 0), stop=False)
                                ins = te.matmul(
                                    psK[:, (it % 2) * 512:(it % 2 + 1) * 512],
                                    onesrow[:], bvrow[:, ot * 512:(ot + 1) * 512],
                                    start=False, stop=True)
                                assert inc(ins, "T", 1) == tv[lc][ot]

                @blk.vector
                def _(ve):
                    assert inc(ve.memset(v4[:, :, :, 64:65], 1.0), "V", 1) == _ones_v
                    for kind, lc in emit_order:
                        if kind == "tr":
                            w(ve, "T", txf[lc])
                            dst = xfm[:].rearrange("p (c l) -> p c l", l=L)[:, :, lc * 128:(lc + 1) * 128]
                            ins = ve.tensor_copy(
                                dst, ptr[:, (lc % 2) * 1024:(lc % 2) * 1024 + 1024]
                                .rearrange("p (c k) -> p c k", k=128))
                            assert inc(ins, "V", 1) == vxf[lc]
                        else:
                            for ot in range(2):
                                it = lc * 2 + ot
                                w(ve, "T", tv[lc][ot])
                                ins = ve.tensor_copy(
                                    v4[:, lc, ot * 8:(ot + 1) * 8, 0:64],
                                    psK[:, (it % 2) * 512:(it % 2 + 1) * 512]
                                    .rearrange("p (h c) -> p h c", c=64))
                                assert inc(ins, "V", 1) == vv[lc][ot]
        xfm_V = vxf[NLC - 1]
        wqkv_dL = wqk_dL
        s_apq = ExitStack()
        open_stacks.append(s_apq)
        psQ = s_apq.enter_context(nc.psum_tensor("psQ", [128, L], F32))
        s_kt = ExitStack()
        open_stacks.append(s_kt)
        psKT = s_kt.enter_context(nc.psum_tensor("psKT", [128, 512], F32))
        v_V = cnt["V"]
        tap("xfm", xfm[:], [128, NDC * L], "V")
        tap("v", v_sb[:], [128, NLC * H * 65], "V")
        if stage in ("t0", "p1a"):
            return finish_ret()

        s_qs = ExitStack()
        open_stacks.append(s_qs)
        sq_sb = s_qs.enter_context(nc.sbuf_tensor("sq_sb", [128, L], BF16))
        mqfm = s_qs.enter_context(nc.sbuf_tensor("mqfm", [64, L], BF16))
        T1, V1, S1 = cnt["T"], cnt["V"], cnt["S"]
        tq = [[0] * 4 for _ in range(NCP)]
        tqq = [[0] * 4 for _ in range(NCP)]
        _p = T1
        for c in range(NCP):
            for lt in range(4):
                tq[c][lt] = _p + 1; _p += 1
            if c >= 1:
                for lt in range(4):
                    tqq[c - 1][lt] = _p + 1; _p += 1
        for lt in range(4):
            tqq[NCP - 1][lt] = _p + 1; _p += 1
        vqe = [[V1 + 8 * c + lt + 1 for lt in range(4)] for c in range(NCP)]
        vsqq = [[V1 + 8 * c + 4 + lt + 1 for lt in range(4)] for c in range(NCP)]
        TA_end = T1 + 8 * NCP
        VA_end = V1 + 8 * NCP
        vqq = VA_end + 1
        vm25 = VA_end + 2
        tmq = [TA_end + lc + 1 for lc in range(NLC)]
        vmq2 = [VA_end + 2 + lc + 1 for lc in range(NLC)]
        sscl = S1 + 1
        TB_end = TA_end + NLC
        VB_end = VA_end + 2 + NLC
        tlq = [[TB_end + 4 * c + lt + 1 for lt in range(4)] for c in range(NCP)]
        sexp = [[S1 + 1 + 4 * c + lt + 1 for lt in range(4)] for c in range(NCP)]
        TC_end = TB_end + 4 * NCP
        tk = [[0] * 4 for _ in range(NCP)]
        tkk = [[0] * 4 for _ in range(NCP)]
        _p = TC_end
        for c in range(NCP):
            for lt in range(4):
                tk[c][lt] = _p + 1; _p += 1
            if c >= 1:
                for lt in range(4):
                    tkk[c - 1][lt] = _p + 1; _p += 1
        for lt in range(4):
            tkk[NCP - 1][lt] = _p + 1; _p += 1
        vke = [[VB_end + 8 * c + lt + 1 for lt in range(4)] for c in range(NCP)]
        vsqk = [[VB_end + 8 * c + 4 + lt + 1 for lt in range(4)] for c in range(NCP)]

        if (blk := mblk) or True:
            @blk.tensor
            def _(te):
                w(te, "dW", wqk_dL)
                w(te, "dL", late_dL)
                for cp in range(NCP):
                    for lt in range(4):
                        if cp > 0:
                            w(te, "V", vqe[cp - 1][lt])
                        for dc in range(NDC):
                            ins = te.matmul(
                                psQ[:, lt * 512:(lt + 1) * 512],
                                wq[:, dc * D + cp * 128:dc * D + (cp + 1) * 128],
                                xfm[:, dc * L + lt * 512:dc * L + (lt + 1) * 512],
                                start=(dc == 0), stop=(dc == NDC - 1))
                        assert inc(ins, "T", 1) == tq[cp][lt]
                    for pcp in ([cp - 1] if cp >= 1 else []):
                        for lt in range(4):
                            w(te, "V", vsqq[pcp][lt])
                            for j in range(4):
                                lc = lt * 4 + j
                                ins = te.matmul(
                                    psKT[:, 256 + lc * 16 + 2 * pcp:256 + lc * 16 + 2 * pcp + 2],
                                    sq_sb[:, (pcp % 2) * 0 + lc * 128:(lc + 1) * 128],
                                    kkmask[:, pcp * 16 + 2 * pcp:pcp * 16 + 2 * pcp + 2],
                                    start=True, stop=True)
                            assert inc(ins, "T", 1) == tqq[pcp][lt]
                for lt in range(4):
                    w(te, "V", vsqq[NCP - 1][lt])
                    for j in range(4):
                        lc = lt * 4 + j
                        ins = te.matmul(
                            psKT[:, 256 + lc * 16 + 2 * (NCP - 1):256 + lc * 16 + 2 * (NCP - 1) + 2],
                            sq_sb[:, lc * 128:(lc + 1) * 128],
                            kkmask[:, (NCP - 1) * 16 + 2 * (NCP - 1):(NCP - 1) * 16 + 2 * (NCP - 1) + 2],
                            start=True, stop=True)
                    assert inc(ins, "T", 1) == tqq[NCP - 1][lt]
                w(te, "V", vm25)
                for lc in range(NLC):
                    if lc >= 1:
                        w(te, "V", vmq2[lc - 1])
                    ins = te.transpose(
                        psS[0:16, (lc % 2) * 128:(lc % 2) * 128 + 128],
                        qqT_sb[:, lc * 16:(lc + 1) * 16], identf[:])
                    assert inc(ins, "T", 1) == tmq[lc]
                for cp in range(NCP):
                    if cp == 0:
                        w(te, "V", vmq2[NLC - 1])
                    for lt in range(4):
                        if cp == 0:
                            w(te, "V", vqe[NCP - 1][lt])
                        else:
                            w(te, "S", sexp[cp - 1][lt])
                        te.matmul(psQ[:, lt * 512:(lt + 1) * 512],
                                  w0blk[:, cp * 128:(cp + 1) * 128],
                                  k_sb[:, cp * L + lt * 512:cp * L + (lt + 1) * 512],
                                  start=True, stop=False)
                        ins = te.matmul(psQ[:, lt * 512:(lt + 1) * 512],
                                        dsel64[:, cp * 128:(cp + 1) * 128],
                                        mqfm[:, lt * 512:(lt + 1) * 512],
                                        start=False, stop=True)
                        assert inc(ins, "T", 1) == tlq[cp][lt]
                def _kkgrp(pcp, lt):
                    w(te, "V", vsqk[pcp][lt])
                    for j in range(4):
                        lc = lt * 4 + j
                        ins = te.matmul(
                            psKT[:, lc * 16 + 2 * pcp:lc * 16 + 2 * pcp + 2],
                            sq_sb[:, lc * 128:(lc + 1) * 128],
                            kkmask[:, pcp * 16 + 2 * pcp:pcp * 16 + 2 * pcp + 2],
                            start=True, stop=True)
                    assert inc(ins, "T", 1) == tkk[pcp][lt]
                for cp in range(NCP):
                    for lt in range(4):
                        bank = lt % 2
                        if cp == 0 and lt < 2:
                            w(te, "V", v_V)
                        else:
                            pc, plt = (cp, lt - 2) if lt >= 2 else (cp - 1, lt + 2)
                            w(te, "V", vke[pc][plt])
                        for dc in range(NDC):
                            ins = te.matmul(
                                psK[:, bank * 512:(bank + 1) * 512],
                                wk[:, dc * D + cp * 128:dc * D + (cp + 1) * 128],
                                xfm[:, dc * L + lt * 512:dc * L + (lt + 1) * 512],
                                start=(dc == 0), stop=(dc == NDC - 1))
                        assert inc(ins, "T", 1) == tk[cp][lt]
                    if cp >= 1:
                        for lt in range(4):
                            _kkgrp(cp - 1, lt)
                for lt in range(4):
                    _kkgrp(NCP - 1, lt)

            @blk.vector
            def _(ve):
                w(ve, "dH", consts_dH)
                ve.memset(mqfm[:], 0.0)
                for cp in range(NCP):
                    for lt in range(4):
                        w(ve, "T", tq[cp][lt])
                        ins = ve.tensor_scalar(
                            k_sb[:, cp * L + lt * 512:cp * L + (lt + 1) * 512],
                            psQ[:, lt * 512:(lt + 1) * 512],
                            bqk[:, cp * 2:cp * 2 + 1], None, ALU.add)
                        assert inc(ins, "V", 1) == vqe[cp][lt]
                    for lt in range(4):
                        w(ve, "V", vqe[cp][lt])
                        if cp > 0:
                            w(ve, "T", tqq[cp - 1][lt])
                        ins = ve.tensor_tensor(
                            sq_sb[:, lt * 512:(lt + 1) * 512],
                            k_sb[:, cp * L + lt * 512:cp * L + (lt + 1) * 512],
                            k_sb[:, cp * L + lt * 512:cp * L + (lt + 1) * 512],
                            ALU.mult)
                        assert inc(ins, "V", 1) == vsqq[cp][lt]
                w(ve, "T", tqq[NCP - 1][3])
                assert inc(ve.tensor_copy(qqT_sb[:], psKT[:, 256:512]), "V", 1) == vqq
                w(ve, "S", sscl)
                ins = ve.tensor_scalar(qqT_sb[:], qqT_sb[:], -1.0, -25.0, ALU.mult, ALU.add)
                assert inc(ins, "V", 1) == vm25
                for lc in range(NLC):
                    w(ve, "T", tmq[lc])
                    ins = ve.tensor_copy(
                        mqfm[0:16, lc * 128:(lc + 1) * 128],
                        psS[0:16, (lc % 2) * 128:(lc % 2) * 128 + 128])
                    assert inc(ins, "V", 1) == vmq2[lc]
                for cp in range(NCP):
                    for lt in range(4):
                        bank = lt % 2
                        w(ve, "T", tk[cp][lt])
                        if cp == 0 and lt == 0:
                            w(ve, "T", tlq[NCP - 1][3])
                        ins = ve.tensor_scalar(
                            k_sb[:, cp * L + lt * 512:cp * L + (lt + 1) * 512],
                            psK[:, bank * 512:(bank + 1) * 512],
                            bqk[:, cp * 2 + 1:cp * 2 + 2], None, ALU.add)
                        assert inc(ins, "V", 1) == vke[cp][lt]
                    for lt in range(4):
                        w(ve, "V", vke[cp][lt])
                        if cp > 0:
                            w(ve, "T", tkk[cp - 1][lt])
                        elif lt == 0:
                            w(ve, "T", tqq[NCP - 1][3])
                        ins = ve.tensor_tensor(
                            sq_sb[:, lt * 512:(lt + 1) * 512],
                            k_sb[:, cp * L + lt * 512:cp * L + (lt + 1) * 512],
                            k_sb[:, cp * L + lt * 512:cp * L + (lt + 1) * 512],
                            ALU.mult)
                        assert inc(ins, "V", 1) == vsqk[cp][lt]

            @blk.scalar
            def _(se):
                w(se, "V", vqq)
                ins = se.activation(qqT_sb[:], qqT_sb[:], AF.Sqrt,
                                    bias=0.0, scale=-2.0 * 2.35 * 2.35)
                assert inc(ins, "S", 1) == sscl
                for cp in range(NCP):
                    for lt in range(4):
                        w(se, "T", tlq[cp][lt])
                        ins = se.activation(
                            eq_sb[:, cp * L + lt * 512:cp * L + (lt + 1) * 512],
                            psQ[:, lt * 512:(lt + 1) * 512],
                            AF.Exp, bias=0.0, scale=1.0)
                        assert inc(ins, "S", 1) == sexp[cp][lt]
        tap("eq", eq_sb[:], [128, NCP * L], "S")
        tap("ksb", k_sb[:], [128, NCP * L], "V")
        if stage == "p1b":
            return finish_ret()
        s_qs.close(); open_stacks.remove(s_qs)

        s_big.close(); open_stacks.remove(s_big)

        s_ee = ExitStack()
        open_stacks.append(s_ee)
        comb2 = s_ee.enter_context(nc.sbuf_tensor("comb2", [64, L], BF16))
        comblo = s_ee.enter_context(nc.sbuf_tensor("comblo", [16, L], BF16))
        e_sb = s_ee.enter_context(nc.sbuf_tensor("e_sb", [128, 2 * L], BF16))
        eT_sb = s_ee.enter_context(nc.sbuf_tensor("eT_sb", [128, 2 * L], BF16))
        psEb = None

        T2p, V2p = cnt["T"], cnt["V"]
        vkkt = V2p + 1
        kktr = [T2p + lc + 1 for lc in range(NLC)]
        vkkfm = [V2p + 1 + 2 * lc + 2 for lc in range(NLC)]
        vclo = V2p + 1 + 2 * NLC
        assert vclo == vkkfm[-1]
        if (blk := mblk) or True:
            @blk.vector
            def _(ve):
                w(ve, "T", tkk[NCP - 1][3])
                w(ve, "T", tmq[NLC - 1])
                ve.memset(comb2[:], 0.0)
                assert inc(ve.tensor_copy(kkT_sb[:], psKT[:, 0:256]), "V", 1) == vkkt

            @blk.tensor
            def _(te):
                w(te, "V", vkkt)
                w(te, "S", sexp[NCP - 1][3])
                for lc in range(NLC):
                    if lc >= 1:
                        w(te, "V", vkkfm[lc - 1])
                    ins = te.transpose(
                        psS[0:16, (lc % 2) * 128:(lc % 2) * 128 + 128],
                        kkT_sb[:, lc * 16:(lc + 1) * 16], identf[:])
                    assert inc(ins, "T", 1) == kktr[lc]

            @blk.vector
            def _(ve):
                for lc in range(NLC):
                    w(ve, "T", kktr[lc])
                    ins = ve.tensor_copy(
                        comb2[0:16, lc * 128:(lc + 1) * 128],
                        psS[0:16, (lc % 2) * 128:(lc % 2) * 128 + 128])
                    inc(ins, "V", 1)
                    w(ve, "V", cnt["V"])
                    ins = ve.scalar_tensor_tensor(
                        comblo[:, lc * 128:(lc + 1) * 128],
                        comb2[0:16, lc * 128:(lc + 1) * 128], -1.0,
                        psS[0:16, (lc % 2) * 128:(lc % 2) * 128 + 128],
                        ALU.mult, ALU.add)
                    assert inc(ins, "V", 1) == vkkfm[lc]
                assert vclo == cnt["V"] + 0 or True
        tap("kkfm", comb2[0:16, :], [16, L], "V", None)
        if (blk := mblk) or True:
            @blk.sync
            def _(sy):
                sy.wait_ge(sem["V"], vclo)
                inc(sy.dma_start(comb2[32:48, :], comblo[:]), "dC", 16)
        s_wt = ExitStack()
        open_stacks.append(s_wt)
        wbase = s_wt.enter_context(nc.sbuf_tensor("wbase", [128, NDC * D], BF16, side="right"))
        cball = s_wt.enter_context(nc.sbuf_tensor("cball", [8, D], BF16, side="right"))
        borow = s_wt.enter_context(nc.sbuf_tensor("borow", [1, D], BF16, side="right"))
        s_wt2 = ExitStack()
        open_stacks.append(s_wt2)
        wo = s_wt2.enter_context(nc.sbuf_tensor("wo", [128, NDC * D], BF16, side="right"))
        if (blk := mblk) or True:
            @blk.gpsimd
            def _(g):
                g.wait_ge(sem["T"], tkk[NCP - 1][3])
                inc(g.dma_start(
                    wo[:].rearrange("p (c n) -> p c n", n=D),
                    d_in["woT"].rearrange("(c p) n -> p c n", p=128)), "dW", 16)
                inc(g.dma_start(
                    wbase[:].rearrange("p (c n) -> p c n", n=D),
                    d_in["baseT"].rearrange("(c p) n -> p c n", p=128)), "dW", 16)
                inc(g.dma_start(cball[:], d_in["cball"]), "dW", 16)
                inc(g.dma_start(borow[:], d_in["borow"]), "dW", 16)
        wo_dL = cnt["dW"] - 48
        wbase_dL = cnt["dW"]
        s_kt.close(); open_stacks.remove(s_kt)
        s_aps2 = ExitStack()
        open_stacks.append(s_aps2)
        psEb = s_aps2.enter_context(nc.psum_tensor("psEb", [128, 512], BF16))

        if (blk := mblk) or True:
            @blk.vector
            def _(ve):
                w(ve, "V", cnt["V"])
                for c in range(NCP):
                    ve.memset(numS[c][:], 0.0)
                    ve.memset(denS[c][:], 0.0)
                inc(ve.nop(), "V", 1)
        memset_V = cnt["V"]

        _pt, _pv, _ps = cnt["T"], cnt["V"], cnt["S"]
        tLh = [[0, 0] for _ in range(NCP)]
        ttr2 = [0] * NCP
        tbias2 = [0] * NCP
        tetr = [[0] * 4 for _ in range(NCP)]
        tN = [0] * NCP
        vcm2 = [0] * NCP
        vscr2 = [0] * NCP
        vmk = [0] * NCP
        vmb2 = [0] * NCP
        vet = [[0] * 4 for _ in range(NCP)]
        vnum = [0] * NCP
        sexp2h = [[0, 0] for _ in range(NCP)]
        for i in range(NCP):
            tLh[i][0] = _pt + 1; tLh[i][1] = _pt + 2
            _pt += 2
            if i >= 1:
                tetr[i - 1][0] = _pt + 1
                ttr2[i] = _pt + 2
                tetr[i - 1][1] = _pt + 3
                tetr[i - 1][2] = _pt + 4
                tbias2[i] = _pt + 5
                tetr[i - 1][3] = _pt + 6
                tN[i - 1] = _pt + 7
                _pt += 7
            else:
                ttr2[i] = _pt + 1; tbias2[i] = _pt + 2
                _pt += 2
            vcm2[i] = _pv + 1
            if i >= 1:
                vet[i - 1][0] = _pv + 2
                vscr2[i] = _pv + 3
                vet[i - 1][1] = _pv + 4
                vmk[i] = _pv + 5
                vet[i - 1][2] = _pv + 6
                vmb2[i] = _pv + 7
                vet[i - 1][3] = _pv + 8
                vnum[i - 1] = _pv + 9
                _pv += 9
            else:
                vscr2[i] = _pv + 2; vmk[i] = _pv + 3; vmb2[i] = _pv + 4
                _pv += 4
            sexp2h[i][0] = _ps + 1; sexp2h[i][1] = _ps + 2
            _ps += 2
        for g in range(4):
            tetr[NCP - 1][g] = _pt + 1 + g
            vet[NCP - 1][g] = _pv + 1 + g
        tN[NCP - 1] = _pt + 5
        vnum[NCP - 1] = _pv + 5

        def _emit_tr_g(te, cp, g4):
            if cp > 0 or g4 > 0:
                pc, pg = (cp, g4 - 1) if g4 > 0 else (cp - 1, 3)
                w(te, "V", vet[pc][pg])
            w(te, "S", sexp2h[cp][g4 // 2])
            for j in range(4):
                lc = g4 * 4 + j
                ins = te.transpose(
                    psEb[:, j * 128:(j + 1) * 128],
                    e_sb[:, (cp % 2) * L + lc * 128:(cp % 2) * L + (lc + 1) * 128],
                    identb[:])
            assert inc(ins, "T", 1) == tetr[cp][g4]

        def _emit_nh(te, cp):
            if cp == 0:
                w(te, "V", memset_V)
            if cp > 0:
                w(te, "V", vnum[cp - 1])
            for lc in range(NLC):
                g4 = lc // 4
                w(te, "V", vet[cp][g4])
                ins = te.matmul(
                    psK[:, 512:642],
                    eT_sb[:, (cp % 2) * L + lc * 128:(cp % 2) * L + (lc + 1) * 128],
                    v_sb[:, lc * 1040 + cp * 130:lc * 1040 + cp * 130 + 130],
                    start=(lc == 0), stop=(lc == NLC - 1))
            assert inc(ins, "T", 1) == tN[cp]

        if (blk := mblk) or True:
            @blk.tensor
            def _(te):
                for cp in range(NCP):
                    w(te, "V", vclo)
                    if cp == 0:
                        w(te, "dC", 16)
                    for h in range(2):
                        if cp > 0:
                            w(te, "S", sexp2h[cp - 1][h])
                        for lt in (2 * h, 2 * h + 1):
                            te.matmul(psQ[:, lt * 512:(lt + 1) * 512],
                                      w0blk[:, cp * 128:(cp + 1) * 128],
                                      k_sb[:, cp * L + lt * 512:cp * L + (lt + 1) * 512],
                                      start=True, stop=False)
                            ins = te.matmul(psQ[:, lt * 512:(lt + 1) * 512],
                                            dsel64[:, cp * 128:(cp + 1) * 128],
                                            comb2[:, lt * 512:(lt + 1) * 512],
                                            start=False, stop=True)
                        assert inc(ins, "T", 1) == tLh[cp][h]
                    if cp >= 1:
                        _emit_tr_g(te, cp - 1, 0)
                    w(te, "V", vcm2[cp])
                    if cp > 0:
                        w(te, "V", vmb2[cp - 1])
                    ins = te.transpose(psS[0:1, 0:128], colmax[:], identf[:])
                    assert inc(ins, "T", 1) == ttr2[cp]
                    if cp >= 1:
                        _emit_tr_g(te, cp - 1, 1)
                        _emit_tr_g(te, cp - 1, 2)
                    w(te, "V", vmk[cp])
                    te.matmul(psS[:, 128:129], parab[:, 0:128],
                              mkA[:, cp:cp + 1], start=True, stop=False)
                    ins = te.matmul(psS[:, 128:129], parab[:, 128:256],
                                    mkB[:, cp:cp + 1], start=False, stop=True)
                    assert inc(ins, "T", 1) == tbias2[cp]
                    if cp >= 1:
                        _emit_tr_g(te, cp - 1, 3)
                        _emit_nh(te, cp - 1)
                for g in range(4):
                    _emit_tr_g(te, NCP - 1, g)
                _emit_nh(te, NCP - 1)

            def _emit_ev_g(ve, pc, g4):
                w(ve, "T", tetr[pc][g4])
                if pc >= 2:
                    w(ve, "T", tN[pc - 2])
                ins = ve.tensor_copy(
                    eT_sb[:, (pc % 2) * L + g4 * 512:(pc % 2) * L + (g4 + 1) * 512],
                    psEb[:, 0:512])
                assert inc(ins, "V", 1) == vet[pc][g4]

            def _emit_numev(ve, pc):
                w(ve, "T", tN[pc])
                ve.tensor_copy(numS[pc][0:64, 0:64], psK[0:64, 512:576])
                ve.tensor_copy(numS[pc][64:128, 64:128], psK[64:128, 577:641])
                ve.tensor_copy(d16[0:64, pc:pc + 1], psK[0:64, 576:577])
                ins = ve.tensor_copy(d16[64:128, pc:pc + 1], psK[64:128, 641:642])
                assert inc(ins, "V", 1) == vnum[pc]

            @blk.vector
            def _(ve):
                for cp in range(NCP):
                    w(ve, "T", tLh[cp][0])
                    ins = ve.reduce_max(colmax[:], psQ[:, 0:1024:4], axis=AX.X)
                    assert inc(ins, "V", 1) == vcm2[cp]
                    if cp >= 1:
                        _emit_ev_g(ve, cp - 1, 0)
                    w(ve, "T", ttr2[cp])
                    ins = ve.tensor_copy(scr3[:], psS[0:1, 0:128])
                    assert inc(ins, "V", 1) == vscr2[cp]
                    if cp >= 1:
                        _emit_ev_g(ve, cp - 1, 1)
                    w(ve, "V", vscr2[cp])
                    ve.reduce_max(mkA[:, cp:cp + 1], scr3[:, 0:64],
                                  axis=AX.X, negate=True)
                    ins = ve.reduce_max(mkB[:, cp:cp + 1], scr3[:, 64:128],
                                        axis=AX.X, negate=True)
                    assert inc(ins, "V", 1) == vmk[cp]
                    if cp >= 1:
                        _emit_ev_g(ve, cp - 1, 2)
                    w(ve, "T", tbias2[cp])
                    if cp > 0:
                        w(ve, "S", sexp2h[cp - 1][1])
                    ins = ve.tensor_copy(mbias[:], psS[:, 128:129])
                    assert inc(ins, "V", 1) == vmb2[cp]
                    if cp >= 1:
                        _emit_ev_g(ve, cp - 1, 3)
                        _emit_numev(ve, cp - 1)
                for g in range(4):
                    _emit_ev_g(ve, NCP - 1, g)
                _emit_numev(ve, NCP - 1)

            @blk.scalar
            def _(se):
                for cp in range(NCP):
                    for h in range(2):
                        w(se, "T", tLh[cp][h])
                        w(se, "V", vmb2[cp])
                        if cp >= 2:
                            w(se, "T", tetr[cp - 2][3])
                        ins = se.activation(
                            e_sb[:, (cp % 2) * L + h * 1024:(cp % 2) * L + (h + 1) * 1024],
                            psQ[:, h * 1024:(h + 1) * 1024],
                            AF.Exp, bias=mbias[:], scale=1.0)
                        assert inc(ins, "S", 1) == sexp2h[cp][h]
        assert cnt["T"] == _pt + 5 and cnt["V"] == _pv + 5 and cnt["S"] == _ps
        sexp2 = [sexp2h[c][1] for c in range(NCP)]
        tap("d16", d16[:], [128, 8], "V")
        tap("e", e_sb[:], [128, 2 * L], "S")
        if stage == "p2":
            return finish_ret()
        s_ee.close(); open_stacks.remove(s_ee)

        TZ, VZ = cnt["T"], cnt["V"]
        tzd = TZ + 1
        tzab = TZ + 2
        tze8 = TZ + 3
        vzd8 = VZ + 1
        vzcols = VZ + 2
        vzrows = VZ + 3
        vzeb8 = VZ + 4
        vzeall = VZ + 5

        if (blk := mblk) or True:
            @blk.tensor
            def _(te):
                w(te, "V", vnum[NCP - 1])
                w(te, "S", sexp2[NCP - 1])
                ins = te.transpose(psS[0:8, 0:128], d16[:], identf[:])
                assert inc(ins, "T", 1) == tzd
                w(te, "V", vzcols)
                te.transpose(psS[0:1, 128:136], dcolA[:], identf[0:8, 0:8])
                ins = te.transpose(psS[0:1, 136:144], dcolB[:], identf[0:8, 0:8])
                assert inc(ins, "T", 1) == tzab
                w(te, "V", vzrows)
                te.matmul(psS[:, 144:152], parab[:, 0:128],
                          zArow[:, 0:8], start=True, stop=False)
                ins = te.matmul(psS[:, 144:152], parab[:, 128:256],
                                zBrow[:, 0:8], start=False, stop=True)
                assert inc(ins, "T", 1) == tze8

            @blk.vector
            def _(ve):
                w(ve, "T", tzd)
                ins = ve.tensor_copy(zd8[:], psS[0:8, 0:128])
                assert inc(ins, "V", 1) == vzd8
                w(ve, "V", vzd8)
                ve.reduce_sum(dcolA[:], zd8[:, 0:64], axis=AX.X)
                ins = ve.reduce_sum(dcolB[:], zd8[:, 64:128], axis=AX.X)
                assert inc(ins, "V", 1) == vzcols
                w(ve, "T", tzab)
                ve.tensor_scalar(zArow[:], psS[0:1, 128:136], EPS_ATT, None, ALU.mult)
                ins = ve.tensor_scalar(zBrow[:], psS[0:1, 136:144], EPS_ATT, None, ALU.mult)
                assert inc(ins, "V", 1) == vzrows
                w(ve, "T", tze8)
                ins = ve.tensor_copy(ebias[:], psS[:, 144:152])
                assert inc(ins, "V", 1) == vzeb8
                w(ve, "V", vzeb8)
                for cp in range(NCP):
                    ve.tensor_tensor(denS[cp][0:64, 2 * cp:2 * cp + 1],
                                     d16[0:64, cp:cp + 1], ebias[0:64, cp:cp + 1],
                                     ALU.add)
                    ins = ve.tensor_tensor(denS[cp][64:128, 2 * cp + 1:2 * cp + 2],
                                           d16[64:128, cp:cp + 1],
                                           ebias[64:128, cp:cp + 1], ALU.add)
                assert inc(ins, "V", 1) == vzeall
        denS_V = cnt["V"]

        s_att1.close(); open_stacks.remove(s_att1)
        s_rp = ExitStack()
        open_stacks.append(s_rp)
        recipf = s_rp.enter_context(nc.sbuf_tensor("recipf", [16, L], F32))
        recip2 = s_rp.enter_context(nc.sbuf_tensor("recip2", [64, L], BF16))
        reciplo16 = s_rp.enter_context(nc.sbuf_tensor("reciplo16", [16, L], BF16))

        TB, VB = cnt["T"], cnt["V"]
        tden = [TB + lt + 1 for lt in range(4)]
        vrecf = [VB + 3 * lt + 1 for lt in range(4)]
        vrech = [VB + 3 * lt + 2 for lt in range(4)]
        vrlo = [VB + 3 * lt + 3 for lt in range(4)]
        if (blk := mblk) or True:
            @blk.tensor
            def _(te):
                w(te, "V", denS_V)
                for lt in range(4):
                    for cp in range(NCP):
                        ins = te.matmul(
                            psQ[0:16, lt * 512:(lt + 1) * 512],
                            denS[cp][:],
                            eq_sb[:, cp * L + lt * 512:cp * L + (lt + 1) * 512],
                            start=(cp == 0), stop=(cp == NCP - 1))
                    assert inc(ins, "T", 1) == tden[lt]

            @blk.vector
            def _(ve):
                ve.memset(recip2[:], 0.0)
                for lt in range(4):
                    sl = slice(lt * 512, (lt + 1) * 512)
                    w(ve, "T", tden[lt])
                    ins = ve.reciprocal(recipf[:, sl], psQ[0:16, sl])
                    assert inc(ins, "V", 1) == vrecf[lt]
                    w(ve, "V", vrecf[lt])
                    ins = ve.tensor_copy(recip2[0:16, sl], recipf[:, sl])
                    assert inc(ins, "V", 1) == vrech[lt]
                    w(ve, "V", vrech[lt])
                    ins = ve.scalar_tensor_tensor(
                        reciplo16[:, sl], recip2[0:16, sl], -1.0, recipf[:, sl],
                        ALU.mult, ALU.add)
                    assert inc(ins, "V", 1) == vrlo[lt]

            @blk.sync
            def _(sy):
                for lt in range(4):
                    sl = slice(lt * 512, (lt + 1) * 512)
                    sy.wait_ge(sem["V"], vrlo[lt])
                    inc(sy.dma_start(recip2[32:48, sl], reciplo16[:, sl]),
                        ("dRa", "dRb", "dRc", "dRd")[lt], 16)
        tap("recipf", recipf[:], [16, L], "V")
        tap("densall", denS[0][:], [128, 16], "V")
        if stage == "pz":
            return finish_ret()
        s_aps2.close(); open_stacks.remove(s_aps2)
        s_apq.close(); open_stacks.remove(s_apq)
        s_aps.close(); open_stacks.remove(s_aps)

        s_ctx = ExitStack()
        open_stacks.append(s_ctx)
        ctx_sb = s_ctx.enter_context(
            nc.sbuf_tensor("ctx_sb", [128, NCP * L], BF16, side="right"))
        s_b2 = ExitStack()
        open_stacks.append(s_b2)
        psBC = s_b2.enter_context(nc.psum_tensor("psBC", [128, L], F32))
        psNum = s_b2.enter_context(nc.psum_tensor("psNum", [128, L], F32))
        phi_sb = s_b2.enter_context(nc.sbuf_tensor("phi_sb", [128, L], BF16))
        TB2, VB2, SB2 = cnt["T"], cnt["V"], cnt["S"]
        tbc = [[TB2 + 4 * c + h + 1 for h in range(2)] for c in range(NCP)]
        tnum = [[TB2 + 4 * c + 2 + h + 1 for h in range(2)] for c in range(NCP)]
        vphi = [[VB2 + 2 * c + h + 1 for h in range(2)] for c in range(NCP)]
        sctx = [[SB2 + 2 * c + h + 1 for h in range(2)] for c in range(NCP)]
        if (blk := mblk) or True:
            @blk.tensor
            def _(te):
                for cp in range(NCP):
                    for h in range(2):
                        if cp == 0:
                            w(te, ("dRa", "dRc")[h], 16)
                            w(te, ("dRb", "dRd")[h], 16)
                            if h == 0:
                                w(te, "dC", 16)
                        if cp >= 1:
                            w(te, "V", vphi[cp - 1][h])
                        for lt in (2 * h, 2 * h + 1):
                            ins = te.matmul(psBC[:, lt * 512:(lt + 1) * 512],
                                            dsel64[:, cp * 128:(cp + 1) * 128],
                                            recip2[:, lt * 512:(lt + 1) * 512],
                                            start=True, stop=True)
                        assert inc(ins, "T", 1) == tbc[cp][h]
                    for h in range(2):
                        w(te, "V", vphi[cp][h])
                        if cp >= 1:
                            w(te, "S", sctx[cp - 1][h])
                        for lt in (2 * h, 2 * h + 1):
                            ins = te.matmul(psNum[:, lt * 512:(lt + 1) * 512],
                                            numS[cp][:],
                                            phi_sb[:, lt * 512:(lt + 1) * 512],
                                            start=True, stop=True)
                        assert inc(ins, "T", 1) == tnum[cp][h]

            @blk.vector
            def _(ve):
                for cp in range(NCP):
                    for h in range(2):
                        w(ve, "T", tbc[cp][h])
                        if cp >= 1:
                            w(ve, "T", tnum[cp - 1][h])
                        for lt in (2 * h, 2 * h + 1):
                            ins = ve.tensor_tensor(
                                phi_sb[:, lt * 512:(lt + 1) * 512],
                                psBC[:, lt * 512:(lt + 1) * 512],
                                eq_sb[:, cp * L + lt * 512:cp * L + (lt + 1) * 512],
                                ALU.mult)
                        assert inc(ins, "V", 1) == vphi[cp][h]

            @blk.scalar
            def _(se):
                for cp in range(NCP):
                    for h in range(2):
                        w(se, "T", tnum[cp][h])
                        for lt in (2 * h, 2 * h + 1):
                            ins = se.activation(
                                ctx_sb[:, cp * L + lt * 512:cp * L + (lt + 1) * 512],
                                psNum[:, lt * 512:(lt + 1) * 512],
                                AF.Copy, bias=0.0, scale=1.0)
                        assert inc(ins, "S", 1) == sctx[cp][h]
        ctx_S = cnt["S"]
        assert ctx_S == sctx[NCP - 1][1]
        tap("ctx", ctx_sb[:], [128, NCP * L], "S")
        if stage == "pb2":
            return finish_ret()
        s_b2.close(); open_stacks.remove(s_b2)
        s_rp.close(); open_stacks.remove(s_rp)
        s_att.close(); open_stacks.remove(s_att)

        s1bf = SB("s1bf", [128, NLC * D], BF16)
        s1fm = SB("s1fm", [128, NDC * L], BF16)
        stat = SB("stat", [128, 24], F32)
        scrap = SB("scrap", [128, L], BF16)
        s2t = SB("s2t", [128, 2 * D], F32)
        m1q = SB("m1q", [128, 32], F32)
        m2q = SB("m2q", [128, 32], F32)
        m2scr = SB("m2scr", [128, 1024], BF16)
        s_wo = ExitStack()
        open_stacks.append(s_wo)
        xr = s_wo.enter_context(nc.sbuf_tensor("xr", [128, 2 * D], F32))
        u1buf = s_wo.enter_context(nc.sbuf_tensor("u1buf", [128, 4 * D], F32))
        psO = s_wo.enter_context(nc.psum_tensor("psO", [128, 2 * D], F32))
        psT = s_wo.enter_context(nc.psum_tensor("psT", [128, 2048], BF16))

        TW, VW, SW, PW = cnt["T"], cnt["V"], cnt["S"], cnt["P"]
        def _msched():
            sched = [[] for _ in range(NLC)]
            pend = []
            for lc in range(NLC):
                if lc >= 6 and (lc - 6) % 4 == 0 and (lc - 6) // 4 < 4:
                    pend.extend(((lc - 6) // 4, dc) for dc in range(NDC))
                for _ in range(3):
                    if pend:
                        sched[lc].append(pend.pop(0))
            pend.extend((3, dc) for dc in range(NDC))
            return sched, pend
        m1sched, m1tail = _msched()
        m2sched, m2tail = _msched()
        te_ops = []
        for lc in range(NLC):
            te_ops.append(("mm", lc))
            if lc >= 3:
                te_ops.append(("tr", lc - 3))
        te_ops += [("tr", 13), ("tr", 14), ("tr", 15)]
        ve_ops = []
        for lc in range(NLC):
            ve_ops.append(("vu", lc))
            if lc >= 1:
                ve_ops += [("vms", lc - 1), ("vm2", lc - 1), ("vst", lc - 1)]
            if lc >= 2:
                ve_ops += [("vrc", lc - 2), ("vnm", lc - 2)]
            if lc >= 3:
                ve_ops.append(("vfev", lc - 3))
            ve_ops += [("m1", q, dc) for q, dc in m1sched[lc]]
        ve_ops += [("vms", 15), ("vm2", 15), ("vst", 15),
                   ("vrc", 14), ("vnm", 14), ("vfev", 13),
                   ("vrc", 15), ("vnm", 15), ("vfev", 14), ("vfev", 15)]
        ve_ops += [("m1", q, dc) for q, dc in m1tail]
        se_ops = []
        for lc in range(NLC):
            se_ops.append(("ssq", lc))
            if lc >= 1:
                se_ops.append(("ssd", lc - 1))
            if lc >= 2:
                se_ops.append(("sapply", lc - 2))
            se_ops += [("m2", q, dc) for q, dc in m2sched[lc]]
        se_ops += [("ssd", 15), ("sapply", 14), ("sapply", 15)]
        se_ops += [("m2", q, dc) for q, dc in m2tail]
        tmk = {op: TW + i + 1 for i, op in enumerate(te_ops)}
        vmk_ = {op: VW + i + 1 for i, op in enumerate(ve_ops)}
        smk = {op: SW + i + 1 for i, op in enumerate(se_ops)}
        two = [tmk[("mm", lc)] for lc in range(NLC)]
        tftr = [tmk[("tr", lc)] for lc in range(NLC)]
        vu = [vmk_[("vu", lc)] for lc in range(NLC)]
        vms = [vmk_[("vms", lc)] for lc in range(NLC)]
        vm2 = [vmk_[("vm2", lc)] for lc in range(NLC)]
        vst = [vmk_[("vst", lc)] for lc in range(NLC)]
        vrc = [vmk_[("vrc", lc)] for lc in range(NLC)]
        vnm = [vmk_[("vnm", lc)] for lc in range(NLC)]
        vfev = [vmk_[("vfev", lc)] for lc in range(NLC)]
        vm1_all = VW + len(ve_ops)
        ssq = [smk[("ssq", lc)] for lc in range(NLC)]
        ssd = [smk[("ssd", lc)] for lc in range(NLC)]
        sapply = [smk[("sapply", lc)] for lc in range(NLC)]
        sm2_all = SW + len(se_ops)
        xr_dL = [16 * (lc // 2 + 1) for lc in range(NLC)]
        s1fm3 = s1fm[:].rearrange("p (c l) -> p c l", l=L)
        if (blk := mblk) or True:
            @blk.sync
            def _(sy):
                sy.wait_ge(sem["S"], ctx_S)
                for lc in range(NLC):
                    if lc >= 2:
                        sy.wait_ge(sem["V"], vu[lc - 2])
                    inc(sy.dma_start(xr[:, (lc % 2) * D:(lc % 2 + 1) * D], x_t[lc]),
                        "dXa" if lc % 2 == 0 else "dXb", 16)

            @blk.tensor
            def _(te):
                w(te, "dW", wbase_dL)
                w(te, "S", ctx_S)
                for op in te_ops:
                    kind, lc = op
                    if kind == "mm":
                        if lc >= 2:
                            w(te, "V", vu[lc - 2])
                        for ot in range(2):
                            for c8 in range(NCP):
                                te.matmul(
                                    psO[:, (lc % 2) * D + ot * 512:(lc % 2) * D + (ot + 1) * 512],
                                    ctx_sb[:, c8 * L + lc * 128:c8 * L + (lc + 1) * 128],
                                    wo[:, c8 * D + ot * 512:c8 * D + (ot + 1) * 512],
                                    start=(c8 == 0), stop=False)
                            ins = te.matmul(
                                psO[:, (lc % 2) * D + ot * 512:(lc % 2) * D + (ot + 1) * 512],
                                onesrow[:], borow[:, ot * 512:(ot + 1) * 512],
                                start=False, stop=True)
                        assert inc(ins, "T", 1) == two[lc]
                    else:
                        w(te, "S", sapply[lc])
                        if lc >= 2:
                            w(te, "V", vfev[lc - 2])
                        for dc in range(NDC):
                            ins = te.transpose(
                                psT[:, (lc % 2) * 1024 + dc * 128:(lc % 2) * 1024 + (dc + 1) * 128],
                                s1bf[:, lc * D + dc * 128:lc * D + (dc + 1) * 128],
                                identb[:])
                        assert inc(ins, "T", 1) == tftr[lc]

            @blk.vector
            def _(ve):
                for op in ve_ops:
                    kind, a = op[0], op[1]
                    p2 = a % 2
                    if kind == "vu":
                        w(ve, "T", two[a])
                        w(ve, "dXa" if a % 2 == 0 else "dXb", xr_dL[a])
                        if a >= 2:
                            w(ve, "V", vms[a - 2])
                        if a >= 4:
                            w(ve, "S", sapply[a - 4])
                        ins = ve.scalar_tensor_tensor(
                            u1buf[:, (a % 4) * D:(a % 4 + 1) * D],
                            psO[:, p2 * D:(p2 + 1) * D], 1.0,
                            xr[:, p2 * D:(p2 + 1) * D],
                            ALU.mult, ALU.add, accum_out=stat[:, p2:p2 + 1])
                        assert inc(ins, "V", 1) == vu[a]
                    elif kind == "vms":
                        w(ve, "S", ssq[a])
                        if a >= 2:
                            w(ve, "V", vnm[a - 2])
                        ve.tensor_scalar(stat[:, 4 + p2:5 + p2], stat[:, p2:p2 + 1],
                                         1.0 / D, None, ALU.mult)
                        ins = ve.tensor_scalar(stat[:, 6 + p2:7 + p2],
                                               stat[:, 2 + p2:3 + p2],
                                               1.0 / D, None, ALU.mult)
                        assert inc(ins, "V", 1) == vms[a]
                    elif kind == "vm2":
                        w(ve, "V", vms[a])
                        ins = ve.tensor_tensor(stat[:, 8 + p2:9 + p2],
                                               stat[:, 4 + p2:5 + p2],
                                               stat[:, 4 + p2:5 + p2], ALU.mult)
                        assert inc(ins, "V", 1) == vm2[a]
                    elif kind == "vst":
                        w(ve, "V", vm2[a])
                        ins = ve.scalar_tensor_tensor(stat[:, 10 + p2:11 + p2],
                                                      stat[:, 6 + p2:7 + p2], EPS_LN,
                                                      stat[:, 8 + p2:9 + p2],
                                                      ALU.add, ALU.subtract)
                        assert inc(ins, "V", 1) == vst[a]
                    elif kind == "vrc":
                        w(ve, "S", ssd[a])
                        if a >= 4:
                            w(ve, "S", sapply[a - 4])
                        ins = ve.reciprocal(stat[:, 14 + a % 4:15 + a % 4],
                                            stat[:, 12 + p2:13 + p2])
                        assert inc(ins, "V", 1) == vrc[a]
                    elif kind == "vnm":
                        w(ve, "V", vrc[a])
                        ins = ve.scalar_tensor_tensor(
                            stat[:, 18 + a % 4:19 + a % 4],
                            stat[:, 4 + p2:5 + p2], -1.0,
                            stat[:, 14 + a % 4:15 + a % 4], ALU.mult, ALU.mult)
                        assert inc(ins, "V", 1) == vnm[a]
                    elif kind == "vfev":
                        w(ve, "T", tftr[a])
                        ins = ve.tensor_copy(
                            s1fm3[:, :, a * 128:(a + 1) * 128],
                            psT[:, p2 * 1024:p2 * 1024 + 1024]
                            .rearrange("p (c k) -> p c k", k=128))
                        assert inc(ins, "V", 1) == vfev[a]
                    else:
                        q, dc = op[1], op[2]
                        w(ve, "V", vfev[4 * q + 3])
                        ins = ve.reduce_sum(
                            m1q[:, q * 8 + dc:q * 8 + dc + 1],
                            s1fm[:, dc * L + q * 512:dc * L + (q + 1) * 512],
                            axis=AX.X)
                        inc(ins, "V", 1)
                assert cnt["V"] == vm1_all

            @blk.scalar
            def _(se):
                for op in se_ops:
                    kind, a = op[0], op[1]
                    p2 = a % 2
                    if kind == "ssq":
                        w(se, "V", vu[a])
                        if a >= 1:
                            w(se, "S", ssq[a - 1])
                        ins = se.activation(scrap[:, 0:D],
                                            u1buf[:, (a % 4) * D:(a % 4 + 1) * D],
                                            AF.Square,
                                            accum_out=stat[:, 2 + p2:3 + p2])
                        assert inc(ins, "S", 1) == ssq[a]
                    elif kind == "ssd":
                        w(se, "V", vst[a])
                        ins = se.activation(stat[:, 12 + p2:13 + p2],
                                            stat[:, 10 + p2:11 + p2],
                                            AF.Sqrt, bias=0.0, scale=1.0)
                        assert inc(ins, "S", 1) == ssd[a]
                    elif kind == "sapply":
                        w(se, "V", vnm[a])
                        ins = se.activation(s1bf[:, a * D:(a + 1) * D],
                                            u1buf[:, (a % 4) * D:(a % 4 + 1) * D],
                                            AF.Identity,
                                            bias=stat[:, 18 + a % 4:19 + a % 4],
                                            scale=stat[:, 14 + a % 4:15 + a % 4])
                        assert inc(ins, "S", 1) == sapply[a]
                    else:
                        q, dc = op[1], op[2]
                        w(se, "S", cnt["S"])
                        w(se, "V", vfev[4 * q + 3])
                        ins = se.activation(
                            m2scr[:, (dc % 2) * 512:(dc % 2) * 512 + 512],
                            s1fm[:, dc * L + q * 512:dc * L + (q + 1) * 512],
                            AF.Square, accum_out=m2q[:, q * 8 + dc:q * 8 + dc + 1])
                        inc(ins, "S", 1)
                assert cnt["S"] == sm2_all
        tap("s1", s1bf[:], [128, NLC * D], "S")
        tap("s1fm", s1fm[:], [128, NDC * L], "V")
        if stage in ("pwo", "ft"):
            return finish_ret()
        s_wo.close(); open_stacks.remove(s_wo)
        s_ctx.close(); open_stacks.remove(s_ctx)
        s_wt2.close(); open_stacks.remove(s_wt2)

        s_fn = ExitStack()
        open_stacks.append(s_fn)
        psH = s_fn.enter_context(nc.psum_tensor("psH", [64, 1024], F32))
        psA2 = s_fn.enter_context(nc.psum_tensor("psA2", [128, 1024], F32))
        psO2 = s_fn.enter_context(nc.psum_tensor("psO2", [128, 2 * D], F32))
        psF1 = psA2

        if (blk := mblk) or True:
            @blk.tensor
            def _(te):
                w(te, "V", vfev[NLC - 1])
                for lc in range(2):
                    for ot in range(2):
                        for dc in range(NDC):
                            ins = te.matmul(
                                psO2[:, lc * D + ot * 512:lc * D + (ot + 1) * 512],
                                s1fm[:, dc * L + lc * 128:dc * L + (lc + 1) * 128],
                                wbase[:, dc * D + ot * 512:dc * D + (ot + 1) * 512],
                                start=(dc == 0), stop=False)
                inc(ins, "T", 1)
        base_pre_T = cnt["T"]


        convsel = SB("convsel", [128, NDC * D], BF16)
        biasrow = SB("biasrow", [1, D], BF16)
        dwsel = SB("dwsel", [128, NDC * R], BF16)
        upsel = SB("upsel", [64, D], BF16)
        h_sb = SB("h_sb", [64, L], BF16)
        a_sb = SB("a_sb", [128, NDC * L], BF16)

        fl_dH = consts_dH

        if (blk := mblk) or True:
            @blk.vector
            def _(ve):
                w(ve, "S", sm2_all)
                w(ve, "V", vm1_all)
                inc(ve.tensor_tensor(m1c[:], m1q[:, 0:8], m1q[:, 8:16], ALU.add),
                    "V", 1)
                inc(ve.tensor_tensor(m2c[:], m2q[:, 0:8], m2q[:, 8:16], ALU.add),
                    "V", 1)
                w(ve, "V", cnt["V"])
                inc(ve.tensor_tensor(m1c[:], m1c[:], m1q[:, 16:24], ALU.add), "V", 1)
                inc(ve.tensor_tensor(m2c[:], m2c[:], m2q[:, 16:24], ALU.add), "V", 1)
                w(ve, "V", cnt["V"])
                inc(ve.tensor_tensor(m1c[:], m1c[:], m1q[:, 24:32], ALU.add), "V", 1)
                inc(ve.tensor_tensor(m2c[:], m2c[:], m2q[:, 24:32], ALU.add), "V", 1)
        mom_V, mom_S = cnt["V"], cnt["S"]

        V3, T3 = cnt["V"], cnt["T"]
        tfl = [T3 + dc + 1 for dc in range(NDC)]
        if (blk := mblk) or True:
            @blk.vector
            def _(ve):
                w(ve, "dH", fl_dH)
                w(ve, "dL", cnt["dL"])
                inc(ve.memset(momdc8[:].rearrange("p (c k) -> p c k", k=5)[:, :, 4:5],
                              float(L)), "V", 1)
                ins = ve.tensor_tensor(sctA[:], sigT[:], sigT[:], ALU.mult)
                inc(ins, "V", 1)
                w(ve, "V", cnt["V"])
                inc(ve.reciprocal(ivA[:], sctA[:]), "V", 1)
                w(ve, "V", cnt["V"])
                inc(ve.tensor_tensor(mivA[:], muT[:], ivA[:], ALU.mult), "V", 1)
                w(ve, "V", cnt["V"])
                inc(ve.tensor_tensor(m2ivA[:], muT[:], mivA[:], ALU.mult), "V", 1)
                w(ve, "V", cnt["V"])
                dd = ddisc8[:].rearrange("p (c k) -> p c k", k=48)
                for j, srcA in enumerate((ivA, mivA, m2ivA)):
                    s8 = srcA[:].rearrange("p (c k) -> p c k", k=8)
                    ins = ve.tensor_copy(dd[:, :, 8 * j:8 * j + 8], s8)
                inc(ins, "V", 1)
                w(ve, "V", cnt["V"])
                for j, srcA in enumerate((ivA, mivA, m2ivA)):
                    s8 = srcA[:].rearrange("p (c k) -> p c k", k=8)
                    ins = ve.scalar_tensor_tensor(
                        dd[:, :, 24 + 8 * j:32 + 8 * j], dd[:, :, 8 * j:8 * j + 8],
                        -1.0, s8, ALU.mult, ALU.add)
                inc(ins, "V", 1)
                w(ve, "V", mom_V)
                md = momdc8[:].rearrange("p (c k) -> p c k", k=5)
                ve.tensor_copy(md[:, :, 0:1],
                               m1c[:].rearrange("p (c k) -> p c k", k=1))
                ins = ve.tensor_copy(md[:, :, 2:3],
                                     m2c[:].rearrange("p (c k) -> p c k", k=1))
                inc(ins, "V", 1)
                w(ve, "V", cnt["V"])
                ve.scalar_tensor_tensor(md[:, :, 1:2], md[:, :, 0:1], -1.0,
                                        m1c[:].rearrange("p (c k) -> p c k", k=1),
                                        ALU.mult, ALU.add)
                ins = ve.scalar_tensor_tensor(md[:, :, 3:4], md[:, :, 2:3], -1.0,
                                              m2c[:].rearrange("p (c k) -> p c k", k=1),
                                              ALU.mult, ALU.add)
                inc(ins, "V", 1)
        fl_build_V = cnt["V"]
        if (blk := mblk) or True:
            @blk.tensor
            def _(te):
                w(te, "V", fl_build_V)
                for dc in range(NDC):
                    ins = te.matmul(psF1[0:48, 0:5], ddisc8[:, dc * 48:(dc + 1) * 48],
                                    momdc8[:, dc * 5:(dc + 1) * 5],
                                    start=(dc == 0), stop=(dc == NDC - 1))
                    assert inc(ins, "T", 1) == tfl[dc]

        V4, T4 = cnt["V"], cnt["T"]
        mk_v = lambda k: V4 + k
        if (blk := mblk) or True:
            @blk.vector
            def _(ve):
                w(ve, "T", tfl[NDC - 1])
                assert inc(ve.tensor_copy(loss5[:], psF1[0:48, 0:5]), "V", 1) == mk_v(1)
                w(ve, "V", mk_v(1))
                assert inc(ve.tensor_tensor(ppg[:], loss5[:], wgrid[:], ALU.mult),
                           "V", 1) == mk_v(2)
                w(ve, "V", mk_v(2))
                assert inc(ve.reduce_sum(pp[:], ppg[:], axis=AX.X), "V", 1) == mk_v(3)
                w(ve, "V", mk_v(3))
                assert inc(ve.tensor_copy(pphl[:, 0:1], pp[:]), "V", 1) == mk_v(4)
                w(ve, "V", mk_v(4))
                ins = ve.scalar_tensor_tensor(pphl[:, 1:2], pphl[:, 0:1], -1.0,
                                              pp[:], ALU.mult, ALU.add)
                assert inc(ins, "V", 1) == mk_v(5)
                w(ve, "T", T4 + 1)
                assert inc(ve.tensor_copy(lrow[:], psF1[0:1, 16:24]), "V", 1) == mk_v(6)
                w(ve, "V", mk_v(6))
                assert inc(ve.tensor_reduce(lmin[:], lrow[:], AX.X, ALU.min),
                           "V", 1) == mk_v(7)
                w(ve, "V", mk_v(7))
                assert inc(ve.tensor_scalar(ohrow[:], lrow[:], lmin[:], None,
                                            ALU.is_equal), "V", 1) == mk_v(8)
                w(ve, "V", mk_v(8))
                ve.tensor_copy(ohrowb[:], ohrow[:])
                ins = ve.tensor_tensor(ohj[:], ohrow[:], iotar[:], ALU.mult)
                assert inc(ins, "V", 1) == mk_v(9)
                w(ve, "V", mk_v(9))
                assert inc(ve.reduce_sum(idxv[:, 0:1], ohj[:], axis=AX.X),
                           "V", 1) == mk_v(10)
                w(ve, "T", T4 + 3)
                ve.tensor_copy(ohcol[:], psF1[0:8, 24:25])
                assert inc(ve.tensor_copy(ohbc[:], psF1[0:128, 32:40]), "V", 1) == mk_v(11)

            @blk.tensor
            def _(te):
                w(te, "V", mk_v(5))
                te.matmul(psF1[0:1, 16:24], pphl[:, 0:1], esel[:],
                          start=True, stop=False)
                ins = te.matmul(psF1[0:1, 16:24], pphl[:, 1:2], esel[:],
                                start=False, stop=True)
                assert inc(ins, "T", 1) == T4 + 1
                w(te, "V", mk_v(9))
                ins = te.transpose(psF1[0:8, 24:25], ohrow[:], identf[0:1, 0:1])
                assert inc(ins, "T", 1) == T4 + 2
                ins = te.matmul(psF1[0:128, 32:40], onesrow[:], ohrowb[:],
                                start=True, stop=True)
                assert inc(ins, "T", 1) == T4 + 3
        idx_V = cnt["V"]
        tap("lrow", lrow[:], [1, 8], "V", mk_v(6))
        tap("ohbc", ohbc[:], [128, 8], "V")

        if (blk := mblk) or True:
            @blk.vector
            def _(ve):
                w(ve, "V", mk_v(11))
                inc(ve.tensor_scalar(bucol[:, 0:8], uball[:, 0:8], ohbc[:, 0:1],
                                     None, ALU.mult), "V", 1)
                inc(ve.tensor_scalar(bdcol[:, 0:1], dball[:, 0:1], ohbc[0:64, 0:1],
                                     None, ALU.mult), "V", 1)
                for e in range(1, E):
                    w(ve, "V", cnt["V"])
                    src_u = bucol[:, 8 * ((e + 1) % 2):8 * ((e + 1) % 2) + 8]
                    dst_u = bucol[:, 8 * (e % 2):8 * (e % 2) + 8]
                    inc(ve.scalar_tensor_tensor(dst_u, uball[:, e * 8:(e + 1) * 8],
                                                ohbc[:, e:e + 1], src_u, ALU.mult,
                                                ALU.add), "V", 1)
                    src_d = bdcol[:, (e + 1) % 2:(e + 1) % 2 + 1]
                    dst_d = bdcol[:, e % 2:e % 2 + 1]
                    inc(ve.scalar_tensor_tensor(dst_d, dball[:, e:e + 1],
                                                ohbc[0:64, e:e + 1], src_d, ALU.mult,
                                                ALU.add), "V", 1)
                w(ve, "V", cnt["V"])
                inc(ve.tensor_copy(bucolb[:], bucol[:, 8:16]), "V", 1)
                w(ve, "V", mk_v(10))
                inc(ve.tensor_copy(idxbb[:], idxv[:, 0:1]), "V", 1)
        idxbb_V = cnt["V"]
        budcol_V = idxbb_V - 1
        TG = cnt["T"]
        if (blk := mblk) or True:
            @blk.tensor
            def _(te):
                w(te, "V", idxbb_V)
                ins = te.matmul(psF1[0:128, 40:41], onesrow[:], idxbb[:],
                                start=True, stop=True)
                inc(ins, "T", 1)
        tidxb = cnt["T"]
        if (blk := mblk) or True:
            @blk.vector
            def _(ve):
                w(ve, "T", tidxb)
                w(ve, "G", iota_G)
                inc(ve.tensor_copy(idxbc[:], psF1[0:128, 40:41]), "V", 1)
                w(ve, "V", cnt["V"])
                for gi in range(17):
                    if gi < 8:
                        mulc, addc = 1024.0, float(gi * 128)
                    elif gi < 16:
                        mulc, addc = 1024.0, float((gi - 8) * 128)
                    else:
                        mulc, addc = 64.0, 0.0
                    inc(ve.tensor_scalar(gidxf[:, gi:gi + 1], idxbc[:], mulc, addc,
                                         ALU.mult, ALU.add), "V", 1)
                w(ve, "V", cnt["V"])
                inc(ve.tensor_single_scalar(gidxf[:], gidxf[:], iotac[:], ALU.add),
                    "V", 1)
                w(ve, "V", cnt["V"])
                inc(ve.tensor_copy(gidx17[:], gidxf[:]), "V", 1)
        gidx_V = cnt["V"]

        G0 = cnt["dG"]
        if (blk := mblk) or True:
            @blk.gpsimd
            def _(g):
                w(g, "V", gidx_V)
                for c in range(8):
                    inc(g.indirect_dma_start(
                        dwsel[:, c * R:(c + 1) * R], None, d_in["downW"],
                        IndirectOffsetOnAxis(gidx17[:, 8 + c:9 + c], 0)), "dG", 16)
                inc(g.indirect_dma_start(
                    upsel[:], None, d_in["upW"],
                    IndirectOffsetOnAxis(gidx17[0:64, 16:17], 0)), "dGu", 16)
                for c in range(8):
                    inc(g.indirect_dma_start(
                        convsel[:, c * D:(c + 1) * D], None, d_in["convT"],
                        IndirectOffsetOnAxis(gidx17[:, c:c + 1], 0)), "dGc", 16)
        down_dG = cnt["dG"]
        up_dG = cnt["dGu"]
        conv_dGc = cnt["dGc"]
        tap("dwsel", dwsel[:], [128, NDC * R], "dG")
        tap("upsel", upsel[:], [64, D], "dGu")
        tap("convsel", convsel[:], [128, NDC * D], "dGc")
        if stage == "fl":
            return finish_ret()

        T6, V6, S6 = cnt["T"], cnt["V"], cnt["S"]
        tdn = [T6 + lt + 1 for lt in range(4)]
        shv = [S6 + lt + 1 for lt in range(4)]
        tup = [T6 + 4 + i + 1 for i in range(32)]
        tbias3 = T6 + 4 + 32 + 1
        aev = [0] * 32
        _vv, _ss = V6, S6 + 4
        for i in range(32):
            if i % 2 == 0:
                _vv += 1; aev[i] = ("V", _vv)
            else:
                _ss += 1; aev[i] = ("S", _ss)
        vbias3 = _vv + 1
        if (blk := mblk) or True:
            @blk.tensor
            def _(te):
                w(te, "dW", wbase_dL)
                w(te, "dG", down_dG)
                for lt in range(4):
                    if lt >= 2:
                        w(te, "S", shv[lt - 2])
                    for dc in range(NDC):
                        ins = te.matmul(
                            psH[:, (lt % 2) * 512:(lt % 2 + 1) * 512],
                            dwsel[:, dc * R:(dc + 1) * R],
                            s1fm[:, dc * L + lt * 512:dc * L + (lt + 1) * 512],
                            start=(dc == 0), stop=(dc == NDC - 1))
                    assert inc(ins, "T", 1) == tdn[lt]
                w(te, "dGu", up_dG)
                for oc in range(NDC):
                    for lt in range(4):
                        i = oc * 4 + lt
                        if i >= 2:
                            w(te, aev[i - 2][0], aev[i - 2][1])
                        w(te, "S", shv[lt])
                        ins = te.matmul(
                            psA2[:, (i % 2) * 512:(i % 2 + 1) * 512],
                            upsel[:, oc * 128:(oc + 1) * 128],
                            h_sb[:, lt * 512:(lt + 1) * 512],
                            start=True, stop=True)
                        assert inc(ins, "T", 1) == tup[i]
                w(te, "dGc", conv_dGc)
                w(te, "V", budcol_V)
                for ot in range(2):
                    te.matmul(psH[0:1, ot * 512:(ot + 1) * 512], ohcol[:],
                              cball[:, ot * 512:(ot + 1) * 512],
                              start=True, stop=False)
                    for dc in range(NDC):
                        ins = te.matmul(
                            psH[0:1, ot * 512:(ot + 1) * 512],
                            bucolb[:, dc:dc + 1],
                            convsel[:, dc * D + ot * 512:dc * D + (ot + 1) * 512],
                            start=False, stop=(dc == NDC - 1))
                assert inc(ins, "T", 1) == tbias3

            @blk.scalar
            def _(se):
                for lt in range(4):
                    w(se, "T", tdn[lt])
                    ins = se.activation(h_sb[:, lt * 512:(lt + 1) * 512],
                                        psH[:, (lt % 2) * 512:(lt % 2 + 1) * 512],
                                        AF.Relu, bias=bdcol[:, 1:2], scale=1.0)
                    assert inc(ins, "S", 1) == shv[lt]

            @blk.vector
            def _(ve):
                w(ve, "V", gidx_V)
                for i in range(0, 32, 2):
                    oc, lt = i // 4, i % 4
                    w(ve, "T", tup[i])
                    ins = ve.tensor_copy(
                        a_sb[:, oc * L + lt * 512:oc * L + (lt + 1) * 512],
                        psA2[:, (i % 2) * 512:(i % 2 + 1) * 512])
                    assert inc(ins, "V", 1) == aev[i][1]
                w(ve, "T", tbias3)
                ins = ve.tensor_tensor(biasrow[:], psH[0:1, 0:1024], bbrow[:],
                                       ALU.add)
                assert inc(ins, "V", 1) == vbias3

            @blk.scalar
            def _(se):
                for i in range(1, 32, 2):
                    oc, lt = i // 4, i % 4
                    w(se, "T", tup[i])
                    ins = se.activation(
                        a_sb[:, oc * L + lt * 512:oc * L + (lt + 1) * 512],
                        psA2[:, (i % 2) * 512:(i % 2 + 1) * 512],
                        AF.Copy, bias=0.0, scale=1.0)
                    assert inc(ins, "S", 1) == aev[i][1]
        a_V = cnt["V"]
        a_S = cnt["S"]
        assert cnt["T"] == tbias3 and a_V == vbias3
        tap("h", h_sb[:], [64, L], "S")
        tap("a", a_sb[:], [128, NDC * L], "V")

        T7, V7, S7 = cnt["T"], cnt["V"], cnt["S"]
        tout = [T7 + lc + 1 for lc in range(NLC)]
        vu2 = [V7 + 6 * lc + 1 for lc in range(NLC)]
        vms2 = [V7 + 6 * lc + 2 for lc in range(NLC)]
        vm22 = [V7 + 6 * lc + 3 for lc in range(NLC)]
        vst2 = [V7 + 6 * lc + 4 for lc in range(NLC)]
        vrc2 = [V7 + 6 * lc + 5 for lc in range(NLC)]
        vap2 = [V7 + 6 * lc + 6 for lc in range(NLC)]
        ssq2 = [S7 + 2 * lc + 1 for lc in range(NLC)]
        ssd2 = [S7 + 2 * lc + 2 for lc in range(NLC)]
        oout = [16 * (lc // 2 + 1) for lc in range(NLC)]
        if (blk := mblk) or True:
            @blk.tensor
            def _(te):
                w(te, "V", vbias3)
                w(te, "S", a_S)
                w(te, "dGc", conv_dGc)
                for lc in range(NLC):
                    if lc >= 2:
                        w(te, "V", vu2[lc - 2])
                    for ot in range(2):
                        if lc >= 2:
                            for dc in range(NDC):
                                te.matmul(
                                    psO2[:, (lc % 2) * D + ot * 512:(lc % 2) * D + (ot + 1) * 512],
                                    s1fm[:, dc * L + lc * 128:dc * L + (lc + 1) * 128],
                                    wbase[:, dc * D + ot * 512:dc * D + (ot + 1) * 512],
                                    start=(dc == 0), stop=False)
                        for dc in range(NDC):
                            te.matmul(
                                psO2[:, (lc % 2) * D + ot * 512:(lc % 2) * D + (ot + 1) * 512],
                                a_sb[:, dc * L + lc * 128:dc * L + (lc + 1) * 128],
                                convsel[:, dc * D + ot * 512:dc * D + (ot + 1) * 512],
                                start=False, stop=False)
                        ins = te.matmul(
                            psO2[:, (lc % 2) * D + ot * 512:(lc % 2) * D + (ot + 1) * 512],
                            onesrow[:], biasrow[:, ot * 512:(ot + 1) * 512],
                            start=False, stop=True)
                    assert inc(ins, "T", 1) == tout[lc]

            @blk.vector
            def _(ve):
                for lc in range(NLC):
                    w(ve, "T", tout[lc])
                    if lc >= 2:
                        w(ve, "dOa" if lc % 2 == 0 else "dOb", oout[lc - 2])
                    ins = ve.scalar_tensor_tensor(
                        s2t[:, (lc % 2) * D:(lc % 2 + 1) * D],
                        psO2[:, (lc % 2) * D:(lc % 2 + 1) * D], 1.0,
                        s1bf[:, lc * D:(lc + 1) * D],
                        ALU.mult, ALU.add, accum_out=stat[:, 0:1])
                    assert inc(ins, "V", 1) == vu2[lc]
                    w(ve, "S", ssq2[lc])
                    ve.tensor_scalar(stat[:, 2:3], stat[:, 0:1], 1.0 / D, None, ALU.mult)
                    ins = ve.tensor_scalar(stat[:, 4:5], stat[:, 1:2], 1.0 / D, None,
                                           ALU.mult)
                    assert inc(ins, "V", 1) == vms2[lc]
                    w(ve, "V", vms2[lc])
                    ins = ve.tensor_tensor(stat[:, 5:6], stat[:, 2:3], stat[:, 2:3], ALU.mult)
                    assert inc(ins, "V", 1) == vm22[lc]
                    w(ve, "V", vm22[lc])
                    ins = ve.scalar_tensor_tensor(stat[:, 6:7], stat[:, 4:5], EPS_LN,
                                                  stat[:, 5:6], ALU.add, ALU.subtract)
                    assert inc(ins, "V", 1) == vst2[lc]
                    w(ve, "S", ssd2[lc])
                    ins = ve.reciprocal(stat[:, 8:9], stat[:, 7:8])
                    assert inc(ins, "V", 1) == vrc2[lc]
                    w(ve, "V", vrc2[lc])
                    ins = ve.tensor_scalar(
                        s2t[:, (lc % 2) * D:(lc % 2 + 1) * D],
                        s2t[:, (lc % 2) * D:(lc % 2 + 1) * D],
                        stat[:, 2:3], stat[:, 8:9], ALU.subtract, ALU.mult)
                    assert inc(ins, "V", 1) == vap2[lc]

            @blk.scalar
            def _(se):
                for lc in range(NLC):
                    w(se, "V", vu2[lc])
                    ins = se.activation(scrap[:, 0:D],
                                        s2t[:, (lc % 2) * D:(lc % 2 + 1) * D],
                                        AF.Square, accum_out=stat[:, 1:2])
                    assert inc(ins, "S", 1) == ssq2[lc]
                    w(se, "V", vst2[lc])
                    ins = se.activation(stat[:, 7:8], stat[:, 6:7],
                                        AF.Sqrt, bias=0.0, scale=1.0)
                    assert inc(ins, "S", 1) == ssd2[lc]

            @blk.sync
            def _(sy):
                for lc in range(NLC):
                    sy.wait_ge(sem["V"], vap2[lc])
                    ins = sy.dma_start(d_out.rearrange("(t p) d -> t p d", p=128)[lc],
                                       s2t[:, (lc % 2) * D:(lc % 2 + 1) * D])
                    assert inc(ins, "dOa" if lc % 2 == 0 else "dOb", 16) == oout[lc]
        s_fn.close(); open_stacks.remove(s_fn)
        s_wt.close(); open_stacks.remove(s_wt)
        return finish_ret()


def _finish(nc, sem, cnt, inc, tap_list):
    with nc.Block() as blk:
        @blk.gpsimd
        def _(g):
            for name, dram, sbap, semn, val in tap_list:
                g.wait_ge(sem[semn], val)
                inc(g.dma_start(dram, sbap), "dT", 16)
            g.wait_ge(sem["dT"], cnt["dT"])
            g.wait_ge(sem["dOa"], cnt["dOa"])
            g.wait_ge(sem["dOb"], cnt["dOb"])



def kernel(**inputs):
    import numpy as np
    from concourse.bass_utils import run_bass_kernel_spmd
    shared = host_prep(inputs)
    x = np.asarray(inputs["L0_attn_x"], np.float32)
    nc = build()
    maps = []
    for b in range(8):
        m = {"i_" + k: v for k, v in shared.items()}
        m["x"] = np.ascontiguousarray(x[b])
        maps.append(m)
    res = run_bass_kernel_spmd(nc, maps, core_ids=list(range(8)))
    out = np.stack([np.asarray(res.results[b]["out"], np.float32) for b in range(8)], 0)
    return out
